# version 23
# baseline (speedup 1.0000x reference)
import contextlib
import os
import numpy as np
import ml_dtypes
import concourse.bass as bass
import concourse.mybir as mybir
from concourse.bass import IndirectOffsetOnAxis
from concourse.bass_utils import run_bass_kernel_spmd

F32 = mybir.dt.float32
BF16 = mybir.dt.bfloat16
I32 = mybir.dt.int32
AF = mybir.ActivationFunctionType
ALU = mybir.AluOpType
AX = mybir.AxisListType

NCORES = 8
T = 2048
D = 1024
DK = 64
E = 32
EPC = 4
CAP = 1024
II = 512
NT = T // 128
OOB = 2.0e6
PHASE = int(os.environ.get("KPHASE", "5"))
KCH = int(os.environ.get("KCH", "16"))
KSUB = int(os.environ.get("KSUB", "63"))
KEL = int(os.environ.get("KEL", "9"))
KW = int(os.environ.get("KW", "1"))
KP2 = int(os.environ.get("KP2", "10"))
KPV = int(os.environ.get("KPV", "4"))


class Sched:

    def __init__(self, nc, es):
        self.nc = nc
        self.sem = {}
        self.cnt = {}
        self.scale = {}
        self.observed = {e: {} for e in ("sp", "act", "pe", "dve", "pool")}
        self.es = es
        for e in ("sp", "act", "pe", "dve", "pool"):
            self._new_sem(e, 1)
        self._new_sem("cc", 1)

    def _new_sem(self, key, scale):
        self.sem[key] = self.es.enter_context(self.nc.semaphore(f"s_{key}"))
        self.cnt[key] = 0
        self.scale[key] = scale

    def stream(self, key):
        if key not in self.sem:
            self._new_sem(key, 16)
        return key

    def _eng(self, e):
        return {"sp": self.nc.sync, "act": self.nc.scalar, "pe": self.nc.tensor,
                "dve": self.nc.vector, "pool": self.nc.gpsimd}[e]

    def _wait(self, engine, tokens):
        eng = self._eng(engine)
        need = {}
        for (k, val) in tokens:
            if val > need.get(k, -1):
                need[k] = val
        for k, val in sorted(need.items()):
            if self.observed[engine].get(k, 0) < val:
                eng.wait_ge(self.sem[k], val * self.scale[k])
                self.observed[engine][k] = val

    def op(self, engine, fn, after=()):
        self._wait(engine, after)
        inst = fn(self._eng(engine))
        inst.then_inc(self.sem[engine], 1)
        self.cnt[engine] += 1
        return (engine, self.cnt[engine])

    def dma(self, engine, stream_key, fn, after=()):
        self.stream(stream_key)
        self._wait(engine, after)
        inst = fn(self._eng(engine))
        inst.then_inc(self.sem[stream_key], 16)
        self.cnt[stream_key] += 1
        return (stream_key, self.cnt[stream_key])

    def collective(self, fn, after=()):
        self._wait("pool", after)
        inst = fn(self.nc.gpsimd)
        inst.then_inc(self.sem["cc"], 1)
        self.cnt["cc"] += 1
        return ("cc", self.cnt["cc"])

    def token(self, key):
        return (key, self.cnt[key])


def build_nc():
    nc = bass.Bass("TRN2", target_bir_lowering=False, debug=False)

    dtn = lambda n, s, d, **kw: nc.dram_tensor(n, s, d, **kw).ap()
    x_in = dtn("x", [T, D], F32, kind="ExternalInput")
    wqT_in = dtn("wqT", [D, DK], F32, kind="ExternalInput")
    wkT_in = dtn("wkT", [D, DK], F32, kind="ExternalInput")
    wvT_in = dtn("wvT", [D, DK], F32, kind="ExternalInput")
    woT_in = dtn("woT", [DK, D], F32, kind="ExternalInput")
    gwT_in = dtn("gwT", [D, E], BF16, kind="ExternalInput")
    gb_in = dtn("gbias", [1, E], F32, kind="ExternalInput")
    w1_in = dtn("w1s", [EPC * D, II], BF16, kind="ExternalInput")
    w3_in = dtn("w3s", [EPC * D, II], BF16, kind="ExternalInput")
    w2_in = dtn("w2s", [EPC * II, D], BF16, kind="ExternalInput")
    sw1_in = dtn("sw1T", [D, II], BF16, kind="ExternalInput")
    sw3_in = dtn("sw3T", [D, II], BF16, kind="ExternalInput")
    sw2_in = dtn("sw2T", [II, D], BF16, kind="ExternalInput")
    idn_in = dtn("idn", [128, 128], BF16, kind="ExternalInput")
    idnf_in = dtn("idnf", [128, 128], F32, kind="ExternalInput")
    maskT_in = dtn("maskT", [128, 128], BF16, kind="ExternalInput")
    triS_in = dtn("triS", [128, 128], BF16, kind="ExternalInput")
    iotaE_in = dtn("iotaE", [128, E], F32, kind="ExternalInput")
    onesf_in = dtn("onesf", [1, 128], F32, kind="ExternalInput")
    onesb_in = dtn("onesb", [128, 1], BF16, kind="ExternalInput")
    tglob_in = dtn("tglob", [128, NT], F32, kind="ExternalInput")
    cmask_in = dtn("cmask", [NCORES, 1], F32, kind="ExternalInput")
    oidx_in = dtn("oidx", [32, 1], I32, kind="ExternalInput")

    out_d = dtn("out", [T, D], F32, kind="ExternalOutput")

    x1d = dtn("x1d", [T, D], F32, kind="Internal")
    xag_i = dtn("xag_i", [T, D], BF16, kind="Internal")
    xall = dtn("xall", [NCORES * T, D], BF16, kind="Internal",
               addr_space=("Shared" if PHASE >= 3 else "Local"))
    cnt_i = dtn("cnt_i", [1, E], F32, kind="Internal")
    cnt_o = dtn("cnt_o", [NCORES, E], F32, kind="Internal")
    amap_i = dtn("amap_i", [E * CAP, 1], F32, kind="Internal")
    amap_o = dtn("amap_o", [E * CAP, 1], F32, kind="Internal")
    yshd = dtn("yshd", [T, D], BF16, kind="Internal")
    yec = dtn("yec", [EPC * CAP, D], BF16, kind="Internal")
    yeall = dtn("yeall", [E * CAP, D], BF16, kind="Internal",
                addr_space=("Shared" if PHASE >= 3 else "Local"))
    agos = [dtn("ago%d" % k, [NCORES * CAP, D], BF16, kind="Internal") for k in range(EPC)]

    es = contextlib.ExitStack()
    sb = lambda n, s, d: es.enter_context(nc.sbuf_tensor(n, s, d))

    idn = sb("c_idn", [128, 128], BF16)
    idnf = sb("c_idnf", [128, 128], F32)
    maskT = sb("c_maskT", [128, 128], BF16)
    triS = sb("c_triS", [128, 128], BF16)
    iotaE = sb("c_iotaE", [128, E], F32)
    onesf = sb("c_onesf", [1, 128], F32)
    onesb = sb("c_onesb", [128, 1], BF16)
    tglob = sb("c_tglob", [128, NT], F32)
    cmask = sb("c_cmask", [NCORES, 1], F32)
    oidx = sb("c_oidx", [32, 1], I32)
    gbias = sb("c_gbias", [1, E], F32)
    w32a = sb("w32a", [128, 512], F32)
    wq_b = sb("wq_b", [128, 512], BF16)
    wk_b = sb("wk_b", [128, 512], BF16)
    wv_b = sb("wv_b", [128, 512], BF16)
    wo_b = sb("wo_b", [64, 1024], BF16)
    gw_b = sb("gw_b", [128, 8 * E], BF16)
    w1e = sb("w1e", [128, 2 * 8 * II], BF16)
    w3e = sb("w3e", [128, 2 * 8 * II], BF16)
    w2e = sb("w2e", [128, 2 * 4 * D], BF16)
    xf = sb("xf", [128, 3 * D], F32)
    xb = sb("xb", [128, 2 * D], BF16)
    qraw = sb("qraw", [64, 512], F32)
    Q_all = sb("Q_all", [64, NT * 128], BF16)
    K_all = sb("K_all", [64, NT * 128], BF16)
    Kf_all = sb("Kf_all", [128, NT * DK], BF16)
    V_all = sb("V_all", [128, NT * 65], BF16)
    S_sb = sb("S_sb", [64, 65], BF16)
    S32_sb = sb("S32_sb", [64, 65], F32)
    qe32 = sb("qe32", [64, 512], F32)
    At_b = sb("At_b", [128, 2 * 128], BF16)
    ao_sb = sb("ao_sb", [128, 2 * 64], BF16)
    aoT_sb = sb("aoT_sb", [64, 2 * 128], BF16)
    r_sb = sb("r_sb", [128, 2 * D], F32)
    x1f_sb = sb("x1f_sb", [128, 2 * D], F32)
    x1b_sb = sb("x1b_sb", [128, 2 * D], BF16)
    sm = sb("sm", [128, 24], F32)
    wg_sb = sb("wg_sb", [128, 4 * NT], F32)
    ek_sb = sb("ek_sb", [128, 4 * NT], F32)
    g_sb = sb("g_sb", [128, 4 * NT], F32)
    posk_sb = sb("posk_sb", [128, 4 * NT], F32)
    I_sb = sb("I_sb", [128, NT * E], BF16)
    rb_sb = sb("rb_sb", [1, NT * E], F32)
    cs_all = sb("cs_all", [1, NT * E], F32)
    counts_sb = sb("counts_sb", [1, E], F32)
    call_sb = sb("call_sb", [NCORES, E], F32)
    base_sb = sb("base_sb", [1, E], F32)
    bias_b = sb("bias_b", [128, E], F32)
    rt_all = sb("rt_all", [1, NT * E], F32)
    tk_sb = sb("tk_sb", [128, 6 * NT], F32)
    gf_sb = sb("gf_sb", [128, 4 * NT], F32)
    sidx_i = sb("sidx_i", [128, 4 * NT], I32)
    zz_sb = sb("zz_sb", [128, 256], F32)
    am_sb = sb("am_sb", [32, 128], F32)
    gi32 = sb("gi32", [128, E], F32)
    gidx_i = sb("gidx_i", [128, E], I32)
    xe_t = sb("xe_t", [128, 3 * D], BF16)
    xeT_sb = sb("xeT_sb", [128, 2 * D], BF16)
    h_pipe = sb("h_pipe", [128, 2 * II], BF16)
    hT_sb = sb("hT_sb", [128, 2 * II], BF16)
    s1_sb = sb("s1_sb", [128, 2 * II], F32)
    ye_sb = sb("ye_sb", [128, 2 * D], BF16)
    ysh_sb = sb("ysh_sb", [128, 2 * D], BF16)
    qf_sb = sb("qf_sb", [64, 128], F32)
    es_x1 = contextlib.ExitStack()
    x1bT = es_x1.enter_context(nc.sbuf_tensor("x1bT", [128, NT * D], BF16))
    xT_all = x1bT
    Vt_all = es_x1.enter_context(nc.sbuf_tensor("Vt_all", [64, NT * 128], BF16))
    lg_all = es_x1.enter_context(nc.sbuf_tensor("lg_all", [128, NT * E], F32))
    sc_all = es_x1.enter_context(nc.sbuf_tensor("sc_all", [128, NT * E], F32))
    u_all = es_x1.enter_context(nc.sbuf_tensor("u_all", [128, NT * E], F32))
    hh_sb = es_x1.enter_context(nc.sbuf_tensor("hh_sb", [128, 4 * NT * E], BF16))
    P_all = es_x1.enter_context(nc.sbuf_tensor("P_all", [128, NT * E], F32))
    rbt_all = es_x1.enter_context(nc.sbuf_tensor("rbt_all", [128, NT * E], F32))
    tmpW = es_x1.enter_context(nc.sbuf_tensor("tmpW", [128, NT * E], F32))
    cmb = {}

    pq = es.enter_context(nc.psum_tensor("pqm", [128, 512], F32))
    pk = es.enter_context(nc.psum_tensor("pkm", [128, 512], F32))
    pm = pq
    prb = pk
    es_att = contextlib.ExitStack()
    tp_att = es_att.enter_context(nc.psum_tensor("tp_att", [128, 1024], BF16))
    tp_b = es_att.enter_context(nc.psum_tensor("tp_b", [128, 1024], BF16))
    pVAN = es_att.enter_context(nc.psum_tensor("pVAN", [128, 512], F32))
    pS = es_att.enter_context(nc.psum_tensor("pS", [64, 2 * 65], F32))
    wo_ps = es_att.enter_context(nc.psum_tensor("wo_ps", [128, 1024], F32))

    S = Sched(nc, es)
    psd = {}

    with es, nc.Block() as block:
        @block.sync
        def _(sync):
            t_c = None
            for dst, src in ((idn, idn_in), (idnf, idnf_in), (maskT, maskT_in),
                             (triS, triS_in), (iotaE, iotaE_in), (onesf, onesf_in),
                             (onesb, onesb_in), (tglob, tglob_in), (cmask, cmask_in),
                             (oidx, oidx_in), (gbias, gb_in)):
                t_c = S.dma("sp", "c", lambda s, dst=dst, src=src: s.dma_start(out=dst[:], in_=src))
            t_consts = t_c

            t_awb = None
            for dst, src in ((wq_b, wqT_in), (wk_b, wkT_in), (wv_b, wvT_in)):
                taw = S.dma("act", "aw", lambda s, src=src: s.dma_start(
                    out=w32a[:].rearrange("p (c j) -> p c j", j=DK),
                    in_=src.rearrange("(c p) j -> p c j", p=128)),
                    after=[] if t_awb is None else [t_awb])
                t_awb = S.op("dve", lambda vv, dst=dst: vv.tensor_copy(dst[:], w32a[:]), after=[taw])
            for half in range(2):
                taw = S.dma("act", "aw", lambda s, half=half: s.dma_start(
                    out=w32a[0:64, 0:512], in_=woT_in[:, half * 512:(half + 1) * 512]),
                    after=[t_awb])
                t_awb = S.op("dve", lambda vv, half=half: vv.tensor_copy(
                    wo_b[:, half * 512:(half + 1) * 512], w32a[0:64, 0:512]), after=[taw])
            t_vones = S.op("dve", lambda vv: vv.memset(V_all[:], 1.0))
            t_szero = S.op("dve", lambda vv: vv.memset(S_sb[:], 0.0))
            t_szero2 = S.op("dve", lambda vv: vv.memset(S32_sb[:], 0.0))
            t_zz = S.op("dve", lambda vv: vv.memset(zz_sb[:], 0.0))
            t_eps = S.op("dve", lambda vv: vv.memset(sm[:, 18:19], 1e-5))
            t_rb0 = S.op("dve", lambda vv: vv.memset(rb_sb[0:1, 0:E], 0.0))
            t_xez = S.op("dve", lambda vv: vv.memset(xe_t[:], 0.0))

            S.dma("act", "gw", lambda s: s.dma_start(
                out=gw_b[:].rearrange("p (c e) -> p c e", e=E),
                in_=gwT_in.rearrange("(c p) e -> p c e", p=128)))
            t_gw = S.token("gw")
            for dst, src, nch in (((w1e, sw1_in, 8), (w3e, sw3_in, 8), (w2e, sw2_in, 4)) if KW else ()):
                off = 8 * II if nch == 8 else 4 * D
                S.dma("act", "we1", lambda s, dst=dst, src=src, nch=nch, off=off: s.dma_start(
                    out=dst[:, off:2 * off].rearrange("p (c i) -> p c i", c=nch),
                    in_=src.rearrange("(c p) i -> p c i", p=128)))
            S.stream("we1"); S.stream("we0")
            t_sw = S.token("we1")
            t_amz = S.dma("sp", "amz", lambda s: s.dma_start(
                out=amap_i.rearrange("(g p b) c -> p g (b c)", p=128, b=128),
                in_=zz_sb[:].rearrange("p (g b) -> p g b", g=2)), after=[t_zz])

            t_wexp = {}

            def load_expert_w(e, gate_tok):
                eb = e % 2
                strm = "we%d" % eb
                deps = [] if gate_tok is None else [gate_tok]
                S.dma("act", strm, lambda s, e=e, eb=eb: s.dma_start(
                    out=w1e[:, eb * 8 * II:(eb + 1) * 8 * II].rearrange("p (c i) -> p c i", c=8),
                    in_=w1_in.rearrange("(e c p) i -> e p c i", e=EPC, p=128)[e]), after=deps)
                S.dma("act", strm, lambda s, e=e, eb=eb: s.dma_start(
                    out=w3e[:, eb * 8 * II:(eb + 1) * 8 * II].rearrange("p (c i) -> p c i", c=8),
                    in_=w3_in.rearrange("(e c p) i -> e p c i", e=EPC, p=128)[e]))
                S.dma("act", strm, lambda s, e=e, eb=eb: s.dma_start(
                    out=w2e[:, eb * 4 * D:(eb + 1) * 4 * D].rearrange("p (c i) -> p c i", c=4),
                    in_=w2_in.rearrange("(e c p) i -> e p c i", e=EPC, p=128)[e]))
                t_wexp[e] = S.token(strm)

            if KW:
                load_expert_w(0, None)

            if PHASE == 0:
                es_att.close()
                S._wait("sp", [S.token(k) for k in list(S.cnt) if S.cnt[k] > 0])
                S.dma("sp", "outw", lambda s: s.dma_start(out=out_d, in_=x1d))
                S._wait("sp", [S.token("outw")])
                return

            t_x, t_x2, t_xb, t_xtr, t_xT = {}, {}, {}, {}, {}
            t_qe, t_ke, t_kf = {}, {}, {}
            t_v, t_vsb, t_A, t_Ab, t_num, t_st, t_Ssb = {}, {}, {}, {}, {}, {}, {}
            t_vtr = {}
            t_sacc = {}
            t_ao, t_aoT, t_aoTs, t_wo, t_r, t_ln, t_x1b, t_x1bT = {}, {}, {}, {}, {}, {}, {}, {}
            t_x1w, t_xagw = {}, {}
            t_Ssb[0] = t_szero

            def smc(i, n):
                b = 2 + (i % 2) * 8
                return sm[:, b + n:b + n + 1]

            sq_prev = [None]

            def ln_chain(i, acc_tok, acc_ap, out_ap, sq_slice, out_deps):
                tm = S.op("act", lambda s: s.mul(smc(i, 1), smc(i, 0), 1.0 / D), after=[acc_tok])
                tng = S.op("act", lambda s: s.mul(smc(i, 2), smc(i, 1), -1.0), after=[tm])
                sq_dep = [tng] + list(out_deps) + ([sq_prev[0]] if sq_prev[0] is not None else [])
                tsq = S.op("act", lambda s: s.activation(
                    out=sq_slice, in_=acc_ap, func=AF.Square, bias=smc(i, 2), accum_out=smc(i, 3)), after=sq_dep)
                sq_prev[0] = tsq
                tva = S.op("act", lambda s: s.mul(smc(i, 4), smc(i, 3), 1.0 / D), after=[tsq])
                tsd = S.op("act", lambda s: s.activation(
                    out=smc(i, 5), in_=smc(i, 4), func=AF.Sqrt, bias=sm[:, 18:19]), after=[tva, t_eps])
                trs = S.op("dve", lambda vv: vv.reciprocal(smc(i, 6), smc(i, 5)), after=[tsd])
                tnm = S.op("act", lambda s: s.activation(
                    out=smc(i, 7), in_=smc(i, 2), func=AF.Copy, scale=smc(i, 6)), after=[trs])
                tl = S.op("act", lambda s: s.activation(
                    out=out_ap, in_=acc_ap, func=AF.Identity, scale=smc(i, 6), bias=smc(i, 7)),
                    after=[tnm] + list(out_deps))
                return tl

            def att_pass1(i):
                s4 = i % 3
                dep = [t_xb[i - 3]] if i >= 3 else []
                t_x[i] = S.dma("sp", "x%d" % s4, lambda s, i=i, s4=s4: s.dma_start(
                    out=xf[:, s4 * D:(s4 + 1) * D], in_=x_in[i * 128:(i + 1) * 128, :]), after=dep)
                dep = [t_x[i]]
                if i >= 2:
                    dep.append(t_xtr[i - 2])
                t_xb[i] = S.op("dve", lambda vv, i=i, s4=s4: vv.tensor_copy(
                    xb[:, (i % 2) * D:(i % 2 + 1) * D], xf[:, s4 * D:(s4 + 1) * D]), after=dep)
                dep = [t_xb[i], t_consts]
                if i >= 1:
                    dep.append(t_xT[i - 1])
                tr = None
                for dc in range(8):
                    tr = S.op("pe", lambda p, dc=dc, i=i: p.transpose(
                        tp_att[:, dc * 128:(dc + 1) * 128],
                        xb[:, (i % 2) * D + dc * 128:(i % 2) * D + (dc + 1) * 128], idn[:]),
                        after=dep if dc == 0 else ())
                t_xtr[i] = tr
                t_xT[i] = S.op("dve", lambda vv, i=i: vv.tensor_copy(
                    xT_all[:, :].rearrange("p (dc t) -> p dc t", t=T)[:, :, i * 128:(i + 1) * 128],
                    tp_att[:, :].rearrange("p (dc t) -> p dc t", t=128)), after=[tr])

            t_qkve = {}

            def qkv_wide():
                last_min = [None]
                for g4 in range(4):
                    depx = [t_xT[4 * g4 + 3], t_awb]
                    for which, wb, bank, dst in (("q", wq_b, pq, Q_all), ("k", wk_b, pk, K_all)):
                        dep = list(depx)
                        if (which, g4 - 1) in t_qkve:
                            dep.append(t_qkve[(which, g4 - 1)])
                        mm = None
                        for dc in range(8):
                            mm = S.op("pe", lambda p, dc=dc, g4=g4, wb=wb, bank=bank: p.matmul(
                                bank[0:64, :], wb[:, dc * DK:(dc + 1) * DK],
                                xT_all[:, dc * T + g4 * 512:dc * T + (g4 + 1) * 512],
                                start=(dc == 0), stop=(dc == 7)), after=dep if dc == 0 else ())
                        tev = S.op("dve", lambda vv, bank=bank: vv.tensor_copy(qraw[:], bank[0:64, :]),
                                   after=[mm] + ([last_min[0]] if last_min[0] is not None else []))
                        texp = S.op("act", lambda s: s.activation(
                            out=qe32[:], in_=qraw[:], func=AF.Exp), after=[tev])
                        tadd = S.op("dve", lambda vv: vv.tensor_scalar(
                            qraw[:], qraw[:], 1.0, None, ALU.add), after=[texp])
                        t_qkve[(which, g4)] = S.op("dve", lambda vv, g4=g4, dst=dst: vv.tensor_tensor(
                            dst[:, g4 * 512:(g4 + 1) * 512], qe32[:], qraw[:], ALU.min), after=[tadd])
                        last_min[0] = t_qkve[(which, g4)]
                    dep = list(depx)
                    if ("v", g4 - 1) in t_qkve:
                        dep.append(t_qkve[("v", g4 - 1)])
                    mm = None
                    for dc in range(8):
                        mm = S.op("pe", lambda p, dc=dc, g4=g4: p.matmul(
                            pVAN[0:64, 0:512], wv_b[:, dc * DK:(dc + 1) * DK],
                            xT_all[:, dc * T + g4 * 512:dc * T + (g4 + 1) * 512],
                            start=(dc == 0), stop=(dc == 7)), after=dep if dc == 0 else ())
                    t_qkve[("v", g4)] = S.op("dve", lambda vv, g4=g4: vv.tensor_copy(
                        Vt_all[:, g4 * 512:(g4 + 1) * 512], pVAN[0:64, 0:512]), after=[mm])

            def kfv_trans():
                prev = None
                for i in range(NT):
                    g4 = i // 4
                    dep = [t_qkve[("k", g4)], t_consts]
                    if prev is not None:
                        dep.append(prev)
                    ttr = S.op("pe", lambda p, i=i: p.transpose(
                        tp_att[:, 0:64], K_all[:, i * 128:(i + 1) * 128], idn[0:64, 0:64]), after=dep)
                    t_kf[i] = S.op("dve", lambda vv, i=i: vv.tensor_copy(
                        Kf_all[:, i * DK:(i + 1) * DK], tp_att[:, 0:64]), after=[ttr])
                    ttr2 = S.op("pe", lambda p, i=i: p.transpose(
                        tp_att[:, 64:128], Vt_all[:, i * 128:(i + 1) * 128], idn[0:64, 0:64]),
                        after=[t_qkve[("v", g4)], t_kf[i]])
                    t_vsb[i] = S.op("dve", lambda vv, i=i: vv.tensor_copy(
                        V_all[:, i * 65:i * 65 + 64], tp_att[:, 64:128]), after=[ttr2, t_vones])
                    prev = t_vsb[i]

            def lp3(s4):
                return 15 - ((15 - s4) % 3)

            def att_pass2(i):
                s4, s2 = i % 3, i % 2
                dep = [t_xb[lp3(s4)]]
                if i >= 3:
                    dep.append(t_r[i - 3])
                t_x2[i] = S.dma("sp", "x%d" % s4, lambda s, i=i, s4=s4: s.dma_start(
                    out=xf[:, s4 * D:(s4 + 1) * D], in_=x_in[i * 128:(i + 1) * 128, :]), after=dep)
                dep = [t_vsb[i]]
                if i >= 1:
                    dep.append(t_Ab[i - 1])
                t_A[i] = S.op("pe", lambda p, i=i: p.matmul(
                    pVAN[:, 0:128], K_all[:, i * 128:(i + 1) * 128], Q_all[:, i * 128:(i + 1) * 128],
                    start=True, stop=True), after=dep)
                dep = [t_A[i], t_consts]
                if i >= 2:
                    dep.append(t_num[i - 2])
                t_Ab[i] = S.op("dve", lambda vv, s2=s2: vv.tensor_tensor(
                    At_b[:, s2 * 128:(s2 + 1) * 128], pVAN[:, 0:128], maskT[:], ALU.mult), after=dep)
                no2 = 128 if s2 == 0 else 224
                dep = [t_Ab[i], t_Ssb[i]]
                if i >= 2:
                    dep.append(t_ao[i - 2])
                if i == 0:
                    t_num[i] = S.op("pe", lambda p, i=i, s2=s2, no2=no2: p.matmul(
                        pVAN[:, no2:no2 + 65], At_b[:, s2 * 128:(s2 + 1) * 128],
                        V_all[:, i * 65:(i + 1) * 65], start=True, stop=True), after=dep)
                else:
                    mmn = S.op("pe", lambda p, i=i, s2=s2, no2=no2: p.matmul(
                        pVAN[:, no2:no2 + 65], At_b[:, s2 * 128:(s2 + 1) * 128],
                        V_all[:, i * 65:(i + 1) * 65], start=True, stop=False), after=dep)
                    t_num[i] = S.op("pe", lambda p, i=i, no2=no2: p.matmul(
                        pVAN[:, no2:no2 + 65], Q_all[:, i * 128:(i + 1) * 128], S_sb[:],
                        start=False, stop=True))
                dep = [t_kf[i], t_vsb[i]]
                if i >= 2:
                    dep.append(t_sacc[i - 2])
                t_st[i] = S.op("pe", lambda p, i=i, s2=s2: p.matmul(
                    pS[:, s2 * 65:(s2 + 1) * 65], Kf_all[:, i * DK:(i + 1) * DK],
                    V_all[:, i * 65:(i + 1) * 65], start=True, stop=True), after=dep)
                t_sacc[i] = S.op("dve", lambda vv, s2=s2: vv.tensor_tensor(
                    S32_sb[:], S32_sb[:], pS[:, s2 * 65:(s2 + 1) * 65], ALU.add),
                    after=[t_st[i], t_szero2])
                if i + 1 < NT:
                    t_Ssb[i + 1] = S.op("dve", lambda vv: vv.tensor_copy(S_sb[:], S32_sb[:]),
                                        after=[t_sacc[i], t_num[i]])
                td1 = S.op("dve", lambda vv, no2=no2: vv.tensor_scalar(
                    sm[:, 0:1], pVAN[:, no2 + 64:no2 + 65], 1e-6, None, ALU.add), after=[t_num[i]])
                td2 = S.op("dve", lambda vv: vv.reciprocal(sm[:, 1:2], sm[:, 0:1]), after=[td1])
                dep = [td2]
                if i >= 2:
                    dep.append(t_aoT[i - 2])
                t_ao[i] = S.op("dve", lambda vv, s2=s2, no2=no2: vv.tensor_scalar(
                    ao_sb[:, s2 * 64:(s2 + 1) * 64], pVAN[:, no2:no2 + 64], sm[:, 1:2], None, ALU.mult), after=dep)
            def att_pass2a2(i):
                s4, s2 = i % 3, i % 2
                dep = [t_ao[i]]
                if i >= 1:
                    dep.append(t_aoTs[i - 1])
                else:
                    dep.append(t_vsb[NT - 1])
                t_aoT[i] = S.op("pe", lambda p, s2=s2: p.transpose(
                    tp_att[0:64, 0:128], ao_sb[:, s2 * 64:(s2 + 1) * 64], idn[:]), after=dep)
                dep = [t_aoT[i]]
                if i >= 2:
                    dep.append(t_wo[i - 2])
                t_aoTs[i] = S.op("dve", lambda vv, s2=s2: vv.tensor_copy(
                    aoT_sb[:, s2 * 128:(s2 + 1) * 128], tp_att[0:64, 0:128]), after=dep)
                dep = [t_aoTs[i], t_awb]
                if i >= 1:
                    dep.append(t_r[i - 1])
                S.op("pe", lambda p, s2=s2: p.matmul(
                    wo_ps[:, 0:512], aoT_sb[:, s2 * 128:(s2 + 1) * 128], wo_b[:, 0:512],
                    start=True, stop=True), after=dep)
                t_wo[i] = S.op("pe", lambda p, s2=s2: p.matmul(
                    wo_ps[:, 512:1024], aoT_sb[:, s2 * 128:(s2 + 1) * 128], wo_b[:, 512:1024],
                    start=True, stop=True))
                dep = [t_wo[i], t_x2[i]]
                if i >= 2:
                    dep.append(t_ln[i - 2])
                t_r[i] = S.op("dve", lambda vv, s4=s4, s2=s2, i=i: vv.scalar_tensor_tensor(
                    r_sb[:, s2 * D:(s2 + 1) * D], xf[:, s4 * D:(s4 + 1) * D], 1.0, wo_ps[:],
                    ALU.mult, ALU.add, accum_out=smc(i, 0)), after=dep)
                out_deps = []
                if i >= 2:
                    out_deps += [t_x1w[i - 2], t_x1b[i - 2]]
                t_ln[i] = ln_chain(i, t_r[i], r_sb[:, s2 * D:(s2 + 1) * D],
                                   x1f_sb[:, s2 * D:(s2 + 1) * D],
                                   x1f_sb[:, s2 * D:(s2 + 1) * D], out_deps)

            def att_pass2b(i):
                s2 = i % 2
                dep = [t_ln[i]]
                if i >= 2:
                    dep += [t_xagw[i - 2], t_x1bT[i - 2]]
                t_x1b[i] = S.op("dve", lambda vv, s2=s2: vv.tensor_copy(
                    x1b_sb[:, s2 * D:(s2 + 1) * D], x1f_sb[:, s2 * D:(s2 + 1) * D]), after=dep)
                t_x1w[i] = S.dma("sp", "x1w%d" % s2, lambda s, i=i, s2=s2: s.dma_start(
                    out=x1d[i * 128:(i + 1) * 128, :], in_=x1f_sb[:, s2 * D:(s2 + 1) * D]), after=[t_ln[i]])
                t_xagw[i] = S.dma("sp", "xag%s%d" % ("AB"[i // 8], s2), lambda s, i=i, s2=s2: s.dma_start(
                    out=xag_i[i * 128:(i + 1) * 128, :], in_=x1b_sb[:, s2 * D:(s2 + 1) * D]), after=[t_x1b[i]])
                dep = [t_x1b[i]]
                if i >= 1:
                    dep.append(t_x1bT[i - 1])
                tr = None
                for dc in range(8):
                    tr = S.op("pe", lambda p, dc=dc, s2=s2: p.transpose(
                        tp_b[:, dc * 128:(dc + 1) * 128],
                        x1b_sb[:, s2 * D + dc * 128:s2 * D + (dc + 1) * 128], idn[:]),
                        after=dep if dc == 0 else ())
                t_x1bT[i] = S.op("dve", lambda vv, i=i: vv.tensor_copy(
                    x1bT[:, i * D:(i + 1) * D], tp_b[:]), after=[tr])

            for i in range(NT):
                att_pass1(i)
            qkv_wide()
            kfv_trans()

            t_bb0 = S.op("pe", lambda p: p.matmul(pm[:, 64:96], onesf[:], gbias[:], start=True, stop=True),
                         after=[t_consts, t_qkve[("q", 3)], t_qkve[("k", 3)]])
            t_bb = S.op("dve", lambda vv: vv.tensor_copy(bias_b[:], pm[:, 64:96]), after=[t_bb0])

            t_lg, t_lge = {}, {}
            t_cse_guard = [None]

            def routing(j):
                dep = [t_x1bT[j], t_gw, t_bb]
                if j >= 1:
                    dep.append(t_lge[j - 1])
                mm = None
                for dc in range(8):
                    mm = S.op("pe", lambda p, dc=dc, j=j: p.matmul(
                        pm[:, 0:32], x1bT[:, j * D + dc * 128:j * D + (dc + 1) * 128],
                        gw_b[:, dc * E:(dc + 1) * E],
                        start=(dc == 0), stop=(dc == 7)), after=dep if dc == 0 else ())
                t_lg[j] = mm
                t_lge[j] = S.op("dve", lambda vv, j=j: vv.tensor_copy(
                    lg_all[:, j * E:(j + 1) * E], pm[:, 0:32]), after=[mm])

            def v3(ap):
                return ap.rearrange("p (j e) -> p j e", e=E)

            def routing_batch():
                t_s = S.op("act", lambda s: s.activation(out=sc_all[:], in_=lg_all[:], func=AF.Sigmoid),
                           after=[t_lge[NT - 1]])
                last = S.op("dve", lambda vv: vv.tensor_tensor(
                    v3(u_all[:]), v3(sc_all[:]),
                    bias_b[:].rearrange("p (c e) -> p c e", c=1).broadcast_to([128, NT, E]),
                    ALU.add), after=[t_s, t_bb])
                for k in range(4):
                    mxc = tk_sb[:, k * NT:(k + 1) * NT]
                    last = S.op("dve", lambda vv, mxc=mxc: vv.reduce_max(
                        mxc, v3(u_all[:]), axis=AX.X), after=[last])
                    hk = hh_sb[:, k * NT * E:(k + 1) * NT * E]
                    last = S.op("dve", lambda vv, hk=hk, mxc=mxc: vv.tensor_tensor(
                        v3(hk), v3(u_all[:]),
                        mxc.rearrange("p (j e) -> p j e", e=1).broadcast_to([128, NT, E]),
                        ALU.is_equal), after=[last])
                    last = S.op("dve", lambda vv, hk=hk: vv.tensor_tensor(
                        tmpW[:], hk, sc_all[:], ALU.mult), after=[last])
                    last = S.op("dve", lambda vv, k=k: vv.reduce_sum(
                        wg_sb[:, k * NT:(k + 1) * NT], v3(tmpW[:]), axis=AX.X), after=[last])
                    last = S.op("dve", lambda vv, hk=hk: vv.tensor_tensor(
                        v3(tmpW[:]), v3(hk),
                        iotaE[:].rearrange("p (c e) -> p c e", c=1).broadcast_to([128, NT, E]),
                        ALU.mult), after=[last])
                    last = S.op("dve", lambda vv, k=k: vv.reduce_sum(
                        ek_sb[:, k * NT:(k + 1) * NT], v3(tmpW[:]), axis=AX.X), after=[last])
                    if k < 3:
                        last = S.op("dve", lambda vv, hk=hk: vv.scalar_tensor_tensor(
                            u_all[:], hk, -1e30, u_all[:], ALU.mult, ALU.add), after=[last])
                lastI = S.op("dve", lambda vv: vv.tensor_tensor(
                    tmpW[:], hh_sb[:, 0:NT * E], hh_sb[:, NT * E:2 * NT * E], ALU.add), after=[last])
                lastI = S.op("dve", lambda vv: vv.tensor_tensor(
                    tmpW[:], tmpW[:], hh_sb[:, 2 * NT * E:3 * NT * E], ALU.add), after=[lastI])
                t_I = S.op("dve", lambda vv: vv.tensor_tensor(
                    I_sb[:], tmpW[:], hh_sb[:, 3 * NT * E:4 * NT * E], ALU.add), after=[lastI])
                t_csm = S.op("pe", lambda p: p.matmul(
                    prb[0:1, :], onesb[:], I_sb[:], start=True, stop=True), after=[t_I, t_consts])
                t_cse = S.op("dve", lambda vv: vv.tensor_copy(cs_all[:], prb[0:1, :]), after=[t_csm])
                t_cse_guard[0] = t_cse
                last3 = t_rb0
                for j in range(1, NT):
                    last3 = S.op("dve", lambda vv, j=j: vv.tensor_tensor(
                        rb_sb[0:1, j * E:(j + 1) * E], rb_sb[0:1, (j - 1) * E:j * E],
                        cs_all[0:1, (j - 1) * E:j * E], ALU.add), after=[last3, t_cse])
                t_cnt = S.op("dve", lambda vv: vv.tensor_tensor(
                    counts_sb[:], rb_sb[0:1, (NT - 1) * E:NT * E],
                    cs_all[0:1, (NT - 1) * E:NT * E], ALU.add), after=[last3])
                last = S.op("dve", lambda vv: vv.tensor_tensor(
                    tk_sb[:, 4 * NT:5 * NT], wg_sb[:, 0:NT], wg_sb[:, NT:2 * NT], ALU.add), after=[last, t_cnt])
                last = S.op("dve", lambda vv: vv.tensor_tensor(
                    tk_sb[:, 4 * NT:5 * NT], tk_sb[:, 4 * NT:5 * NT], wg_sb[:, 2 * NT:3 * NT], ALU.add), after=[last])
                last = S.op("dve", lambda vv: vv.tensor_tensor(
                    tk_sb[:, 4 * NT:5 * NT], tk_sb[:, 4 * NT:5 * NT], wg_sb[:, 3 * NT:4 * NT], ALU.add), after=[last])
                last = S.op("dve", lambda vv: vv.tensor_scalar(
                    tk_sb[:, 4 * NT:5 * NT], tk_sb[:, 4 * NT:5 * NT], 1e-12, None, ALU.max), after=[last])
                last = S.op("dve", lambda vv: vv.reciprocal(
                    tk_sb[:, 5 * NT:6 * NT], tk_sb[:, 4 * NT:5 * NT]), after=[last])
                last = S.op("dve", lambda vv: vv.tensor_tensor(
                    g_sb[:].rearrange("p (k j) -> p k j", k=4),
                    wg_sb[:].rearrange("p (k j) -> p k j", k=4),
                    tk_sb[:, 5 * NT:6 * NT].rearrange("p (c j) -> p c j", c=1).broadcast_to([128, 4, NT]),
                    ALU.mult), after=[last])
                t_pc = {}
                prev_mm = None
                for c in range(4):
                    dep = [t_I, t_consts]
                    if prev_mm is not None:
                        dep.append(t_pc[c - 1])
                    mmp = S.op("pe", lambda p, c=c: p.matmul(
                        pm[:, 0:128], triS[:], I_sb[:, c * 4 * E:(c + 1) * 4 * E], start=True, stop=True),
                        after=dep)
                    prev_mm = mmp
                    t_pc[c] = S.op("dve", lambda vv, c=c: vv.tensor_copy(
                        P_all[:, c * 4 * E:(c + 1) * 4 * E], pm[:, 0:128]), after=[mmp])
                t_posk = {}
                last2 = t_pc[3]
                for k in range(4):
                    hk = hh_sb[:, k * NT * E:(k + 1) * NT * E]
                    last2 = S.op("dve", lambda vv, hk=hk: vv.tensor_tensor(
                        tmpW[:], hk, P_all[:], ALU.mult), after=[last2])
                    last2 = S.op("dve", lambda vv, k=k: vv.reduce_sum(
                        posk_sb[:, k * NT:(k + 1) * NT], v3(tmpW[:]), axis=AX.X), after=[last2])
                    t_posk[k] = last2
                return t_posk, t_cnt, last3


            for i in range(NT + 3):
                if i >= 3:
                    att_pass2b(i - 3)
                    routing(i - 3)
                if i < NT:
                    att_pass2(i)
                if 1 <= i <= NT:
                    att_pass2a2(i - 1)
            if PHASE == 11:
                es_att.close()
                S._wait("sp", [S.token(k) for k in list(S.cnt) if S.cnt[k] > 0])
                S.dma("sp", "outw", lambda s: s.dma_start(out=out_d, in_=x1d))
                S._wait("sp", [S.token("outw")])
                return

            t_attB = [S.token("pe"), S.token("dve"), S.token("act")]

            def finish_partial(gates):
                es_att.close()
                es_x1.close()
                S._wait("sp", [S.token(k) for k in list(S.cnt) if S.cnt[k] > 0])
                S.dma("sp", "outw", lambda s: s.dma_start(out=out_d, in_=x1d))
                S._wait("sp", [S.token("outw")])

            if PHASE == 1:
                finish_partial([])
                return

            es_att.close()
            psd["tp2"] = es.enter_context(nc.psum_tensor("tp2B", [128, 512], BF16))
            psd["h1"] = es.enter_context(nc.psum_tensor("h1B", [128, 1024], F32))
            psd["h3"] = es.enter_context(nc.psum_tensor("h3B", [128, 512], F32))
            psd["y"] = es.enter_context(nc.psum_tensor("yB", [128, 1024], F32))
            def make_ffn_pipe(nt_total, weights, src_tok, dst_write):
                st = {"mm": {}, "sil": {}, "mul": {}, "tr": {}, "hT": {}, "w2": {}, "y": {}, "o": {}}

                def w13(g):
                    if not (0 <= g < nt_total):
                        return
                    tb = g % 2
                    w1ap, w3ap, w2ap, wtoks = weights(g)
                    stok, xTap = src_tok(g)
                    dep = [stok] + list(wtoks)
                    if g >= 2:
                        dep.append(st["sil"][g - 2])
                    mm = None
                    for dc in range(8):
                        mm = S.op("pe", lambda p, dc=dc, tb=tb, w1ap=w1ap, xTap=xTap: p.matmul(
                            psd["h1"][:, tb * II:(tb + 1) * II], xTap(dc), w1ap(dc),
                            start=(dc == 0), stop=(dc == 7)), after=dep if dc == 0 else ())
                    dep2 = [st["mul"][g - 1]] if g >= 1 else []
                    for dc in range(8):
                        mm = S.op("pe", lambda p, dc=dc, w3ap=w3ap, xTap=xTap: p.matmul(
                            psd["h3"][:], xTap(dc), w3ap(dc),
                            start=(dc == 0), stop=(dc == 7)), after=dep2 if dc == 0 else ())
                    st["mm"][g] = mm

                def post13(g):
                    if not (0 <= g < nt_total):
                        return
                    tb = g % 2
                    st["sil"][g] = S.op("act", lambda s, tb=tb: s.activation(
                        out=s1_sb[:, tb * II:(tb + 1) * II], in_=psd["h1"][:, tb * II:(tb + 1) * II],
                        func=AF.Silu), after=[st["mm"][g]])
                    dep = [st["sil"][g]]
                    if g >= 2:
                        dep.append(st["tr"][g - 2])
                    st["mul"][g] = S.op("dve", lambda vv, tb=tb: vv.tensor_tensor(
                        h_pipe[:, tb * II:(tb + 1) * II], s1_sb[:, tb * II:(tb + 1) * II],
                        psd["h3"][:], ALU.mult), after=dep)

                def trh(g):
                    if not (0 <= g < nt_total):
                        return
                    tb = g % 2
                    dep = [st["mul"][g], t_consts]
                    if g >= 1:
                        dep.append(st["hT"][g - 1])
                    tr2 = None
                    for ic in range(4):
                        tr2 = S.op("pe", lambda p, ic=ic, tb=tb: p.transpose(
                            psd["tp2"][:, ic * 128:(ic + 1) * 128],
                            h_pipe[:, tb * II + ic * 128:tb * II + (ic + 1) * 128], idn[:]),
                            after=dep if ic == 0 else ())
                    st["tr"][g] = tr2
                    dep = [tr2]
                    if g >= 2:
                        dep.append(st["w2"][g - 2])
                    st["hT"][g] = S.op("dve", lambda vv, tb=tb: vv.tensor_copy(
                        hT_sb[:, tb * II:(tb + 1) * II], psd["tp2"][:]), after=dep)

                def w2f(g):
                    if not (0 <= g < nt_total):
                        return
                    tb = g % 2
                    _, _, w2ap, _ = weights(g)
                    dep = [st["hT"][g]]
                    if g >= 1:
                        dep.append(st["y"][g - 1])
                    for ic in range(4):
                        for half in range(2):
                            mm2 = S.op("pe", lambda p, ic=ic, half=half, tb=tb, w2ap=w2ap: p.matmul(
                                psd["y"][:, half * 512:(half + 1) * 512],
                                hT_sb[:, tb * II + ic * 128:tb * II + (ic + 1) * 128],
                                w2ap(ic, half),
                                start=(ic == 0), stop=(ic == 3)), after=dep if (ic == 0 and half == 0) else ())
                    st["w2"][g] = mm2

                def postw2(g):
                    if not (0 <= g < nt_total):
                        return
                    tb = g % 2
                    dep = [st["w2"][g]]
                    if g >= 2:
                        dep.append(st["o"][g - 2])
                    st["y"][g] = S.op("act", lambda s, tb=tb: s.activation(
                        out=ye_sb[:, tb * D:(tb + 1) * D], in_=psd["y"][:], func=AF.Copy), after=dep)
                    st["o"][g] = dst_write(g, tb, st["y"][g])

                return st, (w13, post13, trh, w2f, postw2)

            def sh_weights(g):
                return (lambda dc: w1e[:, 8 * II + dc * II:8 * II + (dc + 1) * II],
                        lambda dc: w3e[:, 8 * II + dc * II:8 * II + (dc + 1) * II],
                        lambda ic, half: w2e[:, 4 * D + ic * D + half * 512:4 * D + ic * D + (half + 1) * 512],
                        [t_sw] + t_attB)

            def sh_src(g):
                return (t_x1bT[g], lambda dc, g=g: x1bT[:, g * D + dc * 128:g * D + (dc + 1) * 128])

            def sh_dst(g, tb, ytok):
                return S.dma("sp", "ysh%d" % tb, lambda s, g=g, tb=tb: s.dma_start(
                    out=yshd[g * 128:(g + 1) * 128, :], in_=ye_sb[:, tb * D:(tb + 1) * D]), after=[ytok])

            stS, (sh_w13, sh_post13, sh_trh, sh_w2, sh_postw2) = make_ffn_pipe(
                NT, sh_weights, sh_src, sh_dst)

            t_posk, t_cnt, t_rbend = routing_batch()
            t_ag2a = S.collective(lambda g: g.collective_compute(
                "AllGather", ALU.bypass, replica_groups=[list(range(NCORES))],
                ins=[xag_i[0:T // 2, :]],
                outs=[xall[0:NCORES * T // 2, :]]),
                after=[S.token("xagA0"), S.token("xagA1")])
            t_cntw = S.dma("sp", "cntw", lambda s: s.dma_start(out=cnt_i, in_=counts_sb[:]), after=[t_cnt])
            t_ag1 = S.collective(lambda g: g.collective_compute(
                "AllGather", ALU.bypass, replica_groups=[list(range(NCORES))],
                ins=[cnt_i], outs=[cnt_o]), after=[t_cntw])
            t_ag2 = S.collective(lambda g: g.collective_compute(
                "AllGather", ALU.bypass, replica_groups=[list(range(NCORES))],
                ins=[xag_i[T // 2:T, :]],
                outs=[xall[NCORES * T // 2:NCORES * T, :]]),
                after=[S.token("xagB0"), S.token("xagB1")])
            t_cl = S.dma("sp", "cl", lambda s: s.dma_start(out=call_sb[:], in_=cnt_o), after=[t_ag1])
            t_bm = S.op("pe", lambda p: p.matmul(prb[0:1, 0:E], cmask[:], call_sb[:], start=True, stop=True),
                        after=[t_cl, t_consts, t_cnt, t_cse_guard[0]])
            t_base = S.op("dve", lambda vv: vv.tensor_copy(base_sb[:], prb[0:1, 0:E]), after=[t_bm])

            brs = {}

            def breg(name, val):
                if name not in brs:
                    r = nc.gpsimd.alloc_register(name)
                    nc.gpsimd.reg_mov(r, val)
                    brs[name] = r
                return brs[name]

            last = S.op("dve", lambda vv: vv.tensor_tensor(
                rt_all[0:1, :].rearrange("p (j e) -> p j e", e=E),
                rb_sb[0:1, :].rearrange("p (j e) -> p j e", e=E),
                base_sb[:].rearrange("p (c e) -> p c e", c=1).broadcast_to([1, NT, E]),
                ALU.add), after=[t_base, t_rbend])
            t_rbt = {}
            prevb = None
            for c in range(4):
                dep = [last, t_consts, t_lge[NT - 1]]
                if prevb is not None:
                    dep.append(t_rbt[c - 1])
                bmm = S.op("pe", lambda p, c=c: p.matmul(
                    pm[:, 0:128], onesf[:], rt_all[0:1, c * 128:(c + 1) * 128], start=True, stop=True),
                    after=dep)
                prevb = bmm
                t_rbt[c] = S.op("dve", lambda vv, c=c: vv.tensor_copy(
                    rbt_all[:, c * 128:(c + 1) * 128], pm[:, 0:128]), after=[bmm])
            last = t_rbt[3]
            t_sidx = {}
            for k in range(4):
                hk = hh_sb[:, k * NT * E:(k + 1) * NT * E]
                last = S.op("dve", lambda vv, hk=hk: vv.tensor_tensor(
                    tmpW[:], hk, rbt_all[:], ALU.mult), after=[last, t_posk[k]])
                last = S.op("dve", lambda vv: vv.reduce_sum(
                    tk_sb[:, 0:NT], v3(tmpW[:]), axis=AX.X), after=[last])
                last = S.op("dve", lambda vv, k=k: vv.tensor_tensor(
                    tk_sb[:, NT:2 * NT], tk_sb[:, 0:NT], posk_sb[:, k * NT:(k + 1) * NT], ALU.add), after=[last])
                last = S.op("dve", lambda vv: vv.tensor_scalar(
                    tk_sb[:, 2 * NT:3 * NT], tk_sb[:, NT:2 * NT], float(CAP), None, ALU.is_lt), after=[last])
                last = S.op("dve", lambda vv, k=k: vv.tensor_tensor(
                    gf_sb[:, k * NT:(k + 1) * NT], g_sb[:, k * NT:(k + 1) * NT],
                    tk_sb[:, 2 * NT:3 * NT], ALU.mult), after=[last])
                last = S.op("dve", lambda vv, k=k: vv.scalar_tensor_tensor(
                    tk_sb[:, 3 * NT:4 * NT], ek_sb[:, k * NT:(k + 1) * NT], float(CAP),
                    tk_sb[:, NT:2 * NT], ALU.mult, ALU.add), after=[last])
                last = S.op("dve", lambda vv: vv.scalar_tensor_tensor(
                    tk_sb[:, 4 * NT:5 * NT], tk_sb[:, 2 * NT:3 * NT], -OOB,
                    tk_sb[:, 3 * NT:4 * NT], ALU.mult, ALU.add), after=[last])
                last = S.op("dve", lambda vv, k=k: vv.tensor_scalar(
                    sidx_i[:, k * NT:(k + 1) * NT], tk_sb[:, 4 * NT:5 * NT], OOB, None, ALU.add), after=[last])
                t_sidx[k] = last

            for j in range(NT):
                for k in range(4):
                    col = k * NT + j
                    S.dma("pool", "scat", lambda g, j=j, col=col, k=k: g.indirect_dma_start(
                        out=amap_i,
                        out_offset=IndirectOffsetOnAxis(sidx_i[:, col:col + 1], axis=0),
                        in_=tglob[:, j:j + 1],
                        in_offset=None,
                        bounds_check=breg("b_amap", E * CAP - 1),
                        oob_is_err=False), after=[t_sidx[k], t_amz, t_consts])
            t_ar = S.collective(lambda g: g.collective_compute(
                "AllReduce", ALU.add, replica_groups=[list(range(NCORES))],
                ins=[amap_i.rearrange("(a b) c -> a (b c)", a=E * CAP // 128)],
                outs=[amap_o.rearrange("(a b) c -> a (b c)", a=E * CAP // 128)]),
                after=[S.token("scat")])

            for i in range(NT + 2):
                sh_trh(i - 1)
                sh_w13(i)
                sh_w2(i - 1)
                sh_post13(i)
                sh_postw2(i - 1)

            t_am = S.dma("pool", "amg", lambda g: g.indirect_dma_start(
                out=am_sb[:],
                out_offset=None,
                in_=amap_o.rearrange("(r q) c -> r (q c)", q=128),
                in_offset=IndirectOffsetOnAxis(oidx[:], axis=0),
                bounds_check=breg("b_amr", E * CAP // 128 - 1),
                oob_is_err=False), after=[t_ar, t_consts])
            t_amt = S.op("pe", lambda p: p.transpose(pm[:, 0:32], am_sb[:], idnf[0:32, 0:32]),
                         after=[t_am, t_rbt[3], t_consts])
            tg = S.op("dve", lambda vv: vv.tensor_copy(gi32[:], pm[:, 0:32]), after=[t_amt])
            tg = S.op("dve", lambda vv: vv.tensor_scalar(tmpW[:, 0:E], gi32[:], 0.5, None, ALU.is_gt), after=[tg])
            tg = S.op("dve", lambda vv: vv.scalar_tensor_tensor(
                gi32[:], tmpW[:, 0:E], -OOB, gi32[:], ALU.mult, ALU.add), after=[tg])
            tg = S.op("dve", lambda vv: vv.tensor_scalar(gi32[:], gi32[:], OOB - 1.0, None, ALU.add), after=[tg])
            t_gidx = S.op("dve", lambda vv: vv.tensor_copy(gidx_i[:], gi32[:]), after=[tg])

            if PHASE == 3:
                finish_partial([])
                return

            if PHASE == 2:
                finish_partial([])
                return
            load_expert_w(1, stS["mm"][NT - 1])

            tpX = pk[:, :].bitcast(BF16)
            NGE = EPC * (CAP // 128)
            t_xeg, t_xetr, t_xeT = {}, {}, {}
            t_shend = [stS["o"][NT - 1], stS["o"][NT - 2]]

            def xe_weights(g):
                e = g // (CAP // 128)
                eb = e % 2
                return (lambda dc, eb=eb: w1e[:, eb * 8 * II + dc * II:eb * 8 * II + (dc + 1) * II],
                        lambda dc, eb=eb: w3e[:, eb * 8 * II + dc * II:eb * 8 * II + (dc + 1) * II],
                        lambda ic, half, eb=eb: w2e[:, eb * 4 * D + ic * D + half * 512:eb * 4 * D + ic * D + (half + 1) * 512],
                        [t_wexp[e]] + t_shend)

            def xe_gather(g):
                if not (0 <= g < NGE):
                    return
                s3 = g % 3
                dep = [t_gidx, t_ag2, t_xez]
                if g >= 3:
                    dep.append(t_xetr[g - 3])
                t_xeg[g] = S.dma("pool", "xeg%d" % s3, lambda gg, g=g, s3=s3: gg.indirect_dma_start(
                    out=xe_t[:, s3 * D:(s3 + 1) * D],
                    out_offset=None,
                    in_=xall,
                    in_offset=IndirectOffsetOnAxis(gidx_i[:, g:g + 1], axis=0),
                    bounds_check=breg("b_xall", NCORES * T - 1),
                    oob_is_err=False), after=dep)

            def xe_src(g):
                return (t_xeT[g], lambda dc, g=g: xeT_sb[:, (g % 2) * D + dc * 128:(g % 2) * D + (dc + 1) * 128])

            def xe_trans(g, st):
                if not (0 <= g < NGE):
                    return
                s3, tb = g % 3, g % 2
                dep = [t_xeg[g], t_consts] + (t_attB if g == 0 else [])
                if g >= 1:
                    dep.append(t_xeT[g - 1])
                tr = None
                for dc in range(8):
                    tr = S.op("pe", lambda p, dc=dc, s3=s3: p.transpose(
                        tpX[:, dc * 128:(dc + 1) * 128],
                        xe_t[:, s3 * D + dc * 128:s3 * D + (dc + 1) * 128], idn[:]),
                        after=dep if dc == 0 else ())
                t_xetr[g] = tr
                dep = [tr]
                if g >= 2:
                    dep.append(st["mm"][g - 2])
                t_xeT[g] = S.op("dve", lambda vv, tb=tb: vv.tensor_copy(
                    xeT_sb[:, tb * D:(tb + 1) * D], tpX[:]), after=dep)

            def xe_dst(g, tb, ytok):
                strm = "yew%s%d" % ("ABCD"[g // (NGE // 4)], tb)
                return S.dma("sp", strm, lambda s, g=g, strm=strm: s.dma_start(
                    out=yec[g * 128:(g + 1) * 128, :], in_=ye_sb[:, (g % 2) * D:(g % 2 + 1) * D]), after=[ytok])

            stX, (xe_w13, xe_post13, xe_trh, xe_w2, xe_postw2) = make_ffn_pipe(
                NGE, xe_weights, xe_src, xe_dst)
            t_yewk = {}

            xe_gather(0)
            xe_gather(1)
            xe_gather(2)
            xe_trans(0, stX)
            for g in range(NGE + 2):
                xe_gather(g + 3)
                xe_trans(g + 1, stX)
                xe_trh(g - 1)
                xe_w13(g)
                xe_w2(g - 1)
                xe_post13(g)
                xe_postw2(g - 1)
                ntile = CAP // 128
                if (g - 1) % ntile == ntile - 1 and 0 <= g - 1 < NGE:
                    e = (g - 1) // ntile
                    t_yewk[e] = stX["o"][g - 1]
                    if e + 2 < EPC:
                        load_expert_w(e + 2, stX["w2"][g - 1])

            if PHASE == 4:
                finish_partial([])
                return
            for q in range(4):
                t_ag3 = S.collective(lambda g, q=q: g.collective_compute(
                    "AllGather", ALU.bypass, replica_groups=[list(range(NCORES))],
                    ins=[yec[q * CAP:(q + 1) * CAP, :]],
                    outs=[yeall[q * NCORES * CAP:(q + 1) * NCORES * CAP, :]]),
                    after=[S.token("yew%s0" % "ABCD"[q]), S.token("yew%s1" % "ABCD"[q])])

            es_x1.close()
            yg_sb = es.enter_context(nc.sbuf_tensor("yg_sb", [128, 8 * D], BF16))
            acc_sb = es.enter_context(nc.sbuf_tensor("acc_sb", [128, 3 * D], F32))
            o_sb = es.enter_context(nc.sbuf_tensor("o_sb", [128, 2 * D], F32))
            accb_sb = es.enter_context(nc.sbuf_tensor("accb_sb", [128, 2 * D], F32))
            gys_sb = es.enter_context(nc.sbuf_tensor("gys_sb", [128, 4 * D], BF16))
            t_ygz = S.op("dve", lambda vv: vv.memset(yg_sb[:], 0.0), after=[t_gidx])
            t_x1r, t_yshr, t_ygv, t_acc, t_o, t_ow = {}, {}, {}, {}, {}, {}
            t_ta1, t_ta2, t_tg2, t_tg3, t_tb1, t_txy = {}, {}, {}, {}, {}, {}

            def cmb_main(j):
                s2, s3 = j % 2, j % 3
                dep = [t_x1w[NT - 1], t_x1w[NT - 2], t_gidx]
                if j >= 2:
                    dep.append(t_acc[j - 2])
                t_x1r[j] = S.dma("sp", "x1r%d" % s2, lambda s, j=j, s2=s2: s.dma_start(
                    out=accb_sb[:, s2 * D:(s2 + 1) * D], in_=x1d[j * 128:(j + 1) * 128, :]), after=dep)
                dep = [S.token("ysh0"), S.token("ysh1")]
                if j >= 2:
                    dep.append(t_txy[j - 2])
                t_yshr[j] = S.dma("sp", "yshr%d" % s2, lambda s, j=j, s2=s2: s.dma_start(
                    out=ysh_sb[:, s2 * D:(s2 + 1) * D], in_=yshd[j * 128:(j + 1) * 128, :]), after=dep)
                for k in range(4):
                    col = k * NT + j
                    slot = s2 * 4 + k
                    dep = [t_ag3, t_sidx[k], t_ygz]
                    if j >= 2:
                        dep.append((t_ta1[j - 2], t_ta2[j - 2], t_tg2[j - 2], t_tg3[j - 2])[k])
                    t_ygv[col] = S.dma("pool", "yg%d" % slot, lambda g, col=col, slot=slot: g.indirect_dma_start(
                        out=yg_sb[:, slot * D:(slot + 1) * D],
                        out_offset=None,
                        in_=yeall,
                        in_offset=IndirectOffsetOnAxis(sidx_i[:, col:col + 1], axis=0),
                        bounds_check=breg("b_ye", E * CAP - 1),
                        oob_is_err=False), after=dep)
                dep = [t_ygv[0 * NT + j]]
                if j >= 3:
                    dep.append(t_o[j - 3])
                t_ta1[j] = S.op("dve", lambda vv, j=j, s2=s2: vv.tensor_scalar(
                    acc_sb[:, s3 * D:(s3 + 1) * D], yg_sb[:, (s2 * 4) * D:(s2 * 4 + 1) * D],
                    gf_sb[:, 0 * NT + j:0 * NT + j + 1], None, ALU.mult), after=dep)
                t_ta2[j] = S.op("dve", lambda vv, j=j, s2=s2: vv.scalar_tensor_tensor(
                    acc_sb[:, s3 * D:(s3 + 1) * D], yg_sb[:, (s2 * 4 + 1) * D:(s2 * 4 + 2) * D],
                    gf_sb[:, 1 * NT + j:1 * NT + j + 1], acc_sb[:, s3 * D:(s3 + 1) * D],
                    ALU.mult, ALU.add), after=[t_ta1[j], t_ygv[1 * NT + j]])
                t_tg2[j] = S.op("act", lambda s, j=j, s2=s2: s.activation(
                    out=gys_sb[:, (s2 * 2) * D:(s2 * 2 + 1) * D],
                    in_=yg_sb[:, (s2 * 4 + 2) * D:(s2 * 4 + 3) * D], func=AF.Copy,
                    scale=gf_sb[:, 2 * NT + j:2 * NT + j + 1]),
                    after=[t_ygv[2 * NT + j]] + ([t_tb1[j - 2]] if j >= 2 else []))
                t_tg3[j] = S.op("dve", lambda vv, j=j, s2=s2: vv.tensor_scalar(
                    gys_sb[:, (s2 * 2 + 1) * D:(s2 * 2 + 2) * D],
                    yg_sb[:, (s2 * 4 + 3) * D:(s2 * 4 + 4) * D],
                    gf_sb[:, 3 * NT + j:3 * NT + j + 1], None, ALU.mult),
                    after=[t_ygv[3 * NT + j]] + ([t_tb1[j - 2]] if j >= 2 else []))
            def cmb_mainb(j):
                s2, s3 = j % 2, j % 3
                t_tb1[j] = S.op("dve", lambda vv, s2=s2: vv.tensor_tensor(
                    acc_sb[:, s3 * D:(s3 + 1) * D], acc_sb[:, s3 * D:(s3 + 1) * D],
                    gys_sb[:, (s2 * 2) * D:(s2 * 2 + 1) * D], ALU.add), after=[t_ta2[j], t_tg2[j]])
                tb2 = S.op("dve", lambda vv, s2=s2: vv.tensor_tensor(
                    acc_sb[:, s3 * D:(s3 + 1) * D], acc_sb[:, s3 * D:(s3 + 1) * D],
                    gys_sb[:, (s2 * 2 + 1) * D:(s2 * 2 + 2) * D], ALU.add), after=[t_tb1[j], t_tg3[j]])
                dep = [t_x1r[j], t_yshr[j]]
                t_txy[j] = S.op("dve", lambda vv, s2=s2: vv.tensor_tensor(
                    accb_sb[:, s2 * D:(s2 + 1) * D], accb_sb[:, s2 * D:(s2 + 1) * D],
                    ysh_sb[:, s2 * D:(s2 + 1) * D], ALU.add), after=dep)
                t_acc[j] = S.op("dve", lambda vv, j=j, s2=s2: vv.scalar_tensor_tensor(
                    acc_sb[:, s3 * D:(s3 + 1) * D], acc_sb[:, s3 * D:(s3 + 1) * D], 1.0,
                    accb_sb[:, s2 * D:(s2 + 1) * D], ALU.mult, ALU.add,
                    accum_out=smc(j, 0)), after=[tb2, t_txy[j]])

            def cmb_fin(j):
                s2, s3 = j % 2, j % 3
                out_deps = [t_ln[NT - 1]]
                if j >= 2:
                    out_deps.append(t_ow[j - 2])
                t_o[j] = ln_chain(j, t_acc[j], acc_sb[:, s3 * D:(s3 + 1) * D],
                                  o_sb[:, s2 * D:(s2 + 1) * D],
                                  o_sb[:, s2 * D:(s2 + 1) * D], out_deps)
                t_ow[j] = S.dma("sp", "outw%d" % s2, lambda s, j=j, s2=s2: s.dma_start(
                    out=out_d[j * 128:(j + 1) * 128, :], in_=o_sb[:, s2 * D:(s2 + 1) * D]), after=[t_o[j]])

            for j in range(NT + 3):
                if j >= 3:
                    cmb_fin(j - 3)
                if j < NT:
                    cmb_main(j)
                if 1 <= j <= NT:
                    cmb_mainb(j - 1)

            S._wait("sp", [S.token("outw0"), S.token("outw1")])
    return nc


_NC_CACHE = None


def _get_nc():
    global _NC_CACHE
    if _NC_CACHE is None:
        _NC_CACHE = build_nc()
    return _NC_CACHE


def _bf16(a):
    a = np.asarray(a)
    if a.dtype == np.uint16:
        return a.view(ml_dtypes.bfloat16)
    return a.astype(ml_dtypes.bfloat16) if a.dtype != ml_dtypes.bfloat16 else a


def kernel(L0_attn_x, L0_attn_Wq, L0_attn_Wk, L0_attn_Wv, L0_attn_Wo,
           L0_ffn_gate_w, L0_ffn_bias, L0_ffn_W1, L0_ffn_W3, L0_ffn_W2,
           L0_ffn_sw1, L0_ffn_sw3, L0_ffn_sw2):
    x = np.ascontiguousarray(np.asarray(L0_attn_x, dtype=np.float32))
    wqT = np.ascontiguousarray(np.asarray(L0_attn_Wq, dtype=np.float32).T)
    wkT = np.ascontiguousarray(np.asarray(L0_attn_Wk, dtype=np.float32).T)
    wvT = np.ascontiguousarray(np.asarray(L0_attn_Wv, dtype=np.float32).T)
    woT = np.ascontiguousarray(np.asarray(L0_attn_Wo, dtype=np.float32).T)
    gwT = np.ascontiguousarray(_bf16(L0_ffn_gate_w).T)
    gbias = np.ascontiguousarray(np.asarray(L0_ffn_bias, dtype=np.float32).reshape(1, E))
    eold = np.arange(E)
    enew = 8 * (eold % 4) + eold // 4
    gwT_p = np.empty_like(gwT); gwT_p[:, enew] = gwT
    gbias_p = np.empty_like(gbias); gbias_p[0, enew] = gbias[0]
    gwT, gbias = np.ascontiguousarray(gwT_p), np.ascontiguousarray(gbias_p)
    W1 = _bf16(L0_ffn_W1)
    W3 = _bf16(L0_ffn_W3)
    W2 = _bf16(L0_ffn_W2)
    sw1T = np.ascontiguousarray(_bf16(L0_ffn_sw1).T)
    sw3T = np.ascontiguousarray(_bf16(L0_ffn_sw3).T)
    sw2T = np.ascontiguousarray(_bf16(L0_ffn_sw2).T)

    idn = np.eye(128, dtype=ml_dtypes.bfloat16)
    idnf = np.eye(128, dtype=np.float32)
    maskT = np.triu(np.ones((128, 128), dtype=np.float32)).astype(ml_dtypes.bfloat16)
    triS = np.triu(np.ones((128, 128), dtype=np.float32), k=1).astype(ml_dtypes.bfloat16)
    iotaE = np.tile(np.arange(E, dtype=np.float32), (128, 1))
    onesf = np.ones((1, 128), np.float32)
    onesb = np.ones((128, 1), np.float32).astype(ml_dtypes.bfloat16)

    nc = _get_nc()
    in_maps = []
    for c in range(NCORES):
        ii = np.arange(NT)[None, :]
        tglob = (np.where(ii < NT // 2, c * (T // 2) + ii * 128,
                          NCORES * T // 2 + c * (T // 2) + (ii - NT // 2) * 128)
                 + np.arange(128)[:, None] + 1).astype(np.float32)
        cmask = (np.arange(NCORES) < c).astype(np.float32).reshape(NCORES, 1)
        oidx = np.concatenate([(8 * l + c) * 8 + np.arange(8) for l in range(4)]).astype(np.int32).reshape(32, 1)
        in_maps.append({
            "x": np.ascontiguousarray(x[c]),
            "wqT": wqT, "wkT": wkT, "wvT": wvT, "woT": woT,
            "gwT": gwT, "gbias": gbias,
            "w1s": np.ascontiguousarray(W1[c * EPC:(c + 1) * EPC].reshape(EPC * D, II)),
            "w3s": np.ascontiguousarray(W3[c * EPC:(c + 1) * EPC].reshape(EPC * D, II)),
            "w2s": np.ascontiguousarray(W2[c * EPC:(c + 1) * EPC].reshape(EPC * II, D)),
            "sw1T": sw1T, "sw3T": sw3T, "sw2T": sw2T,
            "idn": idn, "idnf": idnf, "maskT": maskT, "triS": triS, "iotaE": iotaE,
            "onesf": onesf, "onesb": onesb,
            "tglob": tglob, "cmask": cmask, "oidx": oidx,
        })
    res = run_bass_kernel_spmd(nc, in_maps, core_ids=list(range(NCORES)))
    out = np.stack([np.asarray(res.results[c]["out"]) for c in range(NCORES)], axis=0)
    return out.astype(np.float32)


# version 24
# speedup vs baseline: 1.0070x; 1.0070x over previous
import contextlib
import os
import numpy as np
import ml_dtypes
import concourse.bass as bass
import concourse.mybir as mybir
from concourse.bass import IndirectOffsetOnAxis
from concourse.bass_utils import run_bass_kernel_spmd

F32 = mybir.dt.float32
BF16 = mybir.dt.bfloat16
I32 = mybir.dt.int32
AF = mybir.ActivationFunctionType
ALU = mybir.AluOpType
AX = mybir.AxisListType

NCORES = 8
T = 2048
D = 1024
DK = 64
E = 32
EPC = 4
CAP = 1024
II = 512
NT = T // 128
OOB = 2.0e6
PHASE = int(os.environ.get("KPHASE", "5"))
KCH = int(os.environ.get("KCH", "16"))
KSUB = int(os.environ.get("KSUB", "63"))
KEL = int(os.environ.get("KEL", "9"))
KW = int(os.environ.get("KW", "1"))
KP2 = int(os.environ.get("KP2", "10"))
KPV = int(os.environ.get("KPV", "4"))


class Sched:

    def __init__(self, nc, es):
        self.nc = nc
        self.sem = {}
        self.cnt = {}
        self.scale = {}
        self.observed = {e: {} for e in ("sp", "act", "pe", "dve", "pool")}
        self.es = es
        for e in ("sp", "act", "pe", "dve", "pool"):
            self._new_sem(e, 1)
        self._new_sem("cc", 1)

    def _new_sem(self, key, scale):
        self.sem[key] = self.es.enter_context(self.nc.semaphore(f"s_{key}"))
        self.cnt[key] = 0
        self.scale[key] = scale

    def stream(self, key):
        if key not in self.sem:
            self._new_sem(key, 16)
        return key

    def _eng(self, e):
        return {"sp": self.nc.sync, "act": self.nc.scalar, "pe": self.nc.tensor,
                "dve": self.nc.vector, "pool": self.nc.gpsimd}[e]

    def _wait(self, engine, tokens):
        eng = self._eng(engine)
        need = {}
        for (k, val) in tokens:
            if val > need.get(k, -1):
                need[k] = val
        for k, val in sorted(need.items()):
            if self.observed[engine].get(k, 0) < val:
                eng.wait_ge(self.sem[k], val * self.scale[k])
                self.observed[engine][k] = val

    def op(self, engine, fn, after=()):
        self._wait(engine, after)
        inst = fn(self._eng(engine))
        inst.then_inc(self.sem[engine], 1)
        self.cnt[engine] += 1
        return (engine, self.cnt[engine])

    def dma(self, engine, stream_key, fn, after=()):
        self.stream(stream_key)
        self._wait(engine, after)
        inst = fn(self._eng(engine))
        inst.then_inc(self.sem[stream_key], 16)
        self.cnt[stream_key] += 1
        return (stream_key, self.cnt[stream_key])

    def collective(self, fn, after=()):
        self._wait("pool", after)
        inst = fn(self.nc.gpsimd)
        inst.then_inc(self.sem["cc"], 1)
        self.cnt["cc"] += 1
        return ("cc", self.cnt["cc"])

    def token(self, key):
        return (key, self.cnt[key])


def build_nc():
    nc = bass.Bass("TRN2", target_bir_lowering=False, debug=False)

    dtn = lambda n, s, d, **kw: nc.dram_tensor(n, s, d, **kw).ap()
    x_in = dtn("x", [T, D], F32, kind="ExternalInput")
    wqT_in = dtn("wqT", [D, DK], F32, kind="ExternalInput")
    wkT_in = dtn("wkT", [D, DK], F32, kind="ExternalInput")
    wvT_in = dtn("wvT", [D, DK], F32, kind="ExternalInput")
    woT_in = dtn("woT", [DK, D], F32, kind="ExternalInput")
    gwT_in = dtn("gwT", [D, E], BF16, kind="ExternalInput")
    gb_in = dtn("gbias", [1, E], F32, kind="ExternalInput")
    w1_in = dtn("w1s", [EPC * D, II], BF16, kind="ExternalInput")
    w3_in = dtn("w3s", [EPC * D, II], BF16, kind="ExternalInput")
    w2_in = dtn("w2s", [EPC * II, D], BF16, kind="ExternalInput")
    sw1_in = dtn("sw1T", [D, II], BF16, kind="ExternalInput")
    sw3_in = dtn("sw3T", [D, II], BF16, kind="ExternalInput")
    sw2_in = dtn("sw2T", [II, D], BF16, kind="ExternalInput")
    idn_in = dtn("idn", [128, 128], BF16, kind="ExternalInput")
    idnf_in = dtn("idnf", [128, 128], F32, kind="ExternalInput")
    maskT_in = dtn("maskT", [128, 128], BF16, kind="ExternalInput")
    triS_in = dtn("triS", [128, 128], BF16, kind="ExternalInput")
    iotaE_in = dtn("iotaE", [128, E], F32, kind="ExternalInput")
    onesf_in = dtn("onesf", [1, 128], F32, kind="ExternalInput")
    onesb_in = dtn("onesb", [128, 1], BF16, kind="ExternalInput")
    tglob_in = dtn("tglob", [128, NT], F32, kind="ExternalInput")
    cmask_in = dtn("cmask", [NCORES, 1], F32, kind="ExternalInput")
    oidx_in = dtn("oidx", [32, 1], I32, kind="ExternalInput")

    out_d = dtn("out", [T, D], F32, kind="ExternalOutput")

    x1d = dtn("x1d", [T, D], F32, kind="Internal")
    xag_i = dtn("xag_i", [T, D], BF16, kind="Internal")
    xall = dtn("xall", [NCORES * T, D], BF16, kind="Internal",
               addr_space=("Shared" if PHASE >= 3 else "Local"))
    cnt_i = dtn("cnt_i", [1, E], F32, kind="Internal")
    cnt_o = dtn("cnt_o", [NCORES, E], F32, kind="Internal")
    amap_i = dtn("amap_i", [E * CAP, 1], F32, kind="Internal")
    amap_o = dtn("amap_o", [E * CAP, 1], F32, kind="Internal")
    yshd = dtn("yshd", [T, D], BF16, kind="Internal")
    yec = dtn("yec", [EPC * CAP, D], BF16, kind="Internal")
    yeall = dtn("yeall", [E * CAP, D], BF16, kind="Internal",
                addr_space=("Shared" if PHASE >= 3 else "Local"))
    agos = [dtn("ago%d" % k, [NCORES * CAP, D], BF16, kind="Internal") for k in range(EPC)]

    es = contextlib.ExitStack()
    sb = lambda n, s, d: es.enter_context(nc.sbuf_tensor(n, s, d))

    idn = sb("c_idn", [128, 128], BF16)
    idnf = sb("c_idnf", [128, 128], F32)
    maskT = sb("c_maskT", [128, 128], BF16)
    triS = sb("c_triS", [128, 128], BF16)
    iotaE = sb("c_iotaE", [128, E], F32)
    onesf = sb("c_onesf", [1, 128], F32)
    onesb = sb("c_onesb", [128, 1], BF16)
    tglob = sb("c_tglob", [128, NT], F32)
    cmask = sb("c_cmask", [NCORES, 1], F32)
    oidx = sb("c_oidx", [32, 1], I32)
    gbias = sb("c_gbias", [1, E], F32)
    w32a = sb("w32a", [128, 512], F32)
    wq_b = sb("wq_b", [128, 512], BF16)
    wk_b = sb("wk_b", [128, 512], BF16)
    wv_b = sb("wv_b", [128, 512], BF16)
    wo_b = sb("wo_b", [64, 1024], BF16)
    gw_b = sb("gw_b", [128, 8 * E], BF16)
    w1e = sb("w1e", [128, 2 * 8 * II], BF16)
    w3e = sb("w3e", [128, 2 * 8 * II], BF16)
    w2e = sb("w2e", [128, 2 * 4 * D], BF16)
    xf = sb("xf", [128, 3 * D], F32)
    xb = sb("xb", [128, 2 * D], BF16)
    qraw = sb("qraw", [64, 512], F32)
    Q_all = sb("Q_all", [64, NT * 128], BF16)
    K_all = sb("K_all", [64, NT * 128], BF16)
    Kf_all = sb("Kf_all", [128, NT * DK], BF16)
    V_all = sb("V_all", [128, NT * 65], BF16)
    S_sb = sb("S_sb", [64, 65], BF16)
    S32_sb = sb("S32_sb", [64, 65], F32)
    qe32 = sb("qe32", [64, 512], F32)
    At_b = sb("At_b", [128, 2 * 128], BF16)
    ao_sb = sb("ao_sb", [128, 2 * 64], BF16)
    aoT_sb = sb("aoT_sb", [64, 2 * 128], BF16)
    r_sb = sb("r_sb", [128, 2 * D], F32)
    x1f_sb = sb("x1f_sb", [128, 2 * D], F32)
    x1b_sb = sb("x1b_sb", [128, 2 * D], BF16)
    sm = sb("sm", [128, 24], F32)
    wg_sb = sb("wg_sb", [128, 4 * NT], F32)
    ek_sb = sb("ek_sb", [128, 4 * NT], F32)
    g_sb = sb("g_sb", [128, 4 * NT], F32)
    posk_sb = sb("posk_sb", [128, 4 * NT], F32)
    I_sb = sb("I_sb", [128, NT * E], BF16)
    rb_sb = sb("rb_sb", [1, NT * E], F32)
    cs_all = sb("cs_all", [1, NT * E], F32)
    counts_sb = sb("counts_sb", [1, E], F32)
    call_sb = sb("call_sb", [NCORES, E], F32)
    base_sb = sb("base_sb", [1, E], F32)
    bias_b = sb("bias_b", [128, E], F32)
    rt_all = sb("rt_all", [1, NT * E], F32)
    tk_sb = sb("tk_sb", [128, 6 * NT], F32)
    gf_sb = sb("gf_sb", [128, 4 * NT], F32)
    sidx_i = sb("sidx_i", [128, 4 * NT], I32)
    zz_sb = sb("zz_sb", [128, 256], F32)
    am_sb = sb("am_sb", [32, 128], F32)
    gi32 = sb("gi32", [128, E], F32)
    gidx_i = sb("gidx_i", [128, E], I32)
    xe_t = sb("xe_t", [128, 3 * D], BF16)
    xeT_sb = sb("xeT_sb", [128, 2 * D], BF16)
    h_pipe = sb("h_pipe", [128, 2 * II], BF16)
    hT_sb = sb("hT_sb", [128, 2 * II], BF16)
    s1_sb = sb("s1_sb", [128, 2 * II], F32)
    ye_sb = sb("ye_sb", [128, 2 * D], BF16)
    ysh_sb = sb("ysh_sb", [128, 2 * D], BF16)
    qf_sb = sb("qf_sb", [64, 128], F32)
    es_x1 = contextlib.ExitStack()
    x1bT = es_x1.enter_context(nc.sbuf_tensor("x1bT", [128, NT * D], BF16))
    xT_all = x1bT
    Vt_all = es_x1.enter_context(nc.sbuf_tensor("Vt_all", [64, NT * 128], BF16))
    lg_all = es_x1.enter_context(nc.sbuf_tensor("lg_all", [128, NT * E], F32))
    sc_all = es_x1.enter_context(nc.sbuf_tensor("sc_all", [128, NT * E], F32))
    u_all = es_x1.enter_context(nc.sbuf_tensor("u_all", [128, NT * E], F32))
    hh_sb = es_x1.enter_context(nc.sbuf_tensor("hh_sb", [128, 4 * NT * E], BF16))
    P_all = es_x1.enter_context(nc.sbuf_tensor("P_all", [128, NT * E], F32))
    rbt_all = es_x1.enter_context(nc.sbuf_tensor("rbt_all", [128, NT * E], F32))
    tmpW = es_x1.enter_context(nc.sbuf_tensor("tmpW", [128, NT * E], F32))
    cmb = {}

    pq = es.enter_context(nc.psum_tensor("pqm", [128, 512], F32))
    pk = es.enter_context(nc.psum_tensor("pkm", [128, 512], F32))
    pm = pq
    prb = pk
    es_att = contextlib.ExitStack()
    tp_att = es_att.enter_context(nc.psum_tensor("tp_att", [128, 1024], BF16))
    tp_b = es_att.enter_context(nc.psum_tensor("tp_b", [128, 1024], BF16))
    pVAN = es_att.enter_context(nc.psum_tensor("pVAN", [128, 512], F32))
    pS = es_att.enter_context(nc.psum_tensor("pS", [64, 2 * 65], F32))
    wo_ps = es_att.enter_context(nc.psum_tensor("wo_ps", [128, 1024], F32))

    S = Sched(nc, es)
    psd = {}

    with es, nc.Block() as block:
        @block.sync
        def _(sync):
            t_c = None
            for dst, src in ((idn, idn_in), (idnf, idnf_in), (maskT, maskT_in),
                             (triS, triS_in), (iotaE, iotaE_in), (onesf, onesf_in),
                             (onesb, onesb_in), (tglob, tglob_in), (cmask, cmask_in),
                             (oidx, oidx_in), (gbias, gb_in)):
                t_c = S.dma("sp", "c", lambda s, dst=dst, src=src: s.dma_start(out=dst[:], in_=src))
            t_consts = t_c

            t_awb = None
            for dst, src in ((wq_b, wqT_in), (wk_b, wkT_in), (wv_b, wvT_in)):
                taw = S.dma("act", "aw", lambda s, src=src: s.dma_start(
                    out=w32a[:].rearrange("p (c j) -> p c j", j=DK),
                    in_=src.rearrange("(c p) j -> p c j", p=128)),
                    after=[] if t_awb is None else [t_awb])
                t_awb = S.op("dve", lambda vv, dst=dst: vv.tensor_copy(dst[:], w32a[:]), after=[taw])
            for half in range(2):
                taw = S.dma("act", "aw", lambda s, half=half: s.dma_start(
                    out=w32a[0:64, 0:512], in_=woT_in[:, half * 512:(half + 1) * 512]),
                    after=[t_awb])
                t_awb = S.op("dve", lambda vv, half=half: vv.tensor_copy(
                    wo_b[:, half * 512:(half + 1) * 512], w32a[0:64, 0:512]), after=[taw])
            t_vones = S.op("dve", lambda vv: vv.memset(V_all[:], 1.0))
            t_szero = S.op("dve", lambda vv: vv.memset(S_sb[:], 0.0))
            t_szero2 = S.op("dve", lambda vv: vv.memset(S32_sb[:], 0.0))
            t_zz = S.op("dve", lambda vv: vv.memset(zz_sb[:], 0.0))
            t_eps = S.op("dve", lambda vv: vv.memset(sm[:, 18:19], 1e-5))
            t_rb0 = S.op("dve", lambda vv: vv.memset(rb_sb[0:1, 0:E], 0.0))
            t_xez = S.op("dve", lambda vv: vv.memset(xe_t[:], 0.0))

            S.dma("act", "gw", lambda s: s.dma_start(
                out=gw_b[:].rearrange("p (c e) -> p c e", e=E),
                in_=gwT_in.rearrange("(c p) e -> p c e", p=128)))
            t_gw = S.token("gw")
            for dst, src, nch in (((w1e, sw1_in, 8), (w3e, sw3_in, 8), (w2e, sw2_in, 4)) if KW else ()):
                off = 8 * II if nch == 8 else 4 * D
                S.dma("act", "we1", lambda s, dst=dst, src=src, nch=nch, off=off: s.dma_start(
                    out=dst[:, off:2 * off].rearrange("p (c i) -> p c i", c=nch),
                    in_=src.rearrange("(c p) i -> p c i", p=128)))
            S.stream("we1"); S.stream("we0")
            t_sw = S.token("we1")
            t_amz = S.dma("sp", "amz", lambda s: s.dma_start(
                out=amap_i.rearrange("(g p b) c -> p g (b c)", p=128, b=128),
                in_=zz_sb[:].rearrange("p (g b) -> p g b", g=2)), after=[t_zz])

            t_wexp = {}

            def load_expert_w(e, gate_tok):
                eb = e % 2
                strm = "we%d" % eb
                deps = [] if gate_tok is None else [gate_tok]
                S.dma("act", strm, lambda s, e=e, eb=eb: s.dma_start(
                    out=w1e[:, eb * 8 * II:(eb + 1) * 8 * II].rearrange("p (c i) -> p c i", c=8),
                    in_=w1_in.rearrange("(e c p) i -> e p c i", e=EPC, p=128)[e]), after=deps)
                S.dma("act", strm, lambda s, e=e, eb=eb: s.dma_start(
                    out=w3e[:, eb * 8 * II:(eb + 1) * 8 * II].rearrange("p (c i) -> p c i", c=8),
                    in_=w3_in.rearrange("(e c p) i -> e p c i", e=EPC, p=128)[e]))
                S.dma("act", strm, lambda s, e=e, eb=eb: s.dma_start(
                    out=w2e[:, eb * 4 * D:(eb + 1) * 4 * D].rearrange("p (c i) -> p c i", c=4),
                    in_=w2_in.rearrange("(e c p) i -> e p c i", e=EPC, p=128)[e]))
                t_wexp[e] = S.token(strm)

            if KW:
                load_expert_w(0, None)

            if PHASE == 0:
                es_att.close()
                S._wait("sp", [S.token(k) for k in list(S.cnt) if S.cnt[k] > 0])
                S.dma("sp", "outw", lambda s: s.dma_start(out=out_d, in_=x1d))
                S._wait("sp", [S.token("outw")])
                return

            t_x, t_x2, t_xb, t_xtr, t_xT = {}, {}, {}, {}, {}
            t_qe, t_ke, t_kf = {}, {}, {}
            t_v, t_vsb, t_A, t_Ab, t_num, t_st, t_Ssb = {}, {}, {}, {}, {}, {}, {}
            t_vtr = {}
            t_sacc = {}
            t_ao, t_aoT, t_aoTs, t_wo, t_r, t_ln, t_x1b, t_x1bT = {}, {}, {}, {}, {}, {}, {}, {}
            t_tsd = {}
            t_x1w, t_xagw = {}, {}
            t_Ssb[0] = t_szero

            def smc(i, n):
                b = 2 + (i % 2) * 8
                return sm[:, b + n:b + n + 1]

            sq_prev = [None]

            def ln_stats(i, acc_tok, acc_ap, sq_slice, out_deps):
                tm = S.op("act", lambda s: s.mul(smc(i, 1), smc(i, 0), 1.0 / D), after=[acc_tok])
                tng = S.op("act", lambda s: s.mul(smc(i, 2), smc(i, 1), -1.0), after=[tm])
                sq_dep = [tng] + list(out_deps) + ([sq_prev[0]] if sq_prev[0] is not None else [])
                tsq = S.op("act", lambda s: s.activation(
                    out=sq_slice, in_=acc_ap, func=AF.Square, bias=smc(i, 2), accum_out=smc(i, 3)), after=sq_dep)
                sq_prev[0] = tsq
                tva = S.op("act", lambda s: s.mul(smc(i, 4), smc(i, 3), 1.0 / D), after=[tsq])
                tsd = S.op("act", lambda s: s.activation(
                    out=smc(i, 5), in_=smc(i, 4), func=AF.Sqrt, bias=sm[:, 18:19]), after=[tva, t_eps])
                return tsd

            def ln_apply(i, acc_ap, out_ap, out_deps, tsd):
                trs = S.op("dve", lambda vv: vv.reciprocal(smc(i, 6), smc(i, 5)), after=[tsd])
                tnm = S.op("act", lambda s: s.activation(
                    out=smc(i, 7), in_=smc(i, 2), func=AF.Copy, scale=smc(i, 6)), after=[trs])
                tl = S.op("act", lambda s: s.activation(
                    out=out_ap, in_=acc_ap, func=AF.Identity, scale=smc(i, 6), bias=smc(i, 7)),
                    after=[tnm] + list(out_deps))
                return tl

            def ln_chain(i, acc_tok, acc_ap, out_ap, sq_slice, out_deps):
                tsd = ln_stats(i, acc_tok, acc_ap, sq_slice, out_deps)
                return ln_apply(i, acc_ap, out_ap, out_deps, tsd)

            def att_pass1(i):
                s4 = i % 3
                dep = [t_xb[i - 3]] if i >= 3 else []
                t_x[i] = S.dma("sp", "x%d" % s4, lambda s, i=i, s4=s4: s.dma_start(
                    out=xf[:, s4 * D:(s4 + 1) * D], in_=x_in[i * 128:(i + 1) * 128, :]), after=dep)
                dep = [t_x[i]]
                if i >= 2:
                    dep.append(t_xtr[i - 2])
                t_xb[i] = S.op("dve", lambda vv, i=i, s4=s4: vv.tensor_copy(
                    xb[:, (i % 2) * D:(i % 2 + 1) * D], xf[:, s4 * D:(s4 + 1) * D]), after=dep)
                dep = [t_xb[i], t_consts]
                if i >= 1:
                    dep.append(t_xT[i - 1])
                tr = None
                for dc in range(8):
                    tr = S.op("pe", lambda p, dc=dc, i=i: p.transpose(
                        tp_att[:, dc * 128:(dc + 1) * 128],
                        xb[:, (i % 2) * D + dc * 128:(i % 2) * D + (dc + 1) * 128], idn[:]),
                        after=dep if dc == 0 else ())
                t_xtr[i] = tr
                t_xT[i] = S.op("dve", lambda vv, i=i: vv.tensor_copy(
                    xT_all[:, :].rearrange("p (dc t) -> p dc t", t=T)[:, :, i * 128:(i + 1) * 128],
                    tp_att[:, :].rearrange("p (dc t) -> p dc t", t=128)), after=[tr])

            t_qkve = {}

            def qkv_wide():
                last_min = [None]
                for g4 in range(4):
                    depx = [t_xT[4 * g4 + 3], t_awb]
                    for which, wb, bank, dst in (("q", wq_b, pq, Q_all), ("k", wk_b, pk, K_all)):
                        dep = list(depx)
                        if (which, g4 - 1) in t_qkve:
                            dep.append(t_qkve[(which, g4 - 1)])
                        mm = None
                        for dc in range(8):
                            mm = S.op("pe", lambda p, dc=dc, g4=g4, wb=wb, bank=bank: p.matmul(
                                bank[0:64, :], wb[:, dc * DK:(dc + 1) * DK],
                                xT_all[:, dc * T + g4 * 512:dc * T + (g4 + 1) * 512],
                                start=(dc == 0), stop=(dc == 7)), after=dep if dc == 0 else ())
                        tev = S.op("dve", lambda vv, bank=bank: vv.tensor_copy(qraw[:], bank[0:64, :]),
                                   after=[mm] + ([last_min[0]] if last_min[0] is not None else []))
                        texp = S.op("act", lambda s: s.activation(
                            out=qe32[:], in_=qraw[:], func=AF.Exp), after=[tev])
                        tadd = S.op("dve", lambda vv: vv.tensor_scalar(
                            qraw[:], qraw[:], 1.0, None, ALU.add), after=[texp])
                        t_qkve[(which, g4)] = S.op("dve", lambda vv, g4=g4, dst=dst: vv.tensor_tensor(
                            dst[:, g4 * 512:(g4 + 1) * 512], qe32[:], qraw[:], ALU.min), after=[tadd])
                        last_min[0] = t_qkve[(which, g4)]
                    dep = list(depx)
                    if ("v", g4 - 1) in t_qkve:
                        dep.append(t_qkve[("v", g4 - 1)])
                    mm = None
                    for dc in range(8):
                        mm = S.op("pe", lambda p, dc=dc, g4=g4: p.matmul(
                            pVAN[0:64, 0:512], wv_b[:, dc * DK:(dc + 1) * DK],
                            xT_all[:, dc * T + g4 * 512:dc * T + (g4 + 1) * 512],
                            start=(dc == 0), stop=(dc == 7)), after=dep if dc == 0 else ())
                    t_qkve[("v", g4)] = S.op("dve", lambda vv, g4=g4: vv.tensor_copy(
                        Vt_all[:, g4 * 512:(g4 + 1) * 512], pVAN[0:64, 0:512]), after=[mm])

            def kfv_trans():
                prev = None
                for i in range(NT):
                    g4 = i // 4
                    dep = [t_qkve[("k", g4)], t_consts]
                    if prev is not None:
                        dep.append(prev)
                    ttr = S.op("pe", lambda p, i=i: p.transpose(
                        tp_att[:, 0:64], K_all[:, i * 128:(i + 1) * 128], idn[0:64, 0:64]), after=dep)
                    t_kf[i] = S.op("dve", lambda vv, i=i: vv.tensor_copy(
                        Kf_all[:, i * DK:(i + 1) * DK], tp_att[:, 0:64]), after=[ttr])
                    ttr2 = S.op("pe", lambda p, i=i: p.transpose(
                        tp_att[:, 64:128], Vt_all[:, i * 128:(i + 1) * 128], idn[0:64, 0:64]),
                        after=[t_qkve[("v", g4)], t_kf[i]])
                    t_vsb[i] = S.op("dve", lambda vv, i=i: vv.tensor_copy(
                        V_all[:, i * 65:i * 65 + 64], tp_att[:, 64:128]), after=[ttr2, t_vones])
                    prev = t_vsb[i]

            def lp3(s4):
                return 15 - ((15 - s4) % 3)

            def att_pass2(i):
                s4, s2 = i % 3, i % 2
                dep = [t_xb[lp3(s4)]]
                if i >= 3:
                    dep.append(t_r[i - 3])
                t_x2[i] = S.dma("sp", "x%d" % s4, lambda s, i=i, s4=s4: s.dma_start(
                    out=xf[:, s4 * D:(s4 + 1) * D], in_=x_in[i * 128:(i + 1) * 128, :]), after=dep)
                dep = [t_vsb[i]]
                if i >= 1:
                    dep.append(t_Ab[i - 1])
                t_A[i] = S.op("pe", lambda p, i=i: p.matmul(
                    pVAN[:, 0:128], K_all[:, i * 128:(i + 1) * 128], Q_all[:, i * 128:(i + 1) * 128],
                    start=True, stop=True), after=dep)
                dep = [t_A[i], t_consts]
                if i >= 2:
                    dep.append(t_num[i - 2])
                t_Ab[i] = S.op("dve", lambda vv, s2=s2: vv.tensor_tensor(
                    At_b[:, s2 * 128:(s2 + 1) * 128], pVAN[:, 0:128], maskT[:], ALU.mult), after=dep)
                no2 = 128 if s2 == 0 else 224
                dep = [t_Ab[i], t_Ssb[i]]
                if i >= 2:
                    dep.append(t_ao[i - 2])
                if i == 0:
                    t_num[i] = S.op("pe", lambda p, i=i, s2=s2, no2=no2: p.matmul(
                        pVAN[:, no2:no2 + 65], At_b[:, s2 * 128:(s2 + 1) * 128],
                        V_all[:, i * 65:(i + 1) * 65], start=True, stop=True), after=dep)
                else:
                    mmn = S.op("pe", lambda p, i=i, s2=s2, no2=no2: p.matmul(
                        pVAN[:, no2:no2 + 65], At_b[:, s2 * 128:(s2 + 1) * 128],
                        V_all[:, i * 65:(i + 1) * 65], start=True, stop=False), after=dep)
                    t_num[i] = S.op("pe", lambda p, i=i, no2=no2: p.matmul(
                        pVAN[:, no2:no2 + 65], Q_all[:, i * 128:(i + 1) * 128], S_sb[:],
                        start=False, stop=True))
                dep = [t_kf[i], t_vsb[i]]
                if i >= 2:
                    dep.append(t_sacc[i - 2])
                t_st[i] = S.op("pe", lambda p, i=i, s2=s2: p.matmul(
                    pS[:, s2 * 65:(s2 + 1) * 65], Kf_all[:, i * DK:(i + 1) * DK],
                    V_all[:, i * 65:(i + 1) * 65], start=True, stop=True), after=dep)
                t_sacc[i] = S.op("dve", lambda vv, s2=s2: vv.tensor_tensor(
                    S32_sb[:], S32_sb[:], pS[:, s2 * 65:(s2 + 1) * 65], ALU.add),
                    after=[t_st[i], t_szero2])
                if i + 1 < NT:
                    t_Ssb[i + 1] = S.op("dve", lambda vv: vv.tensor_copy(S_sb[:], S32_sb[:]),
                                        after=[t_sacc[i], t_num[i]])
                td1 = S.op("dve", lambda vv, no2=no2: vv.tensor_scalar(
                    sm[:, 0:1], pVAN[:, no2 + 64:no2 + 65], 1e-6, None, ALU.add), after=[t_num[i]])
                td2 = S.op("dve", lambda vv: vv.reciprocal(sm[:, 1:2], sm[:, 0:1]), after=[td1])
                dep = [td2]
                if i >= 2:
                    dep.append(t_aoT[i - 2])
                t_ao[i] = S.op("dve", lambda vv, s2=s2, no2=no2: vv.tensor_scalar(
                    ao_sb[:, s2 * 64:(s2 + 1) * 64], pVAN[:, no2:no2 + 64], sm[:, 1:2], None, ALU.mult), after=dep)
            def att_pass2a2(i):
                s4, s2 = i % 3, i % 2
                dep = [t_ao[i]]
                if i >= 1:
                    dep.append(t_aoTs[i - 1])
                else:
                    dep.append(t_vsb[NT - 1])
                t_aoT[i] = S.op("pe", lambda p, s2=s2: p.transpose(
                    tp_att[0:64, 0:128], ao_sb[:, s2 * 64:(s2 + 1) * 64], idn[:]), after=dep)
                dep = [t_aoT[i]]
                if i >= 2:
                    dep.append(t_wo[i - 2])
                t_aoTs[i] = S.op("dve", lambda vv, s2=s2: vv.tensor_copy(
                    aoT_sb[:, s2 * 128:(s2 + 1) * 128], tp_att[0:64, 0:128]), after=dep)
                dep = [t_aoTs[i], t_awb]
                if i >= 1:
                    dep.append(t_r[i - 1])
                S.op("pe", lambda p, s2=s2: p.matmul(
                    wo_ps[:, 0:512], aoT_sb[:, s2 * 128:(s2 + 1) * 128], wo_b[:, 0:512],
                    start=True, stop=True), after=dep)
                t_wo[i] = S.op("pe", lambda p, s2=s2: p.matmul(
                    wo_ps[:, 512:1024], aoT_sb[:, s2 * 128:(s2 + 1) * 128], wo_b[:, 512:1024],
                    start=True, stop=True))
                dep = [t_wo[i], t_x2[i]]
                if i >= 2:
                    dep.append(t_ln[i - 2])
                t_r[i] = S.op("dve", lambda vv, s4=s4, s2=s2, i=i: vv.scalar_tensor_tensor(
                    r_sb[:, s2 * D:(s2 + 1) * D], xf[:, s4 * D:(s4 + 1) * D], 1.0, wo_ps[:],
                    ALU.mult, ALU.add, accum_out=smc(i, 0)), after=dep)
                out_deps = []
                if i >= 2:
                    out_deps += [t_x1w[i - 2], t_x1b[i - 2]]
                t_tsd[i] = ln_stats(i, t_r[i], r_sb[:, s2 * D:(s2 + 1) * D],
                                    x1f_sb[:, s2 * D:(s2 + 1) * D], out_deps)

            def att_pass2b(i):
                s2 = i % 2
                out_deps = []
                if i >= 2:
                    out_deps += [t_x1w[i - 2], t_x1b[i - 2]]
                t_ln[i] = ln_apply(i, r_sb[:, s2 * D:(s2 + 1) * D],
                                   x1f_sb[:, s2 * D:(s2 + 1) * D], out_deps, t_tsd[i])
                dep = [t_ln[i]]
                if i >= 2:
                    dep += [t_xagw[i - 2], t_x1bT[i - 2]]
                t_x1b[i] = S.op("dve", lambda vv, s2=s2: vv.tensor_copy(
                    x1b_sb[:, s2 * D:(s2 + 1) * D], x1f_sb[:, s2 * D:(s2 + 1) * D]), after=dep)
                t_x1w[i] = S.dma("sp", "x1w%d" % s2, lambda s, i=i, s2=s2: s.dma_start(
                    out=x1d[i * 128:(i + 1) * 128, :], in_=x1f_sb[:, s2 * D:(s2 + 1) * D]), after=[t_ln[i]])
                t_xagw[i] = S.dma("sp", "xag%s%d" % ("AB"[i // 8], s2), lambda s, i=i, s2=s2: s.dma_start(
                    out=xag_i[i * 128:(i + 1) * 128, :], in_=x1b_sb[:, s2 * D:(s2 + 1) * D]), after=[t_x1b[i]])
                dep = [t_x1b[i]]
                if i >= 1:
                    dep.append(t_x1bT[i - 1])
                tr = None
                for dc in range(8):
                    tr = S.op("pe", lambda p, dc=dc, s2=s2: p.transpose(
                        tp_b[:, dc * 128:(dc + 1) * 128],
                        x1b_sb[:, s2 * D + dc * 128:s2 * D + (dc + 1) * 128], idn[:]),
                        after=dep if dc == 0 else ())
                t_x1bT[i] = S.op("dve", lambda vv, i=i: vv.tensor_copy(
                    x1bT[:, i * D:(i + 1) * D], tp_b[:]), after=[tr])

            for i in range(NT):
                att_pass1(i)
            qkv_wide()
            kfv_trans()

            t_bb0 = S.op("pe", lambda p: p.matmul(pm[:, 64:96], onesf[:], gbias[:], start=True, stop=True),
                         after=[t_consts, t_qkve[("q", 3)], t_qkve[("k", 3)]])
            t_bb = S.op("dve", lambda vv: vv.tensor_copy(bias_b[:], pm[:, 64:96]), after=[t_bb0])

            t_lg, t_lge = {}, {}
            t_cse_guard = [None]

            def routing(j):
                dep = [t_x1bT[j], t_gw, t_bb]
                if j >= 1:
                    dep.append(t_lge[j - 1])
                mm = None
                for dc in range(8):
                    mm = S.op("pe", lambda p, dc=dc, j=j: p.matmul(
                        pm[:, 0:32], x1bT[:, j * D + dc * 128:j * D + (dc + 1) * 128],
                        gw_b[:, dc * E:(dc + 1) * E],
                        start=(dc == 0), stop=(dc == 7)), after=dep if dc == 0 else ())
                t_lg[j] = mm
                t_lge[j] = S.op("dve", lambda vv, j=j: vv.tensor_copy(
                    lg_all[:, j * E:(j + 1) * E], pm[:, 0:32]), after=[mm])

            def v3(ap):
                return ap.rearrange("p (j e) -> p j e", e=E)

            def routing_batch():
                t_s = S.op("act", lambda s: s.activation(out=sc_all[:], in_=lg_all[:], func=AF.Sigmoid),
                           after=[t_lge[NT - 1]])
                last = S.op("dve", lambda vv: vv.tensor_tensor(
                    v3(u_all[:]), v3(sc_all[:]),
                    bias_b[:].rearrange("p (c e) -> p c e", c=1).broadcast_to([128, NT, E]),
                    ALU.add), after=[t_s, t_bb])
                for k in range(4):
                    mxc = tk_sb[:, k * NT:(k + 1) * NT]
                    last = S.op("dve", lambda vv, mxc=mxc: vv.reduce_max(
                        mxc, v3(u_all[:]), axis=AX.X), after=[last])
                    hk = hh_sb[:, k * NT * E:(k + 1) * NT * E]
                    last = S.op("dve", lambda vv, hk=hk, mxc=mxc: vv.tensor_tensor(
                        v3(hk), v3(u_all[:]),
                        mxc.rearrange("p (j e) -> p j e", e=1).broadcast_to([128, NT, E]),
                        ALU.is_equal), after=[last])
                    last = S.op("dve", lambda vv, hk=hk: vv.tensor_tensor(
                        tmpW[:], hk, sc_all[:], ALU.mult), after=[last])
                    last = S.op("dve", lambda vv, k=k: vv.reduce_sum(
                        wg_sb[:, k * NT:(k + 1) * NT], v3(tmpW[:]), axis=AX.X), after=[last])
                    last = S.op("dve", lambda vv, hk=hk: vv.tensor_tensor(
                        v3(tmpW[:]), v3(hk),
                        iotaE[:].rearrange("p (c e) -> p c e", c=1).broadcast_to([128, NT, E]),
                        ALU.mult), after=[last])
                    last = S.op("dve", lambda vv, k=k: vv.reduce_sum(
                        ek_sb[:, k * NT:(k + 1) * NT], v3(tmpW[:]), axis=AX.X), after=[last])
                    if k < 3:
                        last = S.op("dve", lambda vv, hk=hk: vv.scalar_tensor_tensor(
                            u_all[:], hk, -1e30, u_all[:], ALU.mult, ALU.add), after=[last])
                lastI = S.op("dve", lambda vv: vv.tensor_tensor(
                    tmpW[:], hh_sb[:, 0:NT * E], hh_sb[:, NT * E:2 * NT * E], ALU.add), after=[last])
                lastI = S.op("dve", lambda vv: vv.tensor_tensor(
                    tmpW[:], tmpW[:], hh_sb[:, 2 * NT * E:3 * NT * E], ALU.add), after=[lastI])
                t_I = S.op("dve", lambda vv: vv.tensor_tensor(
                    I_sb[:], tmpW[:], hh_sb[:, 3 * NT * E:4 * NT * E], ALU.add), after=[lastI])
                t_csm = S.op("pe", lambda p: p.matmul(
                    prb[0:1, :], onesb[:], I_sb[:], start=True, stop=True), after=[t_I, t_consts])
                t_cse = S.op("dve", lambda vv: vv.tensor_copy(cs_all[:], prb[0:1, :]), after=[t_csm])
                t_cse_guard[0] = t_cse
                last3 = t_rb0
                for j in range(1, NT):
                    last3 = S.op("dve", lambda vv, j=j: vv.tensor_tensor(
                        rb_sb[0:1, j * E:(j + 1) * E], rb_sb[0:1, (j - 1) * E:j * E],
                        cs_all[0:1, (j - 1) * E:j * E], ALU.add), after=[last3, t_cse])
                t_cnt = S.op("dve", lambda vv: vv.tensor_tensor(
                    counts_sb[:], rb_sb[0:1, (NT - 1) * E:NT * E],
                    cs_all[0:1, (NT - 1) * E:NT * E], ALU.add), after=[last3])
                last = S.op("dve", lambda vv: vv.tensor_tensor(
                    tk_sb[:, 4 * NT:5 * NT], wg_sb[:, 0:NT], wg_sb[:, NT:2 * NT], ALU.add), after=[last, t_cnt])
                last = S.op("dve", lambda vv: vv.tensor_tensor(
                    tk_sb[:, 4 * NT:5 * NT], tk_sb[:, 4 * NT:5 * NT], wg_sb[:, 2 * NT:3 * NT], ALU.add), after=[last])
                last = S.op("dve", lambda vv: vv.tensor_tensor(
                    tk_sb[:, 4 * NT:5 * NT], tk_sb[:, 4 * NT:5 * NT], wg_sb[:, 3 * NT:4 * NT], ALU.add), after=[last])
                last = S.op("dve", lambda vv: vv.tensor_scalar(
                    tk_sb[:, 4 * NT:5 * NT], tk_sb[:, 4 * NT:5 * NT], 1e-12, None, ALU.max), after=[last])
                last = S.op("dve", lambda vv: vv.reciprocal(
                    tk_sb[:, 5 * NT:6 * NT], tk_sb[:, 4 * NT:5 * NT]), after=[last])
                last = S.op("dve", lambda vv: vv.tensor_tensor(
                    g_sb[:].rearrange("p (k j) -> p k j", k=4),
                    wg_sb[:].rearrange("p (k j) -> p k j", k=4),
                    tk_sb[:, 5 * NT:6 * NT].rearrange("p (c j) -> p c j", c=1).broadcast_to([128, 4, NT]),
                    ALU.mult), after=[last])
                t_pc = {}
                prev_mm = None
                for c in range(4):
                    dep = [t_I, t_consts]
                    if prev_mm is not None:
                        dep.append(t_pc[c - 1])
                    mmp = S.op("pe", lambda p, c=c: p.matmul(
                        pm[:, 0:128], triS[:], I_sb[:, c * 4 * E:(c + 1) * 4 * E], start=True, stop=True),
                        after=dep)
                    prev_mm = mmp
                    t_pc[c] = S.op("dve", lambda vv, c=c: vv.tensor_copy(
                        P_all[:, c * 4 * E:(c + 1) * 4 * E], pm[:, 0:128]), after=[mmp])
                t_posk = {}
                last2 = t_pc[3]
                for k in range(4):
                    hk = hh_sb[:, k * NT * E:(k + 1) * NT * E]
                    last2 = S.op("dve", lambda vv, hk=hk: vv.tensor_tensor(
                        tmpW[:], hk, P_all[:], ALU.mult), after=[last2])
                    last2 = S.op("dve", lambda vv, k=k: vv.reduce_sum(
                        posk_sb[:, k * NT:(k + 1) * NT], v3(tmpW[:]), axis=AX.X), after=[last2])
                    t_posk[k] = last2
                return t_posk, t_cnt, last3


            for i in range(NT + 3):
                if i < NT:
                    att_pass2(i)
                if i >= 3:
                    att_pass2b(i - 3)
                    routing(i - 3)
                if 1 <= i <= NT:
                    att_pass2a2(i - 1)
            if PHASE == 11:
                es_att.close()
                S._wait("sp", [S.token(k) for k in list(S.cnt) if S.cnt[k] > 0])
                S.dma("sp", "outw", lambda s: s.dma_start(out=out_d, in_=x1d))
                S._wait("sp", [S.token("outw")])
                return

            t_attB = [S.token("pe"), S.token("dve"), S.token("act")]

            def finish_partial(gates):
                es_att.close()
                es_x1.close()
                S._wait("sp", [S.token(k) for k in list(S.cnt) if S.cnt[k] > 0])
                S.dma("sp", "outw", lambda s: s.dma_start(out=out_d, in_=x1d))
                S._wait("sp", [S.token("outw")])

            if PHASE == 1:
                finish_partial([])
                return

            es_att.close()
            psd["tp2"] = es.enter_context(nc.psum_tensor("tp2B", [128, 512], BF16))
            psd["h1"] = es.enter_context(nc.psum_tensor("h1B", [128, 1024], F32))
            psd["h3"] = es.enter_context(nc.psum_tensor("h3B", [128, 512], F32))
            psd["y"] = es.enter_context(nc.psum_tensor("yB", [128, 1024], F32))
            def make_ffn_pipe(nt_total, weights, src_tok, dst_write):
                st = {"mm": {}, "sil": {}, "mul": {}, "tr": {}, "hT": {}, "w2": {}, "y": {}, "o": {}}

                def w13(g):
                    if not (0 <= g < nt_total):
                        return
                    tb = g % 2
                    w1ap, w3ap, w2ap, wtoks = weights(g)
                    stok, xTap = src_tok(g)
                    dep = [stok] + list(wtoks)
                    if g >= 2:
                        dep.append(st["sil"][g - 2])
                    mm = None
                    for dc in range(8):
                        mm = S.op("pe", lambda p, dc=dc, tb=tb, w1ap=w1ap, xTap=xTap: p.matmul(
                            psd["h1"][:, tb * II:(tb + 1) * II], xTap(dc), w1ap(dc),
                            start=(dc == 0), stop=(dc == 7)), after=dep if dc == 0 else ())
                    dep2 = [st["mul"][g - 1]] if g >= 1 else []
                    for dc in range(8):
                        mm = S.op("pe", lambda p, dc=dc, w3ap=w3ap, xTap=xTap: p.matmul(
                            psd["h3"][:], xTap(dc), w3ap(dc),
                            start=(dc == 0), stop=(dc == 7)), after=dep2 if dc == 0 else ())
                    st["mm"][g] = mm

                def post13(g):
                    if not (0 <= g < nt_total):
                        return
                    tb = g % 2
                    st["sil"][g] = S.op("act", lambda s, tb=tb: s.activation(
                        out=s1_sb[:, tb * II:(tb + 1) * II], in_=psd["h1"][:, tb * II:(tb + 1) * II],
                        func=AF.Silu), after=[st["mm"][g]])
                    dep = [st["sil"][g]]
                    if g >= 2:
                        dep.append(st["tr"][g - 2])
                    st["mul"][g] = S.op("dve", lambda vv, tb=tb: vv.tensor_tensor(
                        h_pipe[:, tb * II:(tb + 1) * II], s1_sb[:, tb * II:(tb + 1) * II],
                        psd["h3"][:], ALU.mult), after=dep)

                def trh(g):
                    if not (0 <= g < nt_total):
                        return
                    tb = g % 2
                    dep = [st["mul"][g], t_consts]
                    if g >= 1:
                        dep.append(st["hT"][g - 1])
                    tr2 = None
                    for ic in range(4):
                        tr2 = S.op("pe", lambda p, ic=ic, tb=tb: p.transpose(
                            psd["tp2"][:, ic * 128:(ic + 1) * 128],
                            h_pipe[:, tb * II + ic * 128:tb * II + (ic + 1) * 128], idn[:]),
                            after=dep if ic == 0 else ())
                    st["tr"][g] = tr2
                    dep = [tr2]
                    if g >= 2:
                        dep.append(st["w2"][g - 2])
                    st["hT"][g] = S.op("dve", lambda vv, tb=tb: vv.tensor_copy(
                        hT_sb[:, tb * II:(tb + 1) * II], psd["tp2"][:]), after=dep)

                def w2f(g):
                    if not (0 <= g < nt_total):
                        return
                    tb = g % 2
                    _, _, w2ap, _ = weights(g)
                    dep = [st["hT"][g]]
                    if g >= 1:
                        dep.append(st["y"][g - 1])
                    for ic in range(4):
                        for half in range(2):
                            mm2 = S.op("pe", lambda p, ic=ic, half=half, tb=tb, w2ap=w2ap: p.matmul(
                                psd["y"][:, half * 512:(half + 1) * 512],
                                hT_sb[:, tb * II + ic * 128:tb * II + (ic + 1) * 128],
                                w2ap(ic, half),
                                start=(ic == 0), stop=(ic == 3)), after=dep if (ic == 0 and half == 0) else ())
                    st["w2"][g] = mm2

                def postw2(g):
                    if not (0 <= g < nt_total):
                        return
                    tb = g % 2
                    dep = [st["w2"][g]]
                    if g >= 2:
                        dep.append(st["o"][g - 2])
                    st["y"][g] = S.op("act", lambda s, tb=tb: s.activation(
                        out=ye_sb[:, tb * D:(tb + 1) * D], in_=psd["y"][:], func=AF.Copy), after=dep)
                    st["o"][g] = dst_write(g, tb, st["y"][g])

                return st, (w13, post13, trh, w2f, postw2)

            def sh_weights(g):
                return (lambda dc: w1e[:, 8 * II + dc * II:8 * II + (dc + 1) * II],
                        lambda dc: w3e[:, 8 * II + dc * II:8 * II + (dc + 1) * II],
                        lambda ic, half: w2e[:, 4 * D + ic * D + half * 512:4 * D + ic * D + (half + 1) * 512],
                        [t_sw] + t_attB)

            def sh_src(g):
                return (t_x1bT[g], lambda dc, g=g: x1bT[:, g * D + dc * 128:g * D + (dc + 1) * 128])

            def sh_dst(g, tb, ytok):
                return S.dma("sp", "ysh%d" % tb, lambda s, g=g, tb=tb: s.dma_start(
                    out=yshd[g * 128:(g + 1) * 128, :], in_=ye_sb[:, tb * D:(tb + 1) * D]), after=[ytok])

            stS, (sh_w13, sh_post13, sh_trh, sh_w2, sh_postw2) = make_ffn_pipe(
                NT, sh_weights, sh_src, sh_dst)

            t_posk, t_cnt, t_rbend = routing_batch()
            t_ag2a = S.collective(lambda g: g.collective_compute(
                "AllGather", ALU.bypass, replica_groups=[list(range(NCORES))],
                ins=[xag_i[0:T // 2, :]],
                outs=[xall[0:NCORES * T // 2, :]]),
                after=[S.token("xagA0"), S.token("xagA1")])
            t_cntw = S.dma("sp", "cntw", lambda s: s.dma_start(out=cnt_i, in_=counts_sb[:]), after=[t_cnt])
            t_ag1 = S.collective(lambda g: g.collective_compute(
                "AllGather", ALU.bypass, replica_groups=[list(range(NCORES))],
                ins=[cnt_i], outs=[cnt_o]), after=[t_cntw])
            t_ag2 = S.collective(lambda g: g.collective_compute(
                "AllGather", ALU.bypass, replica_groups=[list(range(NCORES))],
                ins=[xag_i[T // 2:T, :]],
                outs=[xall[NCORES * T // 2:NCORES * T, :]]),
                after=[S.token("xagB0"), S.token("xagB1")])
            t_cl = S.dma("sp", "cl", lambda s: s.dma_start(out=call_sb[:], in_=cnt_o), after=[t_ag1])
            t_bm = S.op("pe", lambda p: p.matmul(prb[0:1, 0:E], cmask[:], call_sb[:], start=True, stop=True),
                        after=[t_cl, t_consts, t_cnt, t_cse_guard[0]])
            t_base = S.op("dve", lambda vv: vv.tensor_copy(base_sb[:], prb[0:1, 0:E]), after=[t_bm])

            brs = {}

            def breg(name, val):
                if name not in brs:
                    r = nc.gpsimd.alloc_register(name)
                    nc.gpsimd.reg_mov(r, val)
                    brs[name] = r
                return brs[name]

            last = S.op("dve", lambda vv: vv.tensor_tensor(
                rt_all[0:1, :].rearrange("p (j e) -> p j e", e=E),
                rb_sb[0:1, :].rearrange("p (j e) -> p j e", e=E),
                base_sb[:].rearrange("p (c e) -> p c e", c=1).broadcast_to([1, NT, E]),
                ALU.add), after=[t_base, t_rbend])
            t_rbt = {}
            prevb = None
            for c in range(4):
                dep = [last, t_consts, t_lge[NT - 1]]
                if prevb is not None:
                    dep.append(t_rbt[c - 1])
                bmm = S.op("pe", lambda p, c=c: p.matmul(
                    pm[:, 0:128], onesf[:], rt_all[0:1, c * 128:(c + 1) * 128], start=True, stop=True),
                    after=dep)
                prevb = bmm
                t_rbt[c] = S.op("dve", lambda vv, c=c: vv.tensor_copy(
                    rbt_all[:, c * 128:(c + 1) * 128], pm[:, 0:128]), after=[bmm])
            last = t_rbt[3]
            t_sidx = {}
            for k in range(4):
                hk = hh_sb[:, k * NT * E:(k + 1) * NT * E]
                last = S.op("dve", lambda vv, hk=hk: vv.tensor_tensor(
                    tmpW[:], hk, rbt_all[:], ALU.mult), after=[last, t_posk[k]])
                last = S.op("dve", lambda vv: vv.reduce_sum(
                    tk_sb[:, 0:NT], v3(tmpW[:]), axis=AX.X), after=[last])
                last = S.op("dve", lambda vv, k=k: vv.tensor_tensor(
                    tk_sb[:, NT:2 * NT], tk_sb[:, 0:NT], posk_sb[:, k * NT:(k + 1) * NT], ALU.add), after=[last])
                last = S.op("dve", lambda vv: vv.tensor_scalar(
                    tk_sb[:, 2 * NT:3 * NT], tk_sb[:, NT:2 * NT], float(CAP), None, ALU.is_lt), after=[last])
                last = S.op("dve", lambda vv, k=k: vv.tensor_tensor(
                    gf_sb[:, k * NT:(k + 1) * NT], g_sb[:, k * NT:(k + 1) * NT],
                    tk_sb[:, 2 * NT:3 * NT], ALU.mult), after=[last])
                last = S.op("dve", lambda vv, k=k: vv.scalar_tensor_tensor(
                    tk_sb[:, 3 * NT:4 * NT], ek_sb[:, k * NT:(k + 1) * NT], float(CAP),
                    tk_sb[:, NT:2 * NT], ALU.mult, ALU.add), after=[last])
                last = S.op("dve", lambda vv: vv.scalar_tensor_tensor(
                    tk_sb[:, 4 * NT:5 * NT], tk_sb[:, 2 * NT:3 * NT], -OOB,
                    tk_sb[:, 3 * NT:4 * NT], ALU.mult, ALU.add), after=[last])
                last = S.op("dve", lambda vv, k=k: vv.tensor_scalar(
                    sidx_i[:, k * NT:(k + 1) * NT], tk_sb[:, 4 * NT:5 * NT], OOB, None, ALU.add), after=[last])
                t_sidx[k] = last

            for j in range(NT):
                for k in range(4):
                    col = k * NT + j
                    S.dma("pool", "scat", lambda g, j=j, col=col, k=k: g.indirect_dma_start(
                        out=amap_i,
                        out_offset=IndirectOffsetOnAxis(sidx_i[:, col:col + 1], axis=0),
                        in_=tglob[:, j:j + 1],
                        in_offset=None,
                        bounds_check=breg("b_amap", E * CAP - 1),
                        oob_is_err=False), after=[t_sidx[k], t_amz, t_consts])
            t_ar = S.collective(lambda g: g.collective_compute(
                "AllReduce", ALU.add, replica_groups=[list(range(NCORES))],
                ins=[amap_i.rearrange("(a b) c -> a (b c)", a=E * CAP // 128)],
                outs=[amap_o.rearrange("(a b) c -> a (b c)", a=E * CAP // 128)]),
                after=[S.token("scat")])

            for i in range(NT + 2):
                sh_trh(i - 1)
                sh_w13(i)
                sh_w2(i - 1)
                sh_post13(i)
                sh_postw2(i - 1)

            t_am = S.dma("pool", "amg", lambda g: g.indirect_dma_start(
                out=am_sb[:],
                out_offset=None,
                in_=amap_o.rearrange("(r q) c -> r (q c)", q=128),
                in_offset=IndirectOffsetOnAxis(oidx[:], axis=0),
                bounds_check=breg("b_amr", E * CAP // 128 - 1),
                oob_is_err=False), after=[t_ar, t_consts])
            t_amt = S.op("pe", lambda p: p.transpose(pm[:, 0:32], am_sb[:], idnf[0:32, 0:32]),
                         after=[t_am, t_rbt[3], t_consts])
            tg = S.op("dve", lambda vv: vv.tensor_copy(gi32[:], pm[:, 0:32]), after=[t_amt])
            tg = S.op("dve", lambda vv: vv.tensor_scalar(tmpW[:, 0:E], gi32[:], 0.5, None, ALU.is_gt), after=[tg])
            tg = S.op("dve", lambda vv: vv.scalar_tensor_tensor(
                gi32[:], tmpW[:, 0:E], -OOB, gi32[:], ALU.mult, ALU.add), after=[tg])
            tg = S.op("dve", lambda vv: vv.tensor_scalar(gi32[:], gi32[:], OOB - 1.0, None, ALU.add), after=[tg])
            t_gidx = S.op("dve", lambda vv: vv.tensor_copy(gidx_i[:], gi32[:]), after=[tg])

            if PHASE == 3:
                finish_partial([])
                return

            if PHASE == 2:
                finish_partial([])
                return
            load_expert_w(1, stS["mm"][NT - 1])

            tpX = pk[:, :].bitcast(BF16)
            NGE = EPC * (CAP // 128)
            t_xeg, t_xetr, t_xeT = {}, {}, {}
            t_shend = [stS["o"][NT - 1], stS["o"][NT - 2]]

            def xe_weights(g):
                e = g // (CAP // 128)
                eb = e % 2
                return (lambda dc, eb=eb: w1e[:, eb * 8 * II + dc * II:eb * 8 * II + (dc + 1) * II],
                        lambda dc, eb=eb: w3e[:, eb * 8 * II + dc * II:eb * 8 * II + (dc + 1) * II],
                        lambda ic, half, eb=eb: w2e[:, eb * 4 * D + ic * D + half * 512:eb * 4 * D + ic * D + (half + 1) * 512],
                        [t_wexp[e]] + t_shend)

            def xe_gather(g):
                if not (0 <= g < NGE):
                    return
                s3 = g % 3
                dep = [t_gidx, t_ag2, t_xez]
                if g >= 3:
                    dep.append(t_xetr[g - 3])
                t_xeg[g] = S.dma("pool", "xeg%d" % s3, lambda gg, g=g, s3=s3: gg.indirect_dma_start(
                    out=xe_t[:, s3 * D:(s3 + 1) * D],
                    out_offset=None,
                    in_=xall,
                    in_offset=IndirectOffsetOnAxis(gidx_i[:, g:g + 1], axis=0),
                    bounds_check=breg("b_xall", NCORES * T - 1),
                    oob_is_err=False), after=dep)

            def xe_src(g):
                return (t_xeT[g], lambda dc, g=g: xeT_sb[:, (g % 2) * D + dc * 128:(g % 2) * D + (dc + 1) * 128])

            def xe_trans(g, st):
                if not (0 <= g < NGE):
                    return
                s3, tb = g % 3, g % 2
                dep = [t_xeg[g], t_consts] + (t_attB if g == 0 else [])
                if g >= 1:
                    dep.append(t_xeT[g - 1])
                tr = None
                for dc in range(8):
                    tr = S.op("pe", lambda p, dc=dc, s3=s3: p.transpose(
                        tpX[:, dc * 128:(dc + 1) * 128],
                        xe_t[:, s3 * D + dc * 128:s3 * D + (dc + 1) * 128], idn[:]),
                        after=dep if dc == 0 else ())
                t_xetr[g] = tr
                dep = [tr]
                if g >= 2:
                    dep.append(st["mm"][g - 2])
                t_xeT[g] = S.op("dve", lambda vv, tb=tb: vv.tensor_copy(
                    xeT_sb[:, tb * D:(tb + 1) * D], tpX[:]), after=dep)

            def xe_dst(g, tb, ytok):
                strm = "yew%s%d" % ("AB"[g // (NGE // 2)], tb)
                return S.dma("sp", strm, lambda s, g=g, strm=strm: s.dma_start(
                    out=yec[g * 128:(g + 1) * 128, :], in_=ye_sb[:, (g % 2) * D:(g % 2 + 1) * D]), after=[ytok])

            stX, (xe_w13, xe_post13, xe_trh, xe_w2, xe_postw2) = make_ffn_pipe(
                NGE, xe_weights, xe_src, xe_dst)
            t_yewk = {}

            xe_gather(0)
            xe_gather(1)
            xe_gather(2)
            xe_trans(0, stX)
            for g in range(NGE + 2):
                xe_gather(g + 3)
                xe_trans(g + 1, stX)
                xe_trh(g - 1)
                xe_w13(g)
                xe_w2(g - 1)
                xe_post13(g)
                xe_postw2(g - 1)
                ntile = CAP // 128
                if (g - 1) % ntile == ntile - 1 and 0 <= g - 1 < NGE:
                    e = (g - 1) // ntile
                    t_yewk[e] = stX["o"][g - 1]
                    if e + 2 < EPC:
                        load_expert_w(e + 2, stX["w2"][g - 1])

            if PHASE == 4:
                finish_partial([])
                return
            HCAP = EPC * CAP // 2
            t_ag3a = S.collective(lambda g: g.collective_compute(
                "AllGather", ALU.bypass, replica_groups=[list(range(NCORES))],
                ins=[yec[0:HCAP, :]], outs=[yeall[0:NCORES * HCAP, :]]),
                after=[S.token("yewA0"), S.token("yewA1")])
            t_ag3 = S.collective(lambda g: g.collective_compute(
                "AllGather", ALU.bypass, replica_groups=[list(range(NCORES))],
                ins=[yec[HCAP:2 * HCAP, :]], outs=[yeall[NCORES * HCAP:2 * NCORES * HCAP, :]]),
                after=[S.token("yewB0"), S.token("yewB1")])

            es_x1.close()
            yg_sb = es.enter_context(nc.sbuf_tensor("yg_sb", [128, 8 * D], BF16))
            acc_sb = es.enter_context(nc.sbuf_tensor("acc_sb", [128, 3 * D], F32))
            o_sb = es.enter_context(nc.sbuf_tensor("o_sb", [128, 2 * D], F32))
            accb_sb = es.enter_context(nc.sbuf_tensor("accb_sb", [128, 2 * D], F32))
            gys_sb = es.enter_context(nc.sbuf_tensor("gys_sb", [128, 4 * D], BF16))
            t_ygz = S.op("dve", lambda vv: vv.memset(yg_sb[:], 0.0), after=[t_gidx])
            t_x1r, t_yshr, t_ygv, t_acc, t_o, t_ow = {}, {}, {}, {}, {}, {}
            t_ta1, t_ta2, t_tg2, t_tg3, t_tb1, t_txy = {}, {}, {}, {}, {}, {}

            def cmb_main(j):
                s2, s3 = j % 2, j % 3
                dep = [t_x1w[NT - 1], t_x1w[NT - 2], t_gidx]
                if j >= 2:
                    dep.append(t_acc[j - 2])
                t_x1r[j] = S.dma("sp", "x1r%d" % s2, lambda s, j=j, s2=s2: s.dma_start(
                    out=accb_sb[:, s2 * D:(s2 + 1) * D], in_=x1d[j * 128:(j + 1) * 128, :]), after=dep)
                dep = [S.token("ysh0"), S.token("ysh1")]
                if j >= 2:
                    dep.append(t_txy[j - 2])
                t_yshr[j] = S.dma("sp", "yshr%d" % s2, lambda s, j=j, s2=s2: s.dma_start(
                    out=ysh_sb[:, s2 * D:(s2 + 1) * D], in_=yshd[j * 128:(j + 1) * 128, :]), after=dep)
                for k in range(4):
                    col = k * NT + j
                    slot = s2 * 4 + k
                    dep = [t_ag3, t_sidx[k], t_ygz]
                    if j >= 2:
                        dep.append((t_ta1[j - 2], t_ta2[j - 2], t_tg2[j - 2], t_tg3[j - 2])[k])
                    t_ygv[col] = S.dma("pool", "yg%d" % slot, lambda g, col=col, slot=slot: g.indirect_dma_start(
                        out=yg_sb[:, slot * D:(slot + 1) * D],
                        out_offset=None,
                        in_=yeall,
                        in_offset=IndirectOffsetOnAxis(sidx_i[:, col:col + 1], axis=0),
                        bounds_check=breg("b_ye", E * CAP - 1),
                        oob_is_err=False), after=dep)
                dep = [t_ygv[0 * NT + j]]
                if j >= 3:
                    dep.append(t_o[j - 3])
                t_ta1[j] = S.op("dve", lambda vv, j=j, s2=s2: vv.tensor_scalar(
                    acc_sb[:, s3 * D:(s3 + 1) * D], yg_sb[:, (s2 * 4) * D:(s2 * 4 + 1) * D],
                    gf_sb[:, 0 * NT + j:0 * NT + j + 1], None, ALU.mult), after=dep)
                t_ta2[j] = S.op("dve", lambda vv, j=j, s2=s2: vv.scalar_tensor_tensor(
                    acc_sb[:, s3 * D:(s3 + 1) * D], yg_sb[:, (s2 * 4 + 1) * D:(s2 * 4 + 2) * D],
                    gf_sb[:, 1 * NT + j:1 * NT + j + 1], acc_sb[:, s3 * D:(s3 + 1) * D],
                    ALU.mult, ALU.add), after=[t_ta1[j], t_ygv[1 * NT + j]])
                t_tg2[j] = S.op("act", lambda s, j=j, s2=s2: s.activation(
                    out=gys_sb[:, (s2 * 2) * D:(s2 * 2 + 1) * D],
                    in_=yg_sb[:, (s2 * 4 + 2) * D:(s2 * 4 + 3) * D], func=AF.Copy,
                    scale=gf_sb[:, 2 * NT + j:2 * NT + j + 1]),
                    after=[t_ygv[2 * NT + j]] + ([t_tb1[j - 2]] if j >= 2 else []))
                t_tg3[j] = S.op("dve", lambda vv, j=j, s2=s2: vv.tensor_scalar(
                    gys_sb[:, (s2 * 2 + 1) * D:(s2 * 2 + 2) * D],
                    yg_sb[:, (s2 * 4 + 3) * D:(s2 * 4 + 4) * D],
                    gf_sb[:, 3 * NT + j:3 * NT + j + 1], None, ALU.mult),
                    after=[t_ygv[3 * NT + j]] + ([t_tb1[j - 2]] if j >= 2 else []))
            def cmb_mainb(j):
                s2, s3 = j % 2, j % 3
                t_tb1[j] = S.op("dve", lambda vv, s2=s2: vv.tensor_tensor(
                    acc_sb[:, s3 * D:(s3 + 1) * D], acc_sb[:, s3 * D:(s3 + 1) * D],
                    gys_sb[:, (s2 * 2) * D:(s2 * 2 + 1) * D], ALU.add), after=[t_ta2[j], t_tg2[j]])
                tb2 = S.op("dve", lambda vv, s2=s2: vv.tensor_tensor(
                    acc_sb[:, s3 * D:(s3 + 1) * D], acc_sb[:, s3 * D:(s3 + 1) * D],
                    gys_sb[:, (s2 * 2 + 1) * D:(s2 * 2 + 2) * D], ALU.add), after=[t_tb1[j], t_tg3[j]])
                dep = [t_x1r[j], t_yshr[j]]
                t_txy[j] = S.op("dve", lambda vv, s2=s2: vv.tensor_tensor(
                    accb_sb[:, s2 * D:(s2 + 1) * D], accb_sb[:, s2 * D:(s2 + 1) * D],
                    ysh_sb[:, s2 * D:(s2 + 1) * D], ALU.add), after=dep)
                t_acc[j] = S.op("dve", lambda vv, j=j, s2=s2: vv.scalar_tensor_tensor(
                    acc_sb[:, s3 * D:(s3 + 1) * D], acc_sb[:, s3 * D:(s3 + 1) * D], 1.0,
                    accb_sb[:, s2 * D:(s2 + 1) * D], ALU.mult, ALU.add,
                    accum_out=smc(j, 0)), after=[tb2, t_txy[j]])

            def cmb_fin(j):
                s2, s3 = j % 2, j % 3
                out_deps = [t_ln[NT - 1]]
                if j >= 2:
                    out_deps.append(t_ow[j - 2])
                t_o[j] = ln_chain(j, t_acc[j], acc_sb[:, s3 * D:(s3 + 1) * D],
                                  o_sb[:, s2 * D:(s2 + 1) * D],
                                  o_sb[:, s2 * D:(s2 + 1) * D], out_deps)
                t_ow[j] = S.dma("sp", "outw%d" % s2, lambda s, j=j, s2=s2: s.dma_start(
                    out=out_d[j * 128:(j + 1) * 128, :], in_=o_sb[:, s2 * D:(s2 + 1) * D]), after=[t_o[j]])

            for j in range(NT + 3):
                if j >= 3:
                    cmb_fin(j - 3)
                if j < NT:
                    cmb_main(j)
                if 1 <= j <= NT:
                    cmb_mainb(j - 1)

            S._wait("sp", [S.token("outw0"), S.token("outw1")])
    return nc


_NC_CACHE = None


def _get_nc():
    global _NC_CACHE
    if _NC_CACHE is None:
        _NC_CACHE = build_nc()
    return _NC_CACHE


def _bf16(a):
    a = np.asarray(a)
    if a.dtype == np.uint16:
        return a.view(ml_dtypes.bfloat16)
    return a.astype(ml_dtypes.bfloat16) if a.dtype != ml_dtypes.bfloat16 else a


def kernel(L0_attn_x, L0_attn_Wq, L0_attn_Wk, L0_attn_Wv, L0_attn_Wo,
           L0_ffn_gate_w, L0_ffn_bias, L0_ffn_W1, L0_ffn_W3, L0_ffn_W2,
           L0_ffn_sw1, L0_ffn_sw3, L0_ffn_sw2):
    x = np.ascontiguousarray(np.asarray(L0_attn_x, dtype=np.float32))
    wqT = np.ascontiguousarray(np.asarray(L0_attn_Wq, dtype=np.float32).T)
    wkT = np.ascontiguousarray(np.asarray(L0_attn_Wk, dtype=np.float32).T)
    wvT = np.ascontiguousarray(np.asarray(L0_attn_Wv, dtype=np.float32).T)
    woT = np.ascontiguousarray(np.asarray(L0_attn_Wo, dtype=np.float32).T)
    gwT = np.ascontiguousarray(_bf16(L0_ffn_gate_w).T)
    gbias = np.ascontiguousarray(np.asarray(L0_ffn_bias, dtype=np.float32).reshape(1, E))
    eold = np.arange(E)
    enew = 16 * ((eold % 4) // 2) + 2 * (eold // 4) + (eold % 4) % 2
    gwT_p = np.empty_like(gwT); gwT_p[:, enew] = gwT
    gbias_p = np.empty_like(gbias); gbias_p[0, enew] = gbias[0]
    gwT, gbias = np.ascontiguousarray(gwT_p), np.ascontiguousarray(gbias_p)
    W1 = _bf16(L0_ffn_W1)
    W3 = _bf16(L0_ffn_W3)
    W2 = _bf16(L0_ffn_W2)
    sw1T = np.ascontiguousarray(_bf16(L0_ffn_sw1).T)
    sw3T = np.ascontiguousarray(_bf16(L0_ffn_sw3).T)
    sw2T = np.ascontiguousarray(_bf16(L0_ffn_sw2).T)

    idn = np.eye(128, dtype=ml_dtypes.bfloat16)
    idnf = np.eye(128, dtype=np.float32)
    maskT = np.triu(np.ones((128, 128), dtype=np.float32)).astype(ml_dtypes.bfloat16)
    triS = np.triu(np.ones((128, 128), dtype=np.float32), k=1).astype(ml_dtypes.bfloat16)
    iotaE = np.tile(np.arange(E, dtype=np.float32), (128, 1))
    onesf = np.ones((1, 128), np.float32)
    onesb = np.ones((128, 1), np.float32).astype(ml_dtypes.bfloat16)

    nc = _get_nc()
    in_maps = []
    for c in range(NCORES):
        ii = np.arange(NT)[None, :]
        tglob = (np.where(ii < NT // 2, c * (T // 2) + ii * 128,
                          NCORES * T // 2 + c * (T // 2) + (ii - NT // 2) * 128)
                 + np.arange(128)[:, None] + 1).astype(np.float32)
        cmask = (np.arange(NCORES) < c).astype(np.float32).reshape(NCORES, 1)
        oidx = np.concatenate([16 * c + np.arange(16), 128 + 16 * c + np.arange(16)]).astype(np.int32).reshape(32, 1)
        in_maps.append({
            "x": np.ascontiguousarray(x[c]),
            "wqT": wqT, "wkT": wkT, "wvT": wvT, "woT": woT,
            "gwT": gwT, "gbias": gbias,
            "w1s": np.ascontiguousarray(W1[c * EPC:(c + 1) * EPC].reshape(EPC * D, II)),
            "w3s": np.ascontiguousarray(W3[c * EPC:(c + 1) * EPC].reshape(EPC * D, II)),
            "w2s": np.ascontiguousarray(W2[c * EPC:(c + 1) * EPC].reshape(EPC * II, D)),
            "sw1T": sw1T, "sw3T": sw3T, "sw2T": sw2T,
            "idn": idn, "idnf": idnf, "maskT": maskT, "triS": triS, "iotaE": iotaE,
            "onesf": onesf, "onesb": onesb,
            "tglob": tglob, "cmask": cmask, "oidx": oidx,
        })
    res = run_bass_kernel_spmd(nc, in_maps, core_ids=list(range(NCORES)))
    out = np.stack([np.asarray(res.results[c]["out"]) for c in range(NCORES)], axis=0)
    return out.astype(np.float32)


# version 25 (final)
# speedup vs baseline: 1.0438x; 1.0365x over previous
import contextlib
import os
import numpy as np
import ml_dtypes
import concourse.bass as bass
import concourse.mybir as mybir
from concourse.bass import IndirectOffsetOnAxis
from concourse.bass_utils import run_bass_kernel_spmd

F32 = mybir.dt.float32
BF16 = mybir.dt.bfloat16
I32 = mybir.dt.int32
AF = mybir.ActivationFunctionType
ALU = mybir.AluOpType
AX = mybir.AxisListType

NCORES = 8
T = 2048
D = 1024
DK = 64
E = 32
EPC = 4
CAP = 1024
II = 512
NT = T // 128
OOB = 2.0e6
PHASE = int(os.environ.get("KPHASE", "5"))
KCH = int(os.environ.get("KCH", "16"))
KSUB = int(os.environ.get("KSUB", "63"))
KEL = int(os.environ.get("KEL", "9"))
KW = int(os.environ.get("KW", "1"))
KP2 = int(os.environ.get("KP2", "10"))
KPV = int(os.environ.get("KPV", "4"))


class Sched:

    def __init__(self, nc, es):
        self.nc = nc
        self.sem = {}
        self.cnt = {}
        self.scale = {}
        self.observed = {e: {} for e in ("sp", "act", "pe", "dve", "pool")}
        self.es = es
        for e in ("sp", "act", "pe", "dve", "pool"):
            self._new_sem(e, 1)
        self._new_sem("cc", 1)

    def _new_sem(self, key, scale):
        self.sem[key] = self.es.enter_context(self.nc.semaphore(f"s_{key}"))
        self.cnt[key] = 0
        self.scale[key] = scale

    def stream(self, key):
        if key not in self.sem:
            self._new_sem(key, 16)
        return key

    def _eng(self, e):
        return {"sp": self.nc.sync, "act": self.nc.scalar, "pe": self.nc.tensor,
                "dve": self.nc.vector, "pool": self.nc.gpsimd}[e]

    def _wait(self, engine, tokens):
        eng = self._eng(engine)
        need = {}
        for (k, val) in tokens:
            if val > need.get(k, -1):
                need[k] = val
        for k, val in sorted(need.items()):
            if self.observed[engine].get(k, 0) < val:
                eng.wait_ge(self.sem[k], val * self.scale[k])
                self.observed[engine][k] = val

    def op(self, engine, fn, after=()):
        self._wait(engine, after)
        inst = fn(self._eng(engine))
        inst.then_inc(self.sem[engine], 1)
        self.cnt[engine] += 1
        return (engine, self.cnt[engine])

    def dma(self, engine, stream_key, fn, after=()):
        self.stream(stream_key)
        self._wait(engine, after)
        inst = fn(self._eng(engine))
        inst.then_inc(self.sem[stream_key], 16)
        self.cnt[stream_key] += 1
        return (stream_key, self.cnt[stream_key])

    def collective(self, fn, after=()):
        self._wait("pool", after)
        inst = fn(self.nc.gpsimd)
        inst.then_inc(self.sem["cc"], 1)
        self.cnt["cc"] += 1
        return ("cc", self.cnt["cc"])

    def token(self, key):
        return (key, self.cnt[key])


def build_nc():
    nc = bass.Bass("TRN2", target_bir_lowering=False, debug=False)

    dtn = lambda n, s, d, **kw: nc.dram_tensor(n, s, d, **kw).ap()
    x_in = dtn("x", [T, D], F32, kind="ExternalInput")
    wqT_in = dtn("wqT", [D, DK], F32, kind="ExternalInput")
    wkT_in = dtn("wkT", [D, DK], F32, kind="ExternalInput")
    wvT_in = dtn("wvT", [D, DK], F32, kind="ExternalInput")
    woT_in = dtn("woT", [DK, D], F32, kind="ExternalInput")
    gwT_in = dtn("gwT", [D, E], BF16, kind="ExternalInput")
    gb_in = dtn("gbias", [1, E], F32, kind="ExternalInput")
    w1_in = dtn("w1s", [EPC * D, II], BF16, kind="ExternalInput")
    w3_in = dtn("w3s", [EPC * D, II], BF16, kind="ExternalInput")
    w2_in = dtn("w2s", [EPC * II, D], BF16, kind="ExternalInput")
    sw1_in = dtn("sw1T", [D, II], BF16, kind="ExternalInput")
    sw3_in = dtn("sw3T", [D, II], BF16, kind="ExternalInput")
    sw2_in = dtn("sw2T", [II, D], BF16, kind="ExternalInput")
    idn_in = dtn("idn", [128, 128], BF16, kind="ExternalInput")
    idnf_in = dtn("idnf", [128, 128], F32, kind="ExternalInput")
    maskT_in = dtn("maskT", [128, 128], BF16, kind="ExternalInput")
    triS_in = dtn("triS", [128, 128], BF16, kind="ExternalInput")
    iotaE_in = dtn("iotaE", [128, E], F32, kind="ExternalInput")
    onesf_in = dtn("onesf", [1, 128], F32, kind="ExternalInput")
    onesb_in = dtn("onesb", [128, 1], BF16, kind="ExternalInput")
    tglob_in = dtn("tglob", [128, NT], F32, kind="ExternalInput")
    cmask_in = dtn("cmask", [NCORES, 1], F32, kind="ExternalInput")
    oidx_in = dtn("oidx", [32, 1], I32, kind="ExternalInput")

    out_d = dtn("out", [T, D], F32, kind="ExternalOutput")

    x1d = dtn("x1d", [T, D], F32, kind="Internal")
    xag_i = dtn("xag_i", [T, D], BF16, kind="Internal")
    xall = dtn("xall", [NCORES * T, D], BF16, kind="Internal",
               addr_space=("Shared" if PHASE >= 3 else "Local"))
    cnt_i = dtn("cnt_i", [1, E], F32, kind="Internal")
    cnt_o = dtn("cnt_o", [NCORES, E], F32, kind="Internal")
    amap_i = dtn("amap_i", [E * CAP, 1], F32, kind="Internal")
    amap_o = dtn("amap_o", [E * CAP, 1], F32, kind="Internal")
    yshd = dtn("yshd", [T, D], BF16, kind="Internal")
    yec = dtn("yec", [EPC * CAP, D], BF16, kind="Internal")
    yeall = dtn("yeall", [E * CAP, D], BF16, kind="Internal",
                addr_space=("Shared" if PHASE >= 3 else "Local"))
    agos = [dtn("ago%d" % k, [NCORES * CAP, D], BF16, kind="Internal") for k in range(EPC)]

    es = contextlib.ExitStack()
    sb = lambda n, s, d: es.enter_context(nc.sbuf_tensor(n, s, d))

    idn = sb("c_idn", [128, 128], BF16)
    idnf = sb("c_idnf", [128, 128], F32)
    maskT = sb("c_maskT", [128, 128], BF16)
    triS = sb("c_triS", [128, 128], BF16)
    iotaE = sb("c_iotaE", [128, E], F32)
    onesf = sb("c_onesf", [1, 128], F32)
    onesb = sb("c_onesb", [128, 1], BF16)
    tglob = sb("c_tglob", [128, NT], F32)
    cmask = sb("c_cmask", [NCORES, 1], F32)
    oidx = sb("c_oidx", [32, 1], I32)
    gbias = sb("c_gbias", [1, E], F32)
    w32a = sb("w32a", [128, 512], F32)
    wq_b = sb("wq_b", [128, 512], BF16)
    wk_b = sb("wk_b", [128, 512], BF16)
    wv_b = sb("wv_b", [128, 512], BF16)
    wo_b = sb("wo_b", [64, 1024], BF16)
    gw_b = sb("gw_b", [128, 8 * E], BF16)
    w1e = sb("w1e", [128, 2 * 8 * II], BF16)
    w3e = sb("w3e", [128, 2 * 8 * II], BF16)
    w2e = sb("w2e", [128, 2 * 4 * D], BF16)
    xf = sb("xf", [128, 3 * D], F32)
    xb = sb("xb", [128, 2 * D], BF16)
    qraw = sb("qraw", [64, 512], F32)
    Q_all = sb("Q_all", [64, NT * 128], BF16)
    K_all = sb("K_all", [64, NT * 128], BF16)
    Kf_all = sb("Kf_all", [128, NT * DK], BF16)
    V_all = sb("V_all", [128, NT * 65], BF16)
    S_sb = sb("S_sb", [64, 65], BF16)
    S32_sb = sb("S32_sb", [64, 65], F32)
    qe32 = sb("qe32", [64, 512], F32)
    At_b = sb("At_b", [128, 2 * 128], BF16)
    ao_sb = sb("ao_sb", [128, 2 * 64], BF16)
    aoT_sb = sb("aoT_sb", [64, 2 * 128], BF16)
    r_sb = sb("r_sb", [128, 2 * D], F32)
    x1f_sb = sb("x1f_sb", [128, 2 * D], F32)
    x1b_sb = sb("x1b_sb", [128, 2 * D], BF16)
    sm = sb("sm", [128, 24], F32)
    wg_sb = sb("wg_sb", [128, 4 * NT], F32)
    ek_sb = sb("ek_sb", [128, 4 * NT], F32)
    g_sb = sb("g_sb", [128, 4 * NT], F32)
    posk_sb = sb("posk_sb", [128, 4 * NT], F32)
    I_sb = sb("I_sb", [128, NT * E], BF16)
    rb_sb = sb("rb_sb", [1, NT * E], F32)
    cs_all = sb("cs_all", [1, NT * E], F32)
    counts_sb = sb("counts_sb", [1, E], F32)
    call_sb = sb("call_sb", [NCORES, E], F32)
    base_sb = sb("base_sb", [1, E], F32)
    bias_b = sb("bias_b", [128, E], F32)
    rt_all = sb("rt_all", [1, NT * E], F32)
    tk_sb = sb("tk_sb", [128, 6 * NT], F32)
    gf_sb = sb("gf_sb", [128, 4 * NT], F32)
    sidx_i = sb("sidx_i", [128, 4 * NT], I32)
    zz_sb = sb("zz_sb", [128, 256], F32)
    am_sb = sb("am_sb", [32, 128], F32)
    gi32 = sb("gi32", [128, E], F32)
    gidx_i = sb("gidx_i", [128, E], I32)
    xe_t = sb("xe_t", [128, 3 * D], BF16)
    xeT_sb = sb("xeT_sb", [128, 2 * D], BF16)
    h_pipe = sb("h_pipe", [128, 2 * II], BF16)
    hT_sb = sb("hT_sb", [128, 2 * II], BF16)
    s1_sb = sb("s1_sb", [128, 2 * II], F32)
    ye_sb = sb("ye_sb", [128, 2 * D], BF16)
    ysh_sb = sb("ysh_sb", [128, 2 * D], BF16)
    qf_sb = sb("qf_sb", [64, 128], F32)
    es_x1 = contextlib.ExitStack()
    x1bT = es_x1.enter_context(nc.sbuf_tensor("x1bT", [128, NT * D], BF16))
    xT_all = x1bT
    Vt_all = es_x1.enter_context(nc.sbuf_tensor("Vt_all", [64, NT * 128], BF16))
    lg_all = es_x1.enter_context(nc.sbuf_tensor("lg_all", [128, NT * E], F32))
    sc_all = es_x1.enter_context(nc.sbuf_tensor("sc_all", [128, NT * E], F32))
    u_all = es_x1.enter_context(nc.sbuf_tensor("u_all", [128, NT * E], F32))
    hh_sb = es_x1.enter_context(nc.sbuf_tensor("hh_sb", [128, 4 * NT * E], BF16))
    P_all = es_x1.enter_context(nc.sbuf_tensor("P_all", [128, NT * E], F32))
    rbt_all = es_x1.enter_context(nc.sbuf_tensor("rbt_all", [128, NT * E], F32))
    tmpW = es_x1.enter_context(nc.sbuf_tensor("tmpW", [128, NT * E], F32))
    cmb = {}

    pq = es.enter_context(nc.psum_tensor("pqm", [128, 512], F32))
    pk = es.enter_context(nc.psum_tensor("pkm", [128, 512], F32))
    pm = pq
    prb = pk
    es_att = contextlib.ExitStack()
    tp_att = es_att.enter_context(nc.psum_tensor("tp_att", [128, 1024], BF16))
    tp_b = es_att.enter_context(nc.psum_tensor("tp_b", [128, 1024], BF16))
    pVAN = es_att.enter_context(nc.psum_tensor("pVAN", [128, 512], F32))
    pS = es_att.enter_context(nc.psum_tensor("pS", [64, 2 * 65], F32))
    wo_ps = es_att.enter_context(nc.psum_tensor("wo_ps", [128, 1024], F32))

    S = Sched(nc, es)
    psd = {}

    with es, nc.Block() as block:
        @block.sync
        def _(sync):
            t_c = None
            for dst, src in ((idn, idn_in), (idnf, idnf_in), (maskT, maskT_in),
                             (triS, triS_in), (iotaE, iotaE_in), (onesf, onesf_in),
                             (onesb, onesb_in), (tglob, tglob_in), (cmask, cmask_in),
                             (oidx, oidx_in), (gbias, gb_in)):
                t_c = S.dma("sp", "c", lambda s, dst=dst, src=src: s.dma_start(out=dst[:], in_=src))
            t_consts = t_c

            t_awb = None
            for dst, src in ((wq_b, wqT_in), (wk_b, wkT_in), (wv_b, wvT_in)):
                taw = S.dma("act", "aw", lambda s, src=src: s.dma_start(
                    out=w32a[:].rearrange("p (c j) -> p c j", j=DK),
                    in_=src.rearrange("(c p) j -> p c j", p=128)),
                    after=[] if t_awb is None else [t_awb])
                t_awb = S.op("dve", lambda vv, dst=dst: vv.tensor_copy(dst[:], w32a[:]), after=[taw])
            for half in range(2):
                taw = S.dma("act", "aw", lambda s, half=half: s.dma_start(
                    out=w32a[0:64, 0:512], in_=woT_in[:, half * 512:(half + 1) * 512]),
                    after=[t_awb])
                t_awb = S.op("dve", lambda vv, half=half: vv.tensor_copy(
                    wo_b[:, half * 512:(half + 1) * 512], w32a[0:64, 0:512]), after=[taw])
            t_vones = S.op("dve", lambda vv: vv.memset(V_all[:], 1.0))
            t_szero = S.op("dve", lambda vv: vv.memset(S_sb[:], 0.0))
            t_szero2 = S.op("dve", lambda vv: vv.memset(S32_sb[:], 0.0))
            t_zz = S.op("dve", lambda vv: vv.memset(zz_sb[:], 0.0))
            t_eps = S.op("dve", lambda vv: vv.memset(sm[:, 18:19], 1e-5))
            t_rb0 = S.op("dve", lambda vv: vv.memset(rb_sb[0:1, 0:E], 0.0))
            t_xez = S.op("dve", lambda vv: vv.memset(xe_t[:], 0.0))

            S.dma("act", "gw", lambda s: s.dma_start(
                out=gw_b[:].rearrange("p (c e) -> p c e", e=E),
                in_=gwT_in.rearrange("(c p) e -> p c e", p=128)))
            t_gw = S.token("gw")
            for dst, src, nch in (((w1e, sw1_in, 8), (w3e, sw3_in, 8), (w2e, sw2_in, 4)) if KW else ()):
                off = 8 * II if nch == 8 else 4 * D
                S.dma("act", "we1", lambda s, dst=dst, src=src, nch=nch, off=off: s.dma_start(
                    out=dst[:, off:2 * off].rearrange("p (c i) -> p c i", c=nch),
                    in_=src.rearrange("(c p) i -> p c i", p=128)))
            S.stream("we1"); S.stream("we0")
            t_sw = S.token("we1")
            t_amz = S.dma("sp", "amz", lambda s: s.dma_start(
                out=amap_i.rearrange("(g p b) c -> p g (b c)", p=128, b=128),
                in_=zz_sb[:].rearrange("p (g b) -> p g b", g=2)), after=[t_zz])

            t_wexp = {}

            def load_expert_w(e, gate_tok):
                eb = e % 2
                strm = "we%d" % eb
                deps = [] if gate_tok is None else [gate_tok]
                S.dma("act", strm, lambda s, e=e, eb=eb: s.dma_start(
                    out=w1e[:, eb * 8 * II:(eb + 1) * 8 * II].rearrange("p (c i) -> p c i", c=8),
                    in_=w1_in.rearrange("(e c p) i -> e p c i", e=EPC, p=128)[e]), after=deps)
                S.dma("act", strm, lambda s, e=e, eb=eb: s.dma_start(
                    out=w3e[:, eb * 8 * II:(eb + 1) * 8 * II].rearrange("p (c i) -> p c i", c=8),
                    in_=w3_in.rearrange("(e c p) i -> e p c i", e=EPC, p=128)[e]))
                S.dma("act", strm, lambda s, e=e, eb=eb: s.dma_start(
                    out=w2e[:, eb * 4 * D:(eb + 1) * 4 * D].rearrange("p (c i) -> p c i", c=4),
                    in_=w2_in.rearrange("(e c p) i -> e p c i", e=EPC, p=128)[e]))
                t_wexp[e] = S.token(strm)

            if KW:
                load_expert_w(0, None)

            if PHASE == 0:
                es_att.close()
                S._wait("sp", [S.token(k) for k in list(S.cnt) if S.cnt[k] > 0])
                S.dma("sp", "outw", lambda s: s.dma_start(out=out_d, in_=x1d))
                S._wait("sp", [S.token("outw")])
                return

            t_x, t_x2, t_xb, t_xtr, t_xT = {}, {}, {}, {}, {}
            t_qe, t_ke, t_kf = {}, {}, {}
            t_v, t_vsb, t_A, t_Ab, t_num, t_st, t_Ssb = {}, {}, {}, {}, {}, {}, {}
            t_vtr = {}
            t_sacc = {}
            t_ao, t_aoT, t_aoTs, t_wo, t_r, t_ln, t_x1b, t_x1bT = {}, {}, {}, {}, {}, {}, {}, {}
            t_tsd = {}
            t_x1w, t_xagw = {}, {}
            t_Ssb[0] = t_szero

            def smc(i, n):
                b = 2 + (i % 2) * 8
                return sm[:, b + n:b + n + 1]

            sq_prev = [None]

            def ln_stats(i, acc_tok, acc_ap, sq_slice, out_deps):
                tm = S.op("act", lambda s: s.mul(smc(i, 1), smc(i, 0), 1.0 / D), after=[acc_tok])
                tng = S.op("act", lambda s: s.mul(smc(i, 2), smc(i, 1), -1.0), after=[tm])
                sq_dep = [tng] + list(out_deps) + ([sq_prev[0]] if sq_prev[0] is not None else [])
                tsq = S.op("act", lambda s: s.activation(
                    out=sq_slice, in_=acc_ap, func=AF.Square, bias=smc(i, 2), accum_out=smc(i, 3)), after=sq_dep)
                sq_prev[0] = tsq
                tva = S.op("act", lambda s: s.mul(smc(i, 4), smc(i, 3), 1.0 / D), after=[tsq])
                tsd = S.op("act", lambda s: s.activation(
                    out=smc(i, 5), in_=smc(i, 4), func=AF.Sqrt, bias=sm[:, 18:19]), after=[tva, t_eps])
                return tsd

            def ln_apply(i, acc_ap, out_ap, out_deps, tsd):
                trs = S.op("dve", lambda vv: vv.reciprocal(smc(i, 6), smc(i, 5)), after=[tsd])
                tnm = S.op("act", lambda s: s.activation(
                    out=smc(i, 7), in_=smc(i, 2), func=AF.Copy, scale=smc(i, 6)), after=[trs])
                tl = S.op("act", lambda s: s.activation(
                    out=out_ap, in_=acc_ap, func=AF.Identity, scale=smc(i, 6), bias=smc(i, 7)),
                    after=[tnm] + list(out_deps))
                return tl

            def ln_chain(i, acc_tok, acc_ap, out_ap, sq_slice, out_deps):
                tsd = ln_stats(i, acc_tok, acc_ap, sq_slice, out_deps)
                return ln_apply(i, acc_ap, out_ap, out_deps, tsd)

            def att_pass1(i):
                s4 = i % 3
                dep = [t_xb[i - 3]] if i >= 3 else []
                t_x[i] = S.dma("sp", "x%d" % s4, lambda s, i=i, s4=s4: s.dma_start(
                    out=xf[:, s4 * D:(s4 + 1) * D], in_=x_in[i * 128:(i + 1) * 128, :]), after=dep)
                dep = [t_x[i]]
                if i >= 2:
                    dep.append(t_xtr[i - 2])
                t_xb[i] = S.op("dve", lambda vv, i=i, s4=s4: vv.tensor_copy(
                    xb[:, (i % 2) * D:(i % 2 + 1) * D], xf[:, s4 * D:(s4 + 1) * D]), after=dep)
                dep = [t_xb[i], t_consts]
                if i >= 1:
                    dep.append(t_xT[i - 1])
                tr = None
                for dc in range(8):
                    tr = S.op("pe", lambda p, dc=dc, i=i: p.transpose(
                        tp_att[:, dc * 128:(dc + 1) * 128],
                        xb[:, (i % 2) * D + dc * 128:(i % 2) * D + (dc + 1) * 128], idn[:]),
                        after=dep if dc == 0 else ())
                t_xtr[i] = tr
                t_xT[i] = S.op("dve", lambda vv, i=i: vv.tensor_copy(
                    xT_all[:, :].rearrange("p (dc t) -> p dc t", t=T)[:, :, i * 128:(i + 1) * 128],
                    tp_att[:, :].rearrange("p (dc t) -> p dc t", t=128)), after=[tr])

            t_qkve = {}

            def qkv_wide():
                last_min = [None]
                for g4 in range(4):
                    depx = [t_xT[4 * g4 + 3], t_awb]
                    for which, wb, bank, dst in (("q", wq_b, pq, Q_all), ("k", wk_b, pk, K_all)):
                        dep = list(depx)
                        if (which, g4 - 1) in t_qkve:
                            dep.append(t_qkve[(which, g4 - 1)])
                        mm = None
                        for dc in range(8):
                            mm = S.op("pe", lambda p, dc=dc, g4=g4, wb=wb, bank=bank: p.matmul(
                                bank[0:64, :], wb[:, dc * DK:(dc + 1) * DK],
                                xT_all[:, dc * T + g4 * 512:dc * T + (g4 + 1) * 512],
                                start=(dc == 0), stop=(dc == 7)), after=dep if dc == 0 else ())
                        tev = S.op("dve", lambda vv, bank=bank: vv.tensor_copy(qraw[:], bank[0:64, :]),
                                   after=[mm] + ([last_min[0]] if last_min[0] is not None else []))
                        texp = S.op("act", lambda s: s.activation(
                            out=qe32[:], in_=qraw[:], func=AF.Exp), after=[tev])
                        tadd = S.op("dve", lambda vv: vv.tensor_scalar(
                            qraw[:], qraw[:], 1.0, None, ALU.add), after=[texp])
                        t_qkve[(which, g4)] = S.op("dve", lambda vv, g4=g4, dst=dst: vv.tensor_tensor(
                            dst[:, g4 * 512:(g4 + 1) * 512], qe32[:], qraw[:], ALU.min), after=[tadd])
                        last_min[0] = t_qkve[(which, g4)]
                    dep = list(depx)
                    if ("v", g4 - 1) in t_qkve:
                        dep.append(t_qkve[("v", g4 - 1)])
                    mm = None
                    for dc in range(8):
                        mm = S.op("pe", lambda p, dc=dc, g4=g4: p.matmul(
                            pVAN[0:64, 0:512], wv_b[:, dc * DK:(dc + 1) * DK],
                            xT_all[:, dc * T + g4 * 512:dc * T + (g4 + 1) * 512],
                            start=(dc == 0), stop=(dc == 7)), after=dep if dc == 0 else ())
                    t_qkve[("v", g4)] = S.op("dve", lambda vv, g4=g4: vv.tensor_copy(
                        Vt_all[:, g4 * 512:(g4 + 1) * 512], pVAN[0:64, 0:512]), after=[mm])

            def kfv_trans():
                prev = None
                for i in range(NT):
                    g4 = i // 4
                    dep = [t_qkve[("k", g4)], t_consts]
                    if prev is not None:
                        dep.append(prev)
                    ttr = S.op("pe", lambda p, i=i: p.transpose(
                        tp_att[:, 0:64], K_all[:, i * 128:(i + 1) * 128], idn[0:64, 0:64]), after=dep)
                    t_kf[i] = S.op("dve", lambda vv, i=i: vv.tensor_copy(
                        Kf_all[:, i * DK:(i + 1) * DK], tp_att[:, 0:64]), after=[ttr])
                    ttr2 = S.op("pe", lambda p, i=i: p.transpose(
                        tp_att[:, 64:128], Vt_all[:, i * 128:(i + 1) * 128], idn[0:64, 0:64]),
                        after=[t_qkve[("v", g4)], t_kf[i]])
                    t_vsb[i] = S.op("dve", lambda vv, i=i: vv.tensor_copy(
                        V_all[:, i * 65:i * 65 + 64], tp_att[:, 64:128]), after=[ttr2, t_vones])
                    prev = t_vsb[i]

            def lp3(s4):
                return 15 - ((15 - s4) % 3)

            def att_pass2(i):
                s4, s2 = i % 3, i % 2
                dep = [t_xb[lp3(s4)]]
                if i >= 3:
                    dep.append(t_r[i - 3])
                t_x2[i] = S.dma("sp", "x%d" % s4, lambda s, i=i, s4=s4: s.dma_start(
                    out=xf[:, s4 * D:(s4 + 1) * D], in_=x_in[i * 128:(i + 1) * 128, :]), after=dep)
                dep = [t_vsb[i]]
                if i >= 1:
                    dep.append(t_Ab[i - 1])
                t_A[i] = S.op("pe", lambda p, i=i: p.matmul(
                    pVAN[:, 0:128], K_all[:, i * 128:(i + 1) * 128], Q_all[:, i * 128:(i + 1) * 128],
                    start=True, stop=True), after=dep)
                dep = [t_A[i], t_consts]
                if i >= 2:
                    dep.append(t_num[i - 2])
                t_Ab[i] = S.op("dve", lambda vv, s2=s2: vv.tensor_tensor(
                    At_b[:, s2 * 128:(s2 + 1) * 128], pVAN[:, 0:128], maskT[:], ALU.mult), after=dep)
                no2 = 128 if s2 == 0 else 224
                dep = [t_Ab[i], t_Ssb[i]]
                if i >= 2:
                    dep.append(t_ao[i - 2])
                if i == 0:
                    t_num[i] = S.op("pe", lambda p, i=i, s2=s2, no2=no2: p.matmul(
                        pVAN[:, no2:no2 + 65], At_b[:, s2 * 128:(s2 + 1) * 128],
                        V_all[:, i * 65:(i + 1) * 65], start=True, stop=True), after=dep)
                else:
                    mmn = S.op("pe", lambda p, i=i, s2=s2, no2=no2: p.matmul(
                        pVAN[:, no2:no2 + 65], At_b[:, s2 * 128:(s2 + 1) * 128],
                        V_all[:, i * 65:(i + 1) * 65], start=True, stop=False), after=dep)
                    t_num[i] = S.op("pe", lambda p, i=i, no2=no2: p.matmul(
                        pVAN[:, no2:no2 + 65], Q_all[:, i * 128:(i + 1) * 128], S_sb[:],
                        start=False, stop=True))
                dep = [t_kf[i], t_vsb[i]]
                if i >= 2:
                    dep.append(t_sacc[i - 2])
                t_st[i] = S.op("pe", lambda p, i=i, s2=s2: p.matmul(
                    pS[:, s2 * 65:(s2 + 1) * 65], Kf_all[:, i * DK:(i + 1) * DK],
                    V_all[:, i * 65:(i + 1) * 65], start=True, stop=True), after=dep)
                t_sacc[i] = S.op("dve", lambda vv, s2=s2: vv.tensor_tensor(
                    S32_sb[:], S32_sb[:], pS[:, s2 * 65:(s2 + 1) * 65], ALU.add),
                    after=[t_st[i], t_szero2])
                if i + 1 < NT:
                    t_Ssb[i + 1] = S.op("dve", lambda vv: vv.tensor_copy(S_sb[:], S32_sb[:]),
                                        after=[t_sacc[i], t_num[i]])
                td1 = S.op("dve", lambda vv, no2=no2: vv.tensor_scalar(
                    sm[:, 0:1], pVAN[:, no2 + 64:no2 + 65], 1e-6, None, ALU.add), after=[t_num[i]])
                td2 = S.op("dve", lambda vv: vv.reciprocal(sm[:, 1:2], sm[:, 0:1]), after=[td1])
                dep = [td2]
                if i >= 2:
                    dep.append(t_aoT[i - 2])
                t_ao[i] = S.op("dve", lambda vv, s2=s2, no2=no2: vv.tensor_scalar(
                    ao_sb[:, s2 * 64:(s2 + 1) * 64], pVAN[:, no2:no2 + 64], sm[:, 1:2], None, ALU.mult), after=dep)
            def att_pass2a2(i):
                s4, s2 = i % 3, i % 2
                dep = [t_ao[i]]
                if i >= 1:
                    dep.append(t_aoTs[i - 1])
                else:
                    dep.append(t_vsb[NT - 1])
                t_aoT[i] = S.op("pe", lambda p, s2=s2: p.transpose(
                    tp_att[0:64, 0:128], ao_sb[:, s2 * 64:(s2 + 1) * 64], idn[:]), after=dep)
                dep = [t_aoT[i]]
                if i >= 2:
                    dep.append(t_wo[i - 2])
                t_aoTs[i] = S.op("dve", lambda vv, s2=s2: vv.tensor_copy(
                    aoT_sb[:, s2 * 128:(s2 + 1) * 128], tp_att[0:64, 0:128]), after=dep)
                dep = [t_aoTs[i], t_awb]
                if i >= 1:
                    dep.append(t_r[i - 1])
                S.op("pe", lambda p, s2=s2: p.matmul(
                    wo_ps[:, 0:512], aoT_sb[:, s2 * 128:(s2 + 1) * 128], wo_b[:, 0:512],
                    start=True, stop=True), after=dep)
                t_wo[i] = S.op("pe", lambda p, s2=s2: p.matmul(
                    wo_ps[:, 512:1024], aoT_sb[:, s2 * 128:(s2 + 1) * 128], wo_b[:, 512:1024],
                    start=True, stop=True))
                dep = [t_wo[i], t_x2[i]]
                if i >= 2:
                    dep.append(t_ln[i - 2])
                t_r[i] = S.op("dve", lambda vv, s4=s4, s2=s2, i=i: vv.scalar_tensor_tensor(
                    r_sb[:, s2 * D:(s2 + 1) * D], xf[:, s4 * D:(s4 + 1) * D], 1.0, wo_ps[:],
                    ALU.mult, ALU.add, accum_out=smc(i, 0)), after=dep)
                out_deps = []
                if i >= 2:
                    out_deps += [t_x1w[i - 2], t_x1b[i - 2]]
                t_tsd[i] = ln_stats(i, t_r[i], r_sb[:, s2 * D:(s2 + 1) * D],
                                    x1f_sb[:, s2 * D:(s2 + 1) * D], out_deps)

            def att_pass2b(i):
                s2 = i % 2
                out_deps = []
                if i >= 2:
                    out_deps += [t_x1w[i - 2], t_x1b[i - 2]]
                t_ln[i] = ln_apply(i, r_sb[:, s2 * D:(s2 + 1) * D],
                                   x1f_sb[:, s2 * D:(s2 + 1) * D], out_deps, t_tsd[i])
                dep = [t_ln[i]]
                if i >= 2:
                    dep += [t_xagw[i - 2], t_x1bT[i - 2]]
                t_x1b[i] = S.op("dve", lambda vv, s2=s2: vv.tensor_copy(
                    x1b_sb[:, s2 * D:(s2 + 1) * D], x1f_sb[:, s2 * D:(s2 + 1) * D]), after=dep)
                t_x1w[i] = S.dma("sp", "x1w%d" % s2, lambda s, i=i, s2=s2: s.dma_start(
                    out=x1d[i * 128:(i + 1) * 128, :], in_=x1f_sb[:, s2 * D:(s2 + 1) * D]), after=[t_ln[i]])
                t_xagw[i] = S.dma("sp", "xag%s%d" % ("AB"[i // 8], s2), lambda s, i=i, s2=s2: s.dma_start(
                    out=xag_i[i * 128:(i + 1) * 128, :], in_=x1b_sb[:, s2 * D:(s2 + 1) * D]), after=[t_x1b[i]])
                dep = [t_x1b[i]]
                if i >= 1:
                    dep.append(t_x1bT[i - 1])
                tr = None
                for dc in range(8):
                    tr = S.op("pe", lambda p, dc=dc, s2=s2: p.transpose(
                        tp_b[:, dc * 128:(dc + 1) * 128],
                        x1b_sb[:, s2 * D + dc * 128:s2 * D + (dc + 1) * 128], idn[:]),
                        after=dep if dc == 0 else ())
                t_x1bT[i] = S.op("dve", lambda vv, i=i: vv.tensor_copy(
                    x1bT[:, i * D:(i + 1) * D], tp_b[:]), after=[tr])

            for i in range(NT):
                att_pass1(i)
            qkv_wide()
            kfv_trans()

            t_bb0 = S.op("pe", lambda p: p.matmul(pm[:, 64:96], onesf[:], gbias[:], start=True, stop=True),
                         after=[t_consts, t_qkve[("q", 3)], t_qkve[("k", 3)]])
            t_bb = S.op("dve", lambda vv: vv.tensor_copy(bias_b[:], pm[:, 64:96]), after=[t_bb0])

            t_lg, t_lge = {}, {}
            t_cse_guard = [None]

            def routing(j):
                dep = [t_x1bT[j], t_gw, t_bb]
                if j >= 1:
                    dep.append(t_lge[j - 1])
                mm = None
                for dc in range(8):
                    mm = S.op("pe", lambda p, dc=dc, j=j: p.matmul(
                        pm[:, 0:32], x1bT[:, j * D + dc * 128:j * D + (dc + 1) * 128],
                        gw_b[:, dc * E:(dc + 1) * E],
                        start=(dc == 0), stop=(dc == 7)), after=dep if dc == 0 else ())
                t_lg[j] = mm
                t_lge[j] = S.op("dve", lambda vv, j=j: vv.tensor_copy(
                    lg_all[:, j * E:(j + 1) * E], pm[:, 0:32]), after=[mm])

            def v3(ap):
                return ap.rearrange("p (j e) -> p j e", e=E)

            def routing_batch():
                t_s = S.op("act", lambda s: s.activation(out=sc_all[:], in_=lg_all[:], func=AF.Sigmoid),
                           after=[t_lge[NT - 1]])
                last = S.op("dve", lambda vv: vv.tensor_tensor(
                    v3(u_all[:]), v3(sc_all[:]),
                    bias_b[:].rearrange("p (c e) -> p c e", c=1).broadcast_to([128, NT, E]),
                    ALU.add), after=[t_s, t_bb])
                for k in range(4):
                    mxc = tk_sb[:, k * NT:(k + 1) * NT]
                    last = S.op("dve", lambda vv, mxc=mxc: vv.reduce_max(
                        mxc, v3(u_all[:]), axis=AX.X), after=[last])
                    hk = hh_sb[:, k * NT * E:(k + 1) * NT * E]
                    last = S.op("dve", lambda vv, hk=hk, mxc=mxc: vv.tensor_tensor(
                        v3(hk), v3(u_all[:]),
                        mxc.rearrange("p (j e) -> p j e", e=1).broadcast_to([128, NT, E]),
                        ALU.is_equal), after=[last])
                    last = S.op("dve", lambda vv, hk=hk: vv.tensor_tensor(
                        tmpW[:], hk, sc_all[:], ALU.mult), after=[last])
                    last = S.op("dve", lambda vv, k=k: vv.reduce_sum(
                        wg_sb[:, k * NT:(k + 1) * NT], v3(tmpW[:]), axis=AX.X), after=[last])
                    last = S.op("dve", lambda vv, hk=hk: vv.tensor_tensor(
                        v3(tmpW[:]), v3(hk),
                        iotaE[:].rearrange("p (c e) -> p c e", c=1).broadcast_to([128, NT, E]),
                        ALU.mult), after=[last])
                    last = S.op("dve", lambda vv, k=k: vv.reduce_sum(
                        ek_sb[:, k * NT:(k + 1) * NT], v3(tmpW[:]), axis=AX.X), after=[last])
                    if k < 3:
                        last = S.op("dve", lambda vv, hk=hk: vv.scalar_tensor_tensor(
                            u_all[:], hk, -1e30, u_all[:], ALU.mult, ALU.add), after=[last])
                lastI = S.op("dve", lambda vv: vv.tensor_tensor(
                    tmpW[:], hh_sb[:, 0:NT * E], hh_sb[:, NT * E:2 * NT * E], ALU.add), after=[last])
                lastI = S.op("dve", lambda vv: vv.tensor_tensor(
                    tmpW[:], tmpW[:], hh_sb[:, 2 * NT * E:3 * NT * E], ALU.add), after=[lastI])
                t_I = S.op("dve", lambda vv: vv.tensor_tensor(
                    I_sb[:], tmpW[:], hh_sb[:, 3 * NT * E:4 * NT * E], ALU.add), after=[lastI])
                t_csm = S.op("pe", lambda p: p.matmul(
                    prb[0:1, :], onesb[:], I_sb[:], start=True, stop=True), after=[t_I, t_consts])
                t_cse = S.op("dve", lambda vv: vv.tensor_copy(cs_all[:], prb[0:1, :]), after=[t_csm])
                t_cse_guard[0] = t_cse
                last3 = t_rb0
                for j in range(1, NT):
                    last3 = S.op("dve", lambda vv, j=j: vv.tensor_tensor(
                        rb_sb[0:1, j * E:(j + 1) * E], rb_sb[0:1, (j - 1) * E:j * E],
                        cs_all[0:1, (j - 1) * E:j * E], ALU.add), after=[last3, t_cse])
                t_cnt = S.op("dve", lambda vv: vv.tensor_tensor(
                    counts_sb[:], rb_sb[0:1, (NT - 1) * E:NT * E],
                    cs_all[0:1, (NT - 1) * E:NT * E], ALU.add), after=[last3])
                last = S.op("dve", lambda vv: vv.tensor_tensor(
                    tk_sb[:, 4 * NT:5 * NT], wg_sb[:, 0:NT], wg_sb[:, NT:2 * NT], ALU.add), after=[last, t_cnt])
                last = S.op("dve", lambda vv: vv.tensor_tensor(
                    tk_sb[:, 4 * NT:5 * NT], tk_sb[:, 4 * NT:5 * NT], wg_sb[:, 2 * NT:3 * NT], ALU.add), after=[last])
                last = S.op("dve", lambda vv: vv.tensor_tensor(
                    tk_sb[:, 4 * NT:5 * NT], tk_sb[:, 4 * NT:5 * NT], wg_sb[:, 3 * NT:4 * NT], ALU.add), after=[last])
                last = S.op("dve", lambda vv: vv.tensor_scalar(
                    tk_sb[:, 4 * NT:5 * NT], tk_sb[:, 4 * NT:5 * NT], 1e-12, None, ALU.max), after=[last])
                last = S.op("dve", lambda vv: vv.reciprocal(
                    tk_sb[:, 5 * NT:6 * NT], tk_sb[:, 4 * NT:5 * NT]), after=[last])
                last = S.op("dve", lambda vv: vv.tensor_tensor(
                    g_sb[:].rearrange("p (k j) -> p k j", k=4),
                    wg_sb[:].rearrange("p (k j) -> p k j", k=4),
                    tk_sb[:, 5 * NT:6 * NT].rearrange("p (c j) -> p c j", c=1).broadcast_to([128, 4, NT]),
                    ALU.mult), after=[last])
                t_pc = {}
                prev_mm = None
                for c in range(4):
                    dep = [t_I, t_consts]
                    if prev_mm is not None:
                        dep.append(t_pc[c - 1])
                    mmp = S.op("pe", lambda p, c=c: p.matmul(
                        pm[:, 0:128], triS[:], I_sb[:, c * 4 * E:(c + 1) * 4 * E], start=True, stop=True),
                        after=dep)
                    prev_mm = mmp
                    t_pc[c] = S.op("dve", lambda vv, c=c: vv.tensor_copy(
                        P_all[:, c * 4 * E:(c + 1) * 4 * E], pm[:, 0:128]), after=[mmp])
                t_posk = {}
                last2 = t_pc[3]
                for k in range(4):
                    hk = hh_sb[:, k * NT * E:(k + 1) * NT * E]
                    last2 = S.op("dve", lambda vv, hk=hk: vv.tensor_tensor(
                        tmpW[:], hk, P_all[:], ALU.mult), after=[last2])
                    last2 = S.op("dve", lambda vv, k=k: vv.reduce_sum(
                        posk_sb[:, k * NT:(k + 1) * NT], v3(tmpW[:]), axis=AX.X), after=[last2])
                    t_posk[k] = last2
                return t_posk, t_cnt, last3


            for i in range(NT + 3):
                if i < NT:
                    att_pass2(i)
                if i >= 3:
                    att_pass2b(i - 3)
                    routing(i - 3)
                if 1 <= i <= NT:
                    att_pass2a2(i - 1)
            if PHASE == 11:
                es_att.close()
                S._wait("sp", [S.token(k) for k in list(S.cnt) if S.cnt[k] > 0])
                S.dma("sp", "outw", lambda s: s.dma_start(out=out_d, in_=x1d))
                S._wait("sp", [S.token("outw")])
                return

            t_attB = [S.token("pe"), S.token("dve"), S.token("act")]

            def finish_partial(gates):
                es_att.close()
                es_x1.close()
                S._wait("sp", [S.token(k) for k in list(S.cnt) if S.cnt[k] > 0])
                S.dma("sp", "outw", lambda s: s.dma_start(out=out_d, in_=x1d))
                S._wait("sp", [S.token("outw")])

            if PHASE == 1:
                finish_partial([])
                return

            es_att.close()
            psd["tp2"] = es.enter_context(nc.psum_tensor("tp2B", [128, 512], BF16))
            psd["h1"] = es.enter_context(nc.psum_tensor("h1B", [128, 1024], F32))
            psd["h3"] = es.enter_context(nc.psum_tensor("h3B", [128, 512], F32))
            psd["y"] = es.enter_context(nc.psum_tensor("yB", [128, 1024], F32))
            def make_ffn_pipe(nt_total, weights, src_tok, dst_write):
                st = {"mm": {}, "sil": {}, "mul": {}, "tr": {}, "hT": {}, "w2": {}, "y": {}, "o": {}}

                def w13(g):
                    if not (0 <= g < nt_total):
                        return
                    tb = g % 2
                    w1ap, w3ap, w2ap, wtoks = weights(g)
                    stok, xTap = src_tok(g)
                    dep = [stok] + list(wtoks)
                    if g >= 2:
                        dep.append(st["sil"][g - 2])
                    mm = None
                    for dc in range(8):
                        mm = S.op("pe", lambda p, dc=dc, tb=tb, w1ap=w1ap, xTap=xTap: p.matmul(
                            psd["h1"][:, tb * II:(tb + 1) * II], xTap(dc), w1ap(dc),
                            start=(dc == 0), stop=(dc == 7)), after=dep if dc == 0 else ())
                    dep2 = [st["mul"][g - 1]] if g >= 1 else []
                    for dc in range(8):
                        mm = S.op("pe", lambda p, dc=dc, w3ap=w3ap, xTap=xTap: p.matmul(
                            psd["h3"][:], xTap(dc), w3ap(dc),
                            start=(dc == 0), stop=(dc == 7)), after=dep2 if dc == 0 else ())
                    st["mm"][g] = mm

                def post13(g):
                    if not (0 <= g < nt_total):
                        return
                    tb = g % 2
                    st["sil"][g] = S.op("act", lambda s, tb=tb: s.activation(
                        out=s1_sb[:, tb * II:(tb + 1) * II], in_=psd["h1"][:, tb * II:(tb + 1) * II],
                        func=AF.Silu), after=[st["mm"][g]])
                    dep = [st["sil"][g]]
                    if g >= 2:
                        dep.append(st["tr"][g - 2])
                    st["mul"][g] = S.op("dve", lambda vv, tb=tb: vv.tensor_tensor(
                        h_pipe[:, tb * II:(tb + 1) * II], s1_sb[:, tb * II:(tb + 1) * II],
                        psd["h3"][:], ALU.mult), after=dep)

                def trh(g):
                    if not (0 <= g < nt_total):
                        return
                    tb = g % 2
                    dep = [st["mul"][g], t_consts]
                    if g >= 1:
                        dep.append(st["hT"][g - 1])
                    tr2 = None
                    for ic in range(4):
                        tr2 = S.op("pe", lambda p, ic=ic, tb=tb: p.transpose(
                            psd["tp2"][:, ic * 128:(ic + 1) * 128],
                            h_pipe[:, tb * II + ic * 128:tb * II + (ic + 1) * 128], idn[:]),
                            after=dep if ic == 0 else ())
                    st["tr"][g] = tr2
                    dep = [tr2]
                    if g >= 2:
                        dep.append(st["w2"][g - 2])
                    st["hT"][g] = S.op("dve", lambda vv, tb=tb: vv.tensor_copy(
                        hT_sb[:, tb * II:(tb + 1) * II], psd["tp2"][:]), after=dep)

                def w2f(g):
                    if not (0 <= g < nt_total):
                        return
                    tb = g % 2
                    _, _, w2ap, _ = weights(g)
                    dep = [st["hT"][g]]
                    if g >= 1:
                        dep.append(st["y"][g - 1])
                    for ic in range(4):
                        for half in range(2):
                            mm2 = S.op("pe", lambda p, ic=ic, half=half, tb=tb, w2ap=w2ap: p.matmul(
                                psd["y"][:, half * 512:(half + 1) * 512],
                                hT_sb[:, tb * II + ic * 128:tb * II + (ic + 1) * 128],
                                w2ap(ic, half),
                                start=(ic == 0), stop=(ic == 3)), after=dep if (ic == 0 and half == 0) else ())
                    st["w2"][g] = mm2

                def postw2(g):
                    if not (0 <= g < nt_total):
                        return
                    tb = g % 2
                    dep = [st["w2"][g]]
                    if g >= 2:
                        dep.append(st["o"][g - 2])
                    st["y"][g] = S.op("act", lambda s, tb=tb: s.activation(
                        out=ye_sb[:, tb * D:(tb + 1) * D], in_=psd["y"][:], func=AF.Copy), after=dep)
                    st["o"][g] = dst_write(g, tb, st["y"][g])

                return st, (w13, post13, trh, w2f, postw2)

            def sh_weights(g):
                return (lambda dc: w1e[:, 8 * II + dc * II:8 * II + (dc + 1) * II],
                        lambda dc: w3e[:, 8 * II + dc * II:8 * II + (dc + 1) * II],
                        lambda ic, half: w2e[:, 4 * D + ic * D + half * 512:4 * D + ic * D + (half + 1) * 512],
                        [t_sw] + t_attB)

            def sh_src(g):
                return (t_x1bT[g], lambda dc, g=g: x1bT[:, g * D + dc * 128:g * D + (dc + 1) * 128])

            def sh_dst(g, tb, ytok):
                return S.dma("sp", "ysh%d" % tb, lambda s, g=g, tb=tb: s.dma_start(
                    out=yshd[g * 128:(g + 1) * 128, :], in_=ye_sb[:, tb * D:(tb + 1) * D]), after=[ytok])

            stS, (sh_w13, sh_post13, sh_trh, sh_w2, sh_postw2) = make_ffn_pipe(
                NT, sh_weights, sh_src, sh_dst)

            t_posk, t_cnt, t_rbend = routing_batch()
            t_ag2a = S.collective(lambda g: g.collective_compute(
                "AllGather", ALU.bypass, replica_groups=[list(range(NCORES))],
                ins=[xag_i[0:T // 2, :]],
                outs=[xall[0:NCORES * T // 2, :]]),
                after=[S.token("xagA0"), S.token("xagA1")])
            t_cntw = S.dma("sp", "cntw", lambda s: s.dma_start(out=cnt_i, in_=counts_sb[:]), after=[t_cnt])
            t_ag1 = S.collective(lambda g: g.collective_compute(
                "AllGather", ALU.bypass, replica_groups=[list(range(NCORES))],
                ins=[cnt_i], outs=[cnt_o]), after=[t_cntw])
            t_ag2 = S.collective(lambda g: g.collective_compute(
                "AllGather", ALU.bypass, replica_groups=[list(range(NCORES))],
                ins=[xag_i[T // 2:T, :]],
                outs=[xall[NCORES * T // 2:NCORES * T, :]]),
                after=[S.token("xagB0"), S.token("xagB1")])
            t_cl = S.dma("sp", "cl", lambda s: s.dma_start(out=call_sb[:], in_=cnt_o), after=[t_ag1])
            t_bm = S.op("pe", lambda p: p.matmul(prb[0:1, 0:E], cmask[:], call_sb[:], start=True, stop=True),
                        after=[t_cl, t_consts, t_cnt, t_cse_guard[0]])
            t_base = S.op("dve", lambda vv: vv.tensor_copy(base_sb[:], prb[0:1, 0:E]), after=[t_bm])

            brs = {}

            def breg(name, val):
                if name not in brs:
                    r = nc.gpsimd.alloc_register(name)
                    nc.gpsimd.reg_mov(r, val)
                    brs[name] = r
                return brs[name]

            last = S.op("dve", lambda vv: vv.tensor_tensor(
                rt_all[0:1, :].rearrange("p (j e) -> p j e", e=E),
                rb_sb[0:1, :].rearrange("p (j e) -> p j e", e=E),
                base_sb[:].rearrange("p (c e) -> p c e", c=1).broadcast_to([1, NT, E]),
                ALU.add), after=[t_base, t_rbend])
            t_rbt = {}
            prevb = None
            for c in range(4):
                dep = [last, t_consts, t_lge[NT - 1]]
                if prevb is not None:
                    dep.append(t_rbt[c - 1])
                bmm = S.op("pe", lambda p, c=c: p.matmul(
                    pm[:, 0:128], onesf[:], rt_all[0:1, c * 128:(c + 1) * 128], start=True, stop=True),
                    after=dep)
                prevb = bmm
                t_rbt[c] = S.op("dve", lambda vv, c=c: vv.tensor_copy(
                    rbt_all[:, c * 128:(c + 1) * 128], pm[:, 0:128]), after=[bmm])
            last = t_rbt[3]
            t_sidx = {}
            for k in range(4):
                hk = hh_sb[:, k * NT * E:(k + 1) * NT * E]
                last = S.op("dve", lambda vv, hk=hk: vv.tensor_tensor(
                    tmpW[:], hk, rbt_all[:], ALU.mult), after=[last, t_posk[k]])
                last = S.op("dve", lambda vv: vv.reduce_sum(
                    tk_sb[:, 0:NT], v3(tmpW[:]), axis=AX.X), after=[last])
                last = S.op("dve", lambda vv, k=k: vv.tensor_tensor(
                    tk_sb[:, NT:2 * NT], tk_sb[:, 0:NT], posk_sb[:, k * NT:(k + 1) * NT], ALU.add), after=[last])
                last = S.op("dve", lambda vv: vv.tensor_scalar(
                    tk_sb[:, 2 * NT:3 * NT], tk_sb[:, NT:2 * NT], float(CAP), None, ALU.is_lt), after=[last])
                last = S.op("dve", lambda vv, k=k: vv.tensor_tensor(
                    gf_sb[:, k * NT:(k + 1) * NT], g_sb[:, k * NT:(k + 1) * NT],
                    tk_sb[:, 2 * NT:3 * NT], ALU.mult), after=[last])
                last = S.op("dve", lambda vv, k=k: vv.scalar_tensor_tensor(
                    tk_sb[:, 3 * NT:4 * NT], ek_sb[:, k * NT:(k + 1) * NT], float(CAP),
                    tk_sb[:, NT:2 * NT], ALU.mult, ALU.add), after=[last])
                last = S.op("dve", lambda vv: vv.scalar_tensor_tensor(
                    tk_sb[:, 4 * NT:5 * NT], tk_sb[:, 2 * NT:3 * NT], -OOB,
                    tk_sb[:, 3 * NT:4 * NT], ALU.mult, ALU.add), after=[last])
                last = S.op("dve", lambda vv, k=k: vv.tensor_scalar(
                    sidx_i[:, k * NT:(k + 1) * NT], tk_sb[:, 4 * NT:5 * NT], OOB, None, ALU.add), after=[last])
                t_sidx[k] = last

            for j in range(NT):
                for k in range(4):
                    col = k * NT + j
                    S.dma("pool", "scat", lambda g, j=j, col=col, k=k: g.indirect_dma_start(
                        out=amap_i,
                        out_offset=IndirectOffsetOnAxis(sidx_i[:, col:col + 1], axis=0),
                        in_=tglob[:, j:j + 1],
                        in_offset=None,
                        bounds_check=breg("b_amap", E * CAP - 1),
                        oob_is_err=False), after=[t_sidx[k], t_amz, t_consts])
            t_ar = S.collective(lambda g: g.collective_compute(
                "AllReduce", ALU.add, replica_groups=[list(range(NCORES))],
                ins=[amap_i.rearrange("(a b) c -> a (b c)", a=E * CAP // 128)],
                outs=[amap_o.rearrange("(a b) c -> a (b c)", a=E * CAP // 128)]),
                after=[S.token("scat")])

            for i in range(NT + 2):
                sh_trh(i - 1)
                sh_w13(i)
                sh_w2(i - 1)
                sh_post13(i)
                sh_postw2(i - 1)

            t_am = S.dma("pool", "amg", lambda g: g.indirect_dma_start(
                out=am_sb[:],
                out_offset=None,
                in_=amap_o.rearrange("(r q) c -> r (q c)", q=128),
                in_offset=IndirectOffsetOnAxis(oidx[:], axis=0),
                bounds_check=breg("b_amr", E * CAP // 128 - 1),
                oob_is_err=False), after=[t_ar, t_consts])
            t_amt = S.op("pe", lambda p: p.transpose(pm[:, 0:32], am_sb[:], idnf[0:32, 0:32]),
                         after=[t_am, t_rbt[3], t_consts])
            tg = S.op("dve", lambda vv: vv.tensor_copy(gi32[:], pm[:, 0:32]), after=[t_amt])
            tg = S.op("dve", lambda vv: vv.tensor_scalar(tmpW[:, 0:E], gi32[:], 0.5, None, ALU.is_gt), after=[tg])
            tg = S.op("dve", lambda vv: vv.scalar_tensor_tensor(
                gi32[:], tmpW[:, 0:E], -OOB, gi32[:], ALU.mult, ALU.add), after=[tg])
            tg = S.op("dve", lambda vv: vv.tensor_scalar(gi32[:], gi32[:], OOB - 1.0, None, ALU.add), after=[tg])
            t_gidx = S.op("dve", lambda vv: vv.tensor_copy(gidx_i[:], gi32[:]), after=[tg])

            if PHASE == 3:
                finish_partial([])
                return

            if PHASE == 2:
                finish_partial([])
                return
            load_expert_w(1, stS["mm"][NT - 1])

            tpX = pk[:, :].bitcast(BF16)
            NGE = EPC * (CAP // 128)
            t_xeg, t_xetr, t_xeT = {}, {}, {}
            t_shend = [stS["o"][NT - 1], stS["o"][NT - 2]]

            def xe_weights(g):
                e = g // (CAP // 128)
                eb = e % 2
                return (lambda dc, eb=eb: w1e[:, eb * 8 * II + dc * II:eb * 8 * II + (dc + 1) * II],
                        lambda dc, eb=eb: w3e[:, eb * 8 * II + dc * II:eb * 8 * II + (dc + 1) * II],
                        lambda ic, half, eb=eb: w2e[:, eb * 4 * D + ic * D + half * 512:eb * 4 * D + ic * D + (half + 1) * 512],
                        [t_wexp[e]] + t_shend)

            def xe_gather(g):
                if not (0 <= g < NGE):
                    return
                s3 = g % 3
                dep = [t_gidx, t_ag2, t_xez]
                if g >= 3:
                    dep.append(t_xetr[g - 3])
                t_xeg[g] = S.dma("pool", "xeg%d" % s3, lambda gg, g=g, s3=s3: gg.indirect_dma_start(
                    out=xe_t[:, s3 * D:(s3 + 1) * D],
                    out_offset=None,
                    in_=xall,
                    in_offset=IndirectOffsetOnAxis(gidx_i[:, g:g + 1], axis=0),
                    bounds_check=breg("b_xall", NCORES * T - 1),
                    oob_is_err=False), after=dep)

            def xe_src(g):
                return (t_xeT[g], lambda dc, g=g: xeT_sb[:, (g % 2) * D + dc * 128:(g % 2) * D + (dc + 1) * 128])

            def xe_trans(g, st):
                if not (0 <= g < NGE):
                    return
                s3, tb = g % 3, g % 2
                dep = [t_xeg[g], t_consts] + (t_attB if g == 0 else [])
                if g >= 1:
                    dep.append(t_xeT[g - 1])
                tr = None
                for dc in range(8):
                    tr = S.op("pe", lambda p, dc=dc, s3=s3: p.transpose(
                        tpX[:, dc * 128:(dc + 1) * 128],
                        xe_t[:, s3 * D + dc * 128:s3 * D + (dc + 1) * 128], idn[:]),
                        after=dep if dc == 0 else ())
                t_xetr[g] = tr
                dep = [tr]
                if g >= 2:
                    dep.append(st["mm"][g - 2])
                t_xeT[g] = S.op("dve", lambda vv, tb=tb: vv.tensor_copy(
                    xeT_sb[:, tb * D:(tb + 1) * D], tpX[:]), after=dep)

            def xe_dst(g, tb, ytok):
                strm = "yw%d_%d" % (g // (CAP // 128), tb)
                return S.dma("sp", strm, lambda s, g=g, strm=strm: s.dma_start(
                    out=yec[g * 128:(g + 1) * 128, :], in_=ye_sb[:, (g % 2) * D:(g % 2 + 1) * D]), after=[ytok])

            stX, (xe_w13, xe_post13, xe_trh, xe_w2, xe_postw2) = make_ffn_pipe(
                NGE, xe_weights, xe_src, xe_dst)
            t_yewk = {}

            xe_gather(0)
            xe_gather(1)
            xe_gather(2)
            xe_trans(0, stX)
            for g in range(NGE + 2):
                xe_gather(g + 3)
                xe_trans(g + 1, stX)
                xe_trh(g - 1)
                xe_w13(g)
                xe_w2(g - 1)
                xe_post13(g)
                xe_postw2(g - 1)
                ntile = CAP // 128
                if (g - 1) % ntile == ntile - 1 and 0 <= g - 1 < NGE:
                    e = (g - 1) // ntile
                    t_yewk[e] = stX["o"][g - 1]
                    if e + 2 < EPC:
                        load_expert_w(e + 2, stX["w2"][g - 1])
                    t_ag3 = S.collective(lambda gg, e=e: gg.collective_compute(
                        "AllGather", ALU.bypass, replica_groups=[list(range(NCORES))],
                        ins=[yec[e * CAP:(e + 1) * CAP, :]],
                        outs=[yeall[e * NCORES * CAP:(e + 1) * NCORES * CAP, :]]),
                        after=[S.token("yw%d_0" % e), S.token("yw%d_1" % e)])

            if PHASE == 4:
                finish_partial([])
                return

            es_x1.close()
            yg_sb = es.enter_context(nc.sbuf_tensor("yg_sb", [128, 8 * D], BF16))
            acc_sb = es.enter_context(nc.sbuf_tensor("acc_sb", [128, 3 * D], F32))
            o_sb = es.enter_context(nc.sbuf_tensor("o_sb", [128, 2 * D], F32))
            accb_sb = es.enter_context(nc.sbuf_tensor("accb_sb", [128, 2 * D], F32))
            gys_sb = es.enter_context(nc.sbuf_tensor("gys_sb", [128, 4 * D], BF16))
            t_ygz = S.op("dve", lambda vv: vv.memset(yg_sb[:], 0.0), after=[t_gidx])
            t_x1r, t_yshr, t_ygv, t_acc, t_o, t_ow = {}, {}, {}, {}, {}, {}
            t_ta1, t_ta2, t_tg2, t_tg3, t_tb1, t_txy = {}, {}, {}, {}, {}, {}

            def cmb_main(j):
                s2, s3 = j % 2, j % 3
                dep = [t_x1w[NT - 1], t_x1w[NT - 2], t_gidx]
                if j >= 2:
                    dep.append(t_acc[j - 2])
                t_x1r[j] = S.dma("sp", "x1r%d" % s2, lambda s, j=j, s2=s2: s.dma_start(
                    out=accb_sb[:, s2 * D:(s2 + 1) * D], in_=x1d[j * 128:(j + 1) * 128, :]), after=dep)
                dep = [S.token("ysh0"), S.token("ysh1")]
                if j >= 2:
                    dep.append(t_txy[j - 2])
                t_yshr[j] = S.dma("sp", "yshr%d" % s2, lambda s, j=j, s2=s2: s.dma_start(
                    out=ysh_sb[:, s2 * D:(s2 + 1) * D], in_=yshd[j * 128:(j + 1) * 128, :]), after=dep)
                for k in range(4):
                    col = k * NT + j
                    slot = s2 * 4 + k
                    dep = [t_ag3, t_sidx[k], t_ygz]
                    if j >= 2:
                        dep.append((t_ta1[j - 2], t_ta2[j - 2], t_tg2[j - 2], t_tg3[j - 2])[k])
                    t_ygv[col] = S.dma("pool", "yg%d" % slot, lambda g, col=col, slot=slot: g.indirect_dma_start(
                        out=yg_sb[:, slot * D:(slot + 1) * D],
                        out_offset=None,
                        in_=yeall,
                        in_offset=IndirectOffsetOnAxis(sidx_i[:, col:col + 1], axis=0),
                        bounds_check=breg("b_ye", E * CAP - 1),
                        oob_is_err=False), after=dep)
                dep = [t_ygv[0 * NT + j]]
                if j >= 3:
                    dep.append(t_o[j - 3])
                t_ta1[j] = S.op("dve", lambda vv, j=j, s2=s2: vv.tensor_scalar(
                    acc_sb[:, s3 * D:(s3 + 1) * D], yg_sb[:, (s2 * 4) * D:(s2 * 4 + 1) * D],
                    gf_sb[:, 0 * NT + j:0 * NT + j + 1], None, ALU.mult), after=dep)
                t_ta2[j] = S.op("dve", lambda vv, j=j, s2=s2: vv.scalar_tensor_tensor(
                    acc_sb[:, s3 * D:(s3 + 1) * D], yg_sb[:, (s2 * 4 + 1) * D:(s2 * 4 + 2) * D],
                    gf_sb[:, 1 * NT + j:1 * NT + j + 1], acc_sb[:, s3 * D:(s3 + 1) * D],
                    ALU.mult, ALU.add), after=[t_ta1[j], t_ygv[1 * NT + j]])
                t_tg2[j] = S.op("act", lambda s, j=j, s2=s2: s.activation(
                    out=gys_sb[:, (s2 * 2) * D:(s2 * 2 + 1) * D],
                    in_=yg_sb[:, (s2 * 4 + 2) * D:(s2 * 4 + 3) * D], func=AF.Copy,
                    scale=gf_sb[:, 2 * NT + j:2 * NT + j + 1]),
                    after=[t_ygv[2 * NT + j]] + ([t_tb1[j - 2]] if j >= 2 else []))
                t_tg3[j] = S.op("dve", lambda vv, j=j, s2=s2: vv.tensor_scalar(
                    gys_sb[:, (s2 * 2 + 1) * D:(s2 * 2 + 2) * D],
                    yg_sb[:, (s2 * 4 + 3) * D:(s2 * 4 + 4) * D],
                    gf_sb[:, 3 * NT + j:3 * NT + j + 1], None, ALU.mult),
                    after=[t_ygv[3 * NT + j]] + ([t_tb1[j - 2]] if j >= 2 else []))
            def cmb_mainb(j):
                s2, s3 = j % 2, j % 3
                t_tb1[j] = S.op("dve", lambda vv, s2=s2: vv.tensor_tensor(
                    acc_sb[:, s3 * D:(s3 + 1) * D], acc_sb[:, s3 * D:(s3 + 1) * D],
                    gys_sb[:, (s2 * 2) * D:(s2 * 2 + 1) * D], ALU.add), after=[t_ta2[j], t_tg2[j]])
                tb2 = S.op("dve", lambda vv, s2=s2: vv.tensor_tensor(
                    acc_sb[:, s3 * D:(s3 + 1) * D], acc_sb[:, s3 * D:(s3 + 1) * D],
                    gys_sb[:, (s2 * 2 + 1) * D:(s2 * 2 + 2) * D], ALU.add), after=[t_tb1[j], t_tg3[j]])
                dep = [t_x1r[j], t_yshr[j]]
                t_txy[j] = S.op("dve", lambda vv, s2=s2: vv.tensor_tensor(
                    accb_sb[:, s2 * D:(s2 + 1) * D], accb_sb[:, s2 * D:(s2 + 1) * D],
                    ysh_sb[:, s2 * D:(s2 + 1) * D], ALU.add), after=dep)
                t_acc[j] = S.op("dve", lambda vv, j=j, s2=s2: vv.scalar_tensor_tensor(
                    acc_sb[:, s3 * D:(s3 + 1) * D], acc_sb[:, s3 * D:(s3 + 1) * D], 1.0,
                    accb_sb[:, s2 * D:(s2 + 1) * D], ALU.mult, ALU.add,
                    accum_out=smc(j, 0)), after=[tb2, t_txy[j]])

            def cmb_fin(j):
                s2, s3 = j % 2, j % 3
                out_deps = [t_ln[NT - 1]]
                if j >= 2:
                    out_deps.append(t_ow[j - 2])
                t_o[j] = ln_chain(j, t_acc[j], acc_sb[:, s3 * D:(s3 + 1) * D],
                                  o_sb[:, s2 * D:(s2 + 1) * D],
                                  o_sb[:, s2 * D:(s2 + 1) * D], out_deps)
                t_ow[j] = S.dma("sp", "outw%d" % s2, lambda s, j=j, s2=s2: s.dma_start(
                    out=out_d[j * 128:(j + 1) * 128, :], in_=o_sb[:, s2 * D:(s2 + 1) * D]), after=[t_o[j]])

            for j in range(NT + 3):
                if j >= 3:
                    cmb_fin(j - 3)
                if j < NT:
                    cmb_main(j)
                if 1 <= j <= NT:
                    cmb_mainb(j - 1)

            S._wait("sp", [S.token("outw0"), S.token("outw1")])
    return nc


_NC_CACHE = None


def _get_nc():
    global _NC_CACHE
    if _NC_CACHE is None:
        _NC_CACHE = build_nc()
    return _NC_CACHE


def _bf16(a):
    a = np.asarray(a)
    if a.dtype == np.uint16:
        return a.view(ml_dtypes.bfloat16)
    return a.astype(ml_dtypes.bfloat16) if a.dtype != ml_dtypes.bfloat16 else a


def kernel(L0_attn_x, L0_attn_Wq, L0_attn_Wk, L0_attn_Wv, L0_attn_Wo,
           L0_ffn_gate_w, L0_ffn_bias, L0_ffn_W1, L0_ffn_W3, L0_ffn_W2,
           L0_ffn_sw1, L0_ffn_sw3, L0_ffn_sw2):
    x = np.ascontiguousarray(np.asarray(L0_attn_x, dtype=np.float32))
    wqT = np.ascontiguousarray(np.asarray(L0_attn_Wq, dtype=np.float32).T)
    wkT = np.ascontiguousarray(np.asarray(L0_attn_Wk, dtype=np.float32).T)
    wvT = np.ascontiguousarray(np.asarray(L0_attn_Wv, dtype=np.float32).T)
    woT = np.ascontiguousarray(np.asarray(L0_attn_Wo, dtype=np.float32).T)
    gwT = np.ascontiguousarray(_bf16(L0_ffn_gate_w).T)
    gbias = np.ascontiguousarray(np.asarray(L0_ffn_bias, dtype=np.float32).reshape(1, E))
    eold = np.arange(E)
    enew = 8 * (eold % 4) + eold // 4
    gwT_p = np.empty_like(gwT); gwT_p[:, enew] = gwT
    gbias_p = np.empty_like(gbias); gbias_p[0, enew] = gbias[0]
    gwT, gbias = np.ascontiguousarray(gwT_p), np.ascontiguousarray(gbias_p)
    W1 = _bf16(L0_ffn_W1)
    W3 = _bf16(L0_ffn_W3)
    W2 = _bf16(L0_ffn_W2)
    sw1T = np.ascontiguousarray(_bf16(L0_ffn_sw1).T)
    sw3T = np.ascontiguousarray(_bf16(L0_ffn_sw3).T)
    sw2T = np.ascontiguousarray(_bf16(L0_ffn_sw2).T)

    idn = np.eye(128, dtype=ml_dtypes.bfloat16)
    idnf = np.eye(128, dtype=np.float32)
    maskT = np.triu(np.ones((128, 128), dtype=np.float32)).astype(ml_dtypes.bfloat16)
    triS = np.triu(np.ones((128, 128), dtype=np.float32), k=1).astype(ml_dtypes.bfloat16)
    iotaE = np.tile(np.arange(E, dtype=np.float32), (128, 1))
    onesf = np.ones((1, 128), np.float32)
    onesb = np.ones((128, 1), np.float32).astype(ml_dtypes.bfloat16)

    nc = _get_nc()
    in_maps = []
    for c in range(NCORES):
        ii = np.arange(NT)[None, :]
        tglob = (np.where(ii < NT // 2, c * (T // 2) + ii * 128,
                          NCORES * T // 2 + c * (T // 2) + (ii - NT // 2) * 128)
                 + np.arange(128)[:, None] + 1).astype(np.float32)
        cmask = (np.arange(NCORES) < c).astype(np.float32).reshape(NCORES, 1)
        oidx = (64 * np.arange(EPC)[:, None] + 8 * c
                + np.arange(8)[None, :]).astype(np.int32).reshape(32, 1)
        in_maps.append({
            "x": np.ascontiguousarray(x[c]),
            "wqT": wqT, "wkT": wkT, "wvT": wvT, "woT": woT,
            "gwT": gwT, "gbias": gbias,
            "w1s": np.ascontiguousarray(W1[c * EPC:(c + 1) * EPC].reshape(EPC * D, II)),
            "w3s": np.ascontiguousarray(W3[c * EPC:(c + 1) * EPC].reshape(EPC * D, II)),
            "w2s": np.ascontiguousarray(W2[c * EPC:(c + 1) * EPC].reshape(EPC * II, D)),
            "sw1T": sw1T, "sw3T": sw3T, "sw2T": sw2T,
            "idn": idn, "idnf": idnf, "maskT": maskT, "triS": triS, "iotaE": iotaE,
            "onesf": onesf, "onesb": onesb,
            "tglob": tglob, "cmask": cmask, "oidx": oidx,
        })
    res = run_bass_kernel_spmd(nc, in_maps, core_ids=list(range(NCORES)))
    out = np.stack([np.asarray(res.results[c]["out"]) for c in range(NCORES)], axis=0)
    return out.astype(np.float32)
